# version 02
# baseline (speedup 1.0000x reference)
import numpy as np
import concourse.bass as bass
import concourse.mybir as mybir
from concourse.bass_utils import run_bass_kernel_spmd
from contextlib import ExitStack

F32 = mybir.dt.float32
BF16 = mybir.dt.bfloat16
I32 = mybir.dt.int32
AF = mybir.ActivationFunctionType
ALU = mybir.AluOpType
AX = mybir.AxisListType

D = 2048
DK = 16
H = 16
HD = 128
CH = 128
NCH = 32
OWN = 16
EPS = 1e-5
ISH = 1408
ICN = 11
NE10 = 10
TSUB = 8
TILE = 1024
NTILE = 2
CHUNK_ELEMS = CH * D
DSW = (4, 4, 3)

ENGS = ("sy", "gp", "t", "v", "s",
        "xA", "xB", "st", "ld1", "wA", "wB", "wbig",
        "kvst", "Sst", "kvA", "kvB", "oTA", "oTB", "upA", "upB", "dnw")


def _w3(dram2d, p=128):
    return dram2d.rearrange("(k p) e -> p k e", p=p)


def _b3(buf2d, e):
    return buf2d.rearrange("p (k e) -> p k e", e=e)


class Ctx:
    def __init__(self, nc, stack):
        self.nc = nc
        self.h = {}
        self.cnt = {}
        self.lastw = {}
        for n in ENGS:
            self.h[n] = stack.enter_context(nc.semaphore("sem_" + n))
            self.cnt[n] = 0

    def inc(self, name, inst, k=1):
        inst.then_inc(self.h[name], k)
        self.cnt[name] += k
        return (name, self.cnt[name])

    def mark(self, name):
        return (name, self.cnt[name])

    def wait(self, eng, engname, tok):
        if not tok:
            return
        semname, val = tok
        if val <= 0:
            return
        key = (engname, semname)
        if self.lastw.get(key, -1) >= val:
            return
        eng.wait_ge(self.h[semname], val)
        self.lastw[key] = val

    def barrier_marks(self):
        return dict(self.cnt)


def _emit_barrier(blk, c, marks):
    for en, sect in (("sy", blk.sync), ("gp", blk.gpsimd), ("t", blk.tensor),
                     ("v", blk.vector), ("s", blk.scalar)):
        def f(eng, en=en):
            for sn, v in marks.items():
                if sn != en:
                    c.wait(eng, en, (sn, v))
        sect(f)


def build_program(upto=4, dbg=False, nch=NCH, nmb=48):
    nc = bass.Bass("TRN2", target_bir_lowering=False)

    x_scan = nc.dram_tensor("x_scan", [4096, D], F32, kind="ExternalInput").ap()
    c_row = nc.dram_tensor("c_row", [D], F32, kind="ExternalInput").ap()
    ld_in = nc.dram_tensor("ld", [H], F32, kind="ExternalInput").ap()
    wmod = nc.dram_tensor("wmod", [D, 6 * D], F32, kind="ExternalInput").ap()
    wk = nc.dram_tensor("wk", [D, D], F32, kind="ExternalInput").ap()
    wv = nc.dram_tensor("wv", [D, D], F32, kind="ExternalInput").ap()
    own0 = nc.dram_tensor("own0", [1, 1], I32, kind="ExternalInput").ap()
    if upto >= 2:
        x_own = nc.dram_tensor("x_own", [OWN * CH, D], F32, kind="ExternalInput").ap()
        wq = nc.dram_tensor("wq", [D, D], F32, kind="ExternalInput").ap()
        wg1 = nc.dram_tensor("wg1", [D, HD], F32, kind="ExternalInput").ap()
        wg2 = nc.dram_tensor("wg2", [HD, D], F32, kind="ExternalInput").ap()
        normw = nc.dram_tensor("normw", [D], F32, kind="ExternalInput").ap()
    if upto >= 3:
        wo = nc.dram_tensor("wo", [D, D], F32, kind="ExternalInput").ap()
    if upto >= 4:
        gwT = nc.dram_tensor("gwT", [D, 8], F32, kind="ExternalInput").ap()
        sgT = nc.dram_tensor("sgT", [D, 2 * ISH], F32, kind="ExternalInput").ap()
        suT = nc.dram_tensor("suT", [D, 2 * ISH], F32, kind="ExternalInput").ap()
        sdT = nc.dram_tensor("sdT", [2 * ISH, D], F32, kind="ExternalInput").ap()
        egT = nc.dram_tensor("egT", [8, D, ISH], F32, kind="ExternalInput").ap()
        euT = nc.dram_tensor("euT", [8, D, ISH], F32, kind="ExternalInput").ap()
        edT = nc.dram_tensor("edT", [8, ISH, D], F32, kind="ExternalInput").ap()
        y_out = nc.dram_tensor("y", [OWN * CH, D], F32, kind="ExternalOutput").ap()

    def scratch(name, shape, dt, out_at):
        kind = "ExternalOutput" if (dbg and upto == out_at) else "Internal"
        return nc.dram_tensor(name, shape, dt, kind=kind).ap()

    sc_kvs = scratch("sc_kvs", [NCH * 3 * CH, D], BF16, 1)
    sc_mod = scratch("sc_mod", [6 * CH, D], F32, 1)
    dbg_h = (nc.dram_tensor("dbg_h", [CH, D], BF16, kind="ExternalOutput").ap()
             if (dbg and upto == 1) else None)
    dbg_sm = (nc.dram_tensor("dbg_sm", [CH, 8], F32, kind="ExternalOutput").ap()
              if (dbg and upto == 1) else None)
    dbg_x = (nc.dram_tensor("dbg_x", [CH, D], F32, kind="ExternalOutput").ap()
             if (dbg and upto == 1) else None)
    dbg_ss = (nc.dram_tensor("dbg_ss", [CH, 1], F32, kind="ExternalOutput").ap()
              if (dbg and upto == 1) else None)
    sc_oT = scratch("sc_oT", [OWN * CH, D], BF16, 2) if upto >= 2 else None
    sc_xmid = scratch("sc_xmid", [OWN * CH, D], F32, 3) if upto >= 3 else None
    dbg_oTb = (nc.dram_tensor("dbg_oTb", [CH, D], BF16, kind="ExternalOutput").ap()
               if (dbg and upto == 3) else None)
    dbg_att = (nc.dram_tensor("dbg_att", [CH, D], F32, kind="ExternalOutput").ap()
               if (dbg and upto == 3) else None)

    stack = ExitStack()
    with stack:
        c = Ctx(nc, stack)

        ident = stack.enter_context(nc.sbuf_tensor("ident", [128, 128], BF16))
        ones_f = stack.enter_context(nc.sbuf_tensor("ones_f", [128, 128], F32))
        ones_1p = stack.enter_context(nc.sbuf_tensor("ones_1p", [1, 128], F32))
        ldb = stack.enter_context(nc.sbuf_tensor("ldb", [128, H], F32))
        qdec16 = stack.enter_context(nc.sbuf_tensor("qdec16", [128, H], F32))
        kdec16 = stack.enter_context(nc.sbuf_tensor("kdec16", [128, H], F32))
        cdec16 = stack.enter_context(nc.sbuf_tensor("cdec16", [128, H], F32))
        normw_fc = (stack.enter_context(nc.sbuf_tensor("normw_fc", [128, DK], F32))
                    if upto >= 2 else None)
        it_p1 = stack.enter_context(nc.sbuf_tensor("it_p1", [128, 1], F32))
        it_km = stack.enter_context(nc.sbuf_tensor("it_km", [128, 1], F32))
        c128 = stack.enter_context(nc.sbuf_tensor("c128", [128, 1], F32))
        eps_sb = stack.enter_context(nc.sbuf_tensor("eps_sb", [128, 1], F32))
        ld_sb = stack.enter_context(nc.sbuf_tensor("ld_sb", [1, H], F32))
        own0_sb = stack.enter_context(nc.sbuf_tensor("own0_sb", [1, 1], I32))
        smalls = stack.enter_context(nc.sbuf_tensor("smalls", [128, 8], F32))
        ssumsq = stack.enter_context(nc.sbuf_tensor("ssumsq", [128, 1], F32))
        junk = stack.enter_context(nc.sbuf_tensor("junk", [128, D], BF16))
        sa_rep = stack.enter_context(nc.sbuf_tensor("sa_rep", [128, D], F32))
        ca1_rep = stack.enter_context(nc.sbuf_tensor("ca1_rep", [128, D], F32))

        blk = stack.enter_context(nc.Block())

        p1 = ExitStack()
        wk_sb = p1.enter_context(nc.sbuf_tensor("wk_sb", [128, DK * D], BF16))
        wv_sb = p1.enter_context(nc.sbuf_tensor("wv_sb", [128, DK * D], BF16))

        p0 = ExitStack()
        crep = p0.enter_context(nc.sbuf_tensor("crep", [128, D], BF16))
        c_fc = p0.enter_context(nc.sbuf_tensor("c_fc", [128, DK], F32))
        csil = p0.enter_context(nc.sbuf_tensor("csil", [128, DK], F32))
        wmbuf = [p0.enter_context(nc.sbuf_tensor(f"wmbuf{i}", [128, DK * 256], BF16))
                 for i in range(2)]
        modv = p0.enter_context(nc.sbuf_tensor("modv", [128, D], F32))
        modt = [p0.enter_context(nc.sbuf_tensor(f"modt{i}", [128, D], F32))
                for i in range(2)]
        ps_m = p0.enter_context(nc.psum_tensor("ps_m", [128, 1024], F32))
        ps_sm = p0.enter_context(nc.psum_tensor("ps_sm", [128, 16], F32))

        @blk.sync
        def _(sy):
            i1 = sy.dma_start(out=ld_sb[:, :], in_=ld_in[None, :])
            c.inc("ld1", i1, 16)
            i1 = sy.dma_start(out=own0_sb[:, :], in_=own0)
            c.inc("ld1", i1, 16)
            i1 = sy.dma_start(out=c_fc[:, :], in_=c_row.rearrange("(k p) -> p k", p=128),
                              allow_slow_non_contiguous=True)
            c.inc("ld1", i1, 16)
            if upto >= 2:
                i1 = sy.dma_start(out=normw_fc[:, :],
                                  in_=normw.rearrange("(k p) -> p k", p=128),
                                  allow_slow_non_contiguous=True)
                c.inc("ld1", i1, 16)

        m_sy0 = c.mark("ld1")

        @blk.gpsimd
        def _(g):
            mk = c.inc("gp", g.memset(ident[:, :], 0.0), 1)
            g.memset(ones_f[:, :], 1.0)
            c.wait(g, "gp", mk)
            g.affine_select(out=ident[:, :], in_=ident[:, :],
                            compare_op=ALU.not_equal, fill=1.0,
                            base=0, pattern=[[-1, 128]], channel_multiplier=1)
            g.memset(ones_1p[:, :], 1.0)
            g.memset(c128[:, :], 128.0)
            g.memset(eps_sb[:, :], EPS)
            g.iota(it_p1[:, :], pattern=[[0, 1]], base=1, channel_multiplier=1,
                   allow_small_or_imprecise_dtypes=True)
            i1 = g.iota(it_km[:, :], pattern=[[0, 1]], base=127, channel_multiplier=-1,
                        allow_small_or_imprecise_dtypes=True)
            c.inc("gp", i1, 1)

        m_gp_const = c.mark("gp")

        @blk.tensor
        def _(t):
            c.wait(t, "t", m_sy0)
            c.wait(t, "t", m_gp_const)
            i1 = t.matmul(ps_sm[:, :], ones_1p[:, :], ld_sb[:, :], start=True, stop=True)
            c.inc("t", i1, 1)

        m_t_ldb = c.mark("t")

        @blk.vector
        def _(v):
            c.wait(v, "v", m_t_ldb)
            i1 = v.tensor_copy(ldb[:, :], ps_sm[:, :])
            c.inc("v", i1, 1)

        m_v_ldb = c.mark("v")

        @blk.scalar
        def _(s):
            c.wait(s, "s", m_v_ldb)
            for h in range(H):
                s.activation(qdec16[:, h:h + 1], it_p1[:, :], AF.Exp,
                             scale=ldb[:, h:h + 1])
                s.activation(kdec16[:, h:h + 1], it_km[:, :], AF.Exp,
                             scale=ldb[:, h:h + 1])
                s.activation(cdec16[:, h:h + 1], c128[:, :], AF.Exp,
                             scale=ldb[:, h:h + 1])
            i1 = s.activation(csil[:, :], c_fc[:, :], AF.Silu)
            c.inc("s", i1, 1)

        m_s_dec = c.mark("s")

        @blk.vector
        def _(v):
            c.wait(v, "v", m_s_dec)
            for ch in range(DK):
                i1 = v.tensor_scalar(crep[:, ch * 128:(ch + 1) * 128], ones_f[:, :],
                                     csil[:, ch:ch + 1], None, op0=ALU.mult)
            c.inc("v", i1, 1)

        m_v_crep = c.mark("v")

        NMB = nmb
        m_wm, m_mm_mod, m_modev, m_modst = {}, {}, {}, {}

        for nb in range(NMB):
            vec, q = nb // 8, nb % 8

            @blk.gpsimd
            def _(g, nb=nb):
                if nb >= 2:
                    c.wait(g, "gp", m_mm_mod[nb - 2])
                i1 = g.dma_start(
                    out=_b3(wmbuf[nb % 2][:, :], 256),
                    in_=_w3(wmod[:, nb * 256:(nb + 1) * 256]))
                m_wm[nb] = c.inc("wA" if nb % 2 == 0 else "wB", i1, 16)

            @blk.tensor
            def _(t, nb=nb):
                c.wait(t, "t", m_v_crep)
                c.wait(t, "t", m_wm[nb])
                if nb >= 2:
                    c.wait(t, "t", m_modev[nb - 2])
                for dk in range(DK):
                    i1 = t.matmul(ps_m[:, (nb % 2) * 512:(nb % 2) * 512 + 256],
                                  crep[:, dk * 128:(dk + 1) * 128],
                                  wmbuf[nb % 2][:, dk * 256:(dk + 1) * 256],
                                  start=(dk == 0), stop=(dk == DK - 1))
                m_mm_mod[nb] = c.inc("t", i1, 1)

            @blk.vector
            def _(v, nb=nb, vec=vec, q=q):
                c.wait(v, "v", m_mm_mod[nb])
                if q == 0 and nb >= 8:
                    c.wait(v, "v", m_modst[nb - 1])
                src = ps_m[:, (nb % 2) * 512:(nb % 2) * 512 + 256]
                dst = modv[:, q * 256:(q + 1) * 256]
                if vec in (1, 4):
                    i1 = v.tensor_scalar(dst, src, 1.0, None, op0=ALU.add)
                else:
                    i1 = v.tensor_copy(dst, src)
                m_modev[nb] = c.inc("v", i1, 1)
                if q == 7 and vec in (2, 5):
                    c.wait(v, "v", m_modev[nb])
                    v.tensor_scalar(modt[0][:, :], modv[:, :], 0.25, None, op0=ALU.max)
                    mk = c.inc("v", v.tensor_scalar(modt[1][:, :], modv[:, :], -0.25,
                                                    None, op0=ALU.min), 1)
                    c.wait(v, "v", mk)
                    v.tensor_scalar(modv[:, :], modv[:, :], 0.0, None, op0=ALU.is_ge)
                    mk = c.inc("v", v.tensor_sub(modt[0][:, :], modt[0][:, :],
                                                 modt[1][:, :]), 1)
                    c.wait(v, "v", mk)
                    mk = c.inc("v", v.tensor_mul(modv[:, :], modv[:, :],
                                                 modt[0][:, :]), 1)
                    c.wait(v, "v", mk)
                    i1 = v.tensor_add(modv[:, :], modv[:, :], modt[1][:, :])
                    m_modev[nb] = c.inc("v", i1, 1)

            if q == 7:
                @blk.sync
                def _(sy, nb=nb, vec=vec):
                    c.wait(sy, "sy", m_modev[nb])
                    i1 = sy.dma_start(out=sc_mod[vec * CH:(vec + 1) * CH, :],
                                      in_=modv[:, :])
                    m_modst[nb] = c.inc("st", i1, 16)

        @blk.gpsimd
        def _(g):
            i1 = g.dma_start(out=_b3(wk_sb[:, :], D), in_=_w3(wk))
            c.inc("wbig", i1, 16)
            i1 = g.dma_start(out=_b3(wv_sb[:, :], D), in_=_w3(wv))
            c.inc("wbig", i1, 16)

        m_gp_wkv = c.mark("wbig")

        @blk.sync
        def _(sy):
            c.wait(sy, "sy", c.mark("st"))
            i1 = sy.dma_start(out=sa_rep[:, :], in_=sc_mod[0:CH, :])
            c.inc("ld1", i1, 16)
            i1 = sy.dma_start(out=ca1_rep[:, :], in_=sc_mod[CH:2 * CH, :])
            c.inc("ld1", i1, 16)

        p0.close()

        bar0 = c.barrier_marks()
        _emit_barrier(blk, c, bar0)

        xb = [p1.enter_context(nc.sbuf_tensor(f"xb{i}", [128, D], F32)) for i in range(2)]
        h_bf = p1.enter_context(nc.sbuf_tensor("h_bf", [128, D], BF16))
        hT = p1.enter_context(nc.sbuf_tensor("hT", [128, D], BF16))
        k_tok = p1.enter_context(nc.sbuf_tensor("k_tok", [128, D], BF16))
        v_tok = p1.enter_context(nc.sbuf_tensor("v_tok", [128, D], BF16))
        kT_sb = p1.enter_context(nc.sbuf_tensor("kT_sb", [128, D], BF16))
        S_sb = p1.enter_context(nc.sbuf_tensor("S_sb", [128, D], F32))
        stmp = p1.enter_context(nc.sbuf_tensor("stmp", [128, 128], F32))
        pt1 = p1.enter_context(nc.psum_tensor("pt1", [128, D], BF16))
        pkv = p1.enter_context(nc.psum_tensor("pkv", [128, D], F32))
        pS = p1.enter_context(nc.psum_tensor("pS", [128, 1024], F32))

        m_x, m_sq, m_var, m_rstds, m_hbf = {}, {}, {}, {}, {}
        m_tph, m_hTe, m_kv, m_kve, m_kdec = {}, {}, {}, {}, {}
        m_tpk, m_kTe, m_kTst, m_vst = {}, {}, {}, {}
        m_sst, m_dS, m_supd = {}, {}, {}

        @blk.vector
        def _(v):
            i1 = v.memset(S_sb[:, :], 0.0)
            c.inc("v", i1, 1)

        m_v_sinit = c.mark("v")

        for g_ch in range(nch):
            par = g_ch % 2

            @blk.gpsimd
            def _(g, g_ch=g_ch):
                if g_ch >= 1:
                    c.wait(g, "gp", m_supd[(g_ch - 1, 3)])
                else:
                    c.wait(g, "gp", m_v_sinit)
                i1 = g.dma_start(out=sc_kvs[(g_ch * 3 + 2) * CH:(g_ch * 3 + 3) * CH, :],
                                 in_=S_sb[:, :])
                m_sst[g_ch] = c.inc("Sst", i1, 16)

            @blk.sync
            def _(sy, g_ch=g_ch, par=par):
                if g_ch >= 2:
                    c.wait(sy, "sy", m_hbf[g_ch - 2])
                    c.wait(sy, "sy", m_sq[g_ch - 2])
                i1 = sy.dma_start(out=xb[par][:, :],
                                  in_=x_scan[g_ch * CH:(g_ch + 1) * CH, :])
                m_x[g_ch] = c.inc("xA" if par == 0 else "xB", i1, 16)

            @blk.scalar
            def _(s, g_ch=g_ch, par=par):
                c.wait(s, "s", m_x[g_ch])
                i1 = s.activation(junk[:, :], xb[par][:, :], AF.Square,
                                  accum_out=ssumsq[:, :])
                m_sq[g_ch] = c.inc("s", i1, 1)

            @blk.vector
            def _(v, g_ch=g_ch, par=par):
                c.wait(v, "v", m_x[g_ch])
                mk = c.inc("v", v.reduce_sum(smalls[:, 0:1], xb[par][:, :], axis=AX.X,
                                             negate=True), 1)
                c.wait(v, "v", mk)
                mk = c.inc("v", v.tensor_scalar(smalls[:, 1:2], smalls[:, 0:1],
                                                1.0 / D, None, op0=ALU.mult), 1)
                c.wait(v, "v", mk)
                c.wait(v, "v", m_sq[g_ch])
                mk = c.inc("v", v.tensor_scalar(smalls[:, 2:3], ssumsq[:, :],
                                                1.0 / D, None, op0=ALU.mult), 1)
                c.wait(v, "v", mk)
                mk = c.inc("v", v.tensor_mul(smalls[:, 3:4], smalls[:, 1:2],
                                             smalls[:, 1:2]), 1)
                c.wait(v, "v", mk)
                i1 = v.tensor_sub(smalls[:, 4:5], smalls[:, 2:3], smalls[:, 3:4])
                m_var[g_ch] = c.inc("v", i1, 1)

            @blk.scalar
            def _(s, g_ch=g_ch):
                c.wait(s, "s", m_var[g_ch])
                i1 = s.activation(smalls[:, 5:6], smalls[:, 4:5], AF.Sqrt, bias=eps_sb[:, 0:1])
                m_rstds[g_ch] = c.inc("s", i1, 1)

            @blk.vector
            def _(v, g_ch=g_ch, par=par):
                c.wait(v, "v", m_rstds[g_ch])
                v.tensor_scalar(xb[par][:, :], xb[par][:, :], smalls[:, 1:2], None,
                                op0=ALU.add)
                i1 = v.reciprocal(smalls[:, 6:7], smalls[:, 5:6])
                mr = c.inc("v", i1, 1)
                c.wait(v, "v", mr)
                mk = c.inc("v", v.scalar_tensor_tensor(xb[par][:, :], ca1_rep[:, :],
                                                       smalls[:, 6:7], xb[par][:, :],
                                                       op0=ALU.mult, op1=ALU.mult), 1)
                c.wait(v, "v", mk)
                if g_ch >= 1:
                    c.wait(v, "v", m_tph[g_ch - 1])
                i1 = v.tensor_add(h_bf[:, :], xb[par][:, :], sa_rep[:, :])
                m_hbf[g_ch] = c.inc("v", i1, 1)

            if dbg and upto == 1 and g_ch == 0:
                @blk.sync
                def _(sy):
                    c.wait(sy, "sy", m_var[0])
                    i1 = sy.dma_start(out=dbg_x, in_=xb[0][:, :])
                    c.inc("st", i1, 16)
                    i1 = sy.dma_start(out=dbg_ss, in_=ssumsq[:, :])
                    c.inc("st", i1, 16)
                    c.wait(sy, "sy", m_hbf[0])
                    i1 = sy.dma_start(out=dbg_h, in_=h_bf[:, :])
                    c.inc("st", i1, 16)
                    i1 = sy.dma_start(out=dbg_sm, in_=smalls[:, :])
                    c.inc("st", i1, 16)

            @blk.tensor
            def _(t, g_ch=g_ch):
                c.wait(t, "t", m_hbf[g_ch])
                if g_ch >= 1:
                    c.wait(t, "t", m_kTe[g_ch - 1])
                for dk in range(DK):
                    i1 = t.transpose(pt1[:, dk * 128:(dk + 1) * 128],
                                     h_bf[:, dk * 128:(dk + 1) * 128], ident[:, :])
                m_tph[g_ch] = c.inc("t", i1, 1)

            @blk.vector
            def _(v, g_ch=g_ch):
                c.wait(v, "v", m_tph[g_ch])
                if g_ch >= 1:
                    c.wait(v, "v", m_kv[(g_ch - 1, 3)])
                for eb in range(4):
                    i1 = v.tensor_copy(hT[:, eb * 512:(eb + 1) * 512],
                                       pt1[:, eb * 512:(eb + 1) * 512])
                m_hTe[g_ch] = c.inc("v", i1, 1)

            for eb in range(4):
                pp = eb % 2

                @blk.tensor
                def _(t, g_ch=g_ch, eb=eb, pp=pp):
                    c.wait(t, "t", m_hTe[g_ch])
                    c.wait(t, "t", m_gp_wkv)
                    guard = m_kve.get((g_ch, eb - 2), m_kve.get((g_ch - 1, eb + 2), 0))
                    c.wait(t, "t", guard)
                    for dk in range(DK):
                        t.matmul(pkv[:, pp * 1024:pp * 1024 + 512],
                                 hT[:, dk * 128:(dk + 1) * 128],
                                 wk_sb[:, dk * D + eb * 512:dk * D + (eb + 1) * 512],
                                 start=(dk == 0), stop=(dk == DK - 1))
                        i1 = t.matmul(pkv[:, pp * 1024 + 512:pp * 1024 + 1024],
                                      hT[:, dk * 128:(dk + 1) * 128],
                                      wv_sb[:, dk * D + eb * 512:dk * D + (eb + 1) * 512],
                                      start=(dk == 0), stop=(dk == DK - 1))
                    m_kv[(g_ch, eb)] = c.inc("t", i1, 1)

                @blk.scalar
                def _(s, g_ch=g_ch, eb=eb, pp=pp):
                    c.wait(s, "s", m_kv[(g_ch, eb)])
                    if g_ch >= 1 and eb == 0:
                        c.wait(s, "s", m_dS[(g_ch - 1, 3)])
                        c.wait(s, "s", m_vst[g_ch - 1])
                        c.wait(s, "s", m_tpk[g_ch - 1])
                        c.wait(s, "s", m_kdec[g_ch - 1])
                    s.activation(k_tok[:, eb * 512:(eb + 1) * 512],
                                 pkv[:, pp * 1024:pp * 1024 + 512], AF.Silu)
                    i1 = s.activation(v_tok[:, eb * 512:(eb + 1) * 512],
                                      pkv[:, pp * 1024 + 512:pp * 1024 + 1024], AF.Silu)
                    m_kve[(g_ch, eb)] = c.inc("s", i1, 1)

            @blk.tensor
            def _(t, g_ch=g_ch):
                c.wait(t, "t", m_kve[(g_ch, 3)])
                c.wait(t, "t", m_hTe[g_ch])
                for h in range(H):
                    i1 = t.transpose(pt1[:, h * 128:(h + 1) * 128],
                                     k_tok[:, h * 128:(h + 1) * 128], ident[:, :])
                m_tpk[g_ch] = c.inc("t", i1, 1)

            @blk.vector
            def _(v, g_ch=g_ch):
                c.wait(v, "v", m_tpk[g_ch])
                if g_ch >= 1:
                    c.wait(v, "v", m_vst[g_ch - 1])
                for eb in range(4):
                    i1 = v.tensor_copy(kT_sb[:, eb * 512:(eb + 1) * 512],
                                       pt1[:, eb * 512:(eb + 1) * 512])
                m_kTe[g_ch] = c.inc("v", i1, 1)

            @blk.vector
            def _(v, g_ch=g_ch):
                c.wait(v, "v", m_tpk[g_ch])
                if g_ch >= 1:
                    c.wait(v, "v", m_dS[(g_ch - 1, 3)])
                for h in range(H):
                    i1 = v.tensor_scalar(k_tok[:, h * 128:(h + 1) * 128],
                                         k_tok[:, h * 128:(h + 1) * 128],
                                         kdec16[:, h:h + 1], None, op0=ALU.mult)
                m_kdec[g_ch] = c.inc("v", i1, 1)

            @blk.gpsimd
            def _(g, g_ch=g_ch):
                c.wait(g, "gp", m_kTe[g_ch])
                i1 = g.dma_start(out=sc_kvs[(g_ch * 3 + 0) * CH:(g_ch * 3 + 1) * CH, :],
                                 in_=kT_sb[:, :])
                m_kTst[g_ch] = c.inc("kvst", i1, 16)
                c.wait(g, "gp", m_kve[(g_ch, 3)])
                i1 = g.dma_start(out=sc_kvs[(g_ch * 3 + 1) * CH:(g_ch * 3 + 2) * CH, :],
                                 in_=v_tok[:, :])
                m_vst[g_ch] = c.inc("kvst", i1, 16)

            for hg in range(4):
                @blk.tensor
                def _(t, g_ch=g_ch, hg=hg):
                    c.wait(t, "t", m_kdec[g_ch])
                    guard = m_supd.get((g_ch, hg - 2), m_supd.get((g_ch - 1, hg + 2), 0))
                    c.wait(t, "t", guard)
                    for h in range(hg * 4, hg * 4 + 4):
                        i1 = t.matmul(pS[:, (hg % 2) * 512 + (h % 4) * 128:
                                         (hg % 2) * 512 + (h % 4 + 1) * 128],
                                      k_tok[:, h * 128:(h + 1) * 128],
                                      v_tok[:, h * 128:(h + 1) * 128],
                                      start=True, stop=True)
                    m_dS[(g_ch, hg)] = c.inc("t", i1, 1)

                @blk.vector
                def _(v, g_ch=g_ch, hg=hg):
                    c.wait(v, "v", m_dS[(g_ch, hg)])
                    c.wait(v, "v", m_sst[g_ch])
                    for h in range(hg * 4, hg * 4 + 4):
                        i1 = v.scalar_tensor_tensor(
                            S_sb[:, h * 128:(h + 1) * 128],
                            S_sb[:, h * 128:(h + 1) * 128],
                            cdec16[:, h:h + 1],
                            pS[:, (hg % 2) * 512 + (h % 4) * 128:
                               (hg % 2) * 512 + (h % 4 + 1) * 128],
                            op0=ALU.mult, op1=ALU.add)
                    m_supd[(g_ch, hg)] = c.inc("v", i1, 1)

        if upto == 1:
            @blk.gpsimd
            def _(g):
                for sn in ("st", "Sst", "kvst", "v", "s", "t"):
                    c.wait(g, "gp", c.mark(sn))
            p1.close()
            return nc
        p1.close()

        bar1 = c.barrier_marks()
        _emit_barrier(blk, c, bar1)

        p2 = ExitStack()
        wq_sb = p2.enter_context(nc.sbuf_tensor("wq_sb", [128, DK * D], BF16))
        wg1_sb = p2.enter_context(nc.sbuf_tensor("wg1_sb", [128, DK * 128], BF16))
        wg2_sb = p2.enter_context(nc.sbuf_tensor("wg2_sb", [128, D], BF16))
        Dt = p2.enter_context(nc.sbuf_tensor("Dt", [128, H * 128], F32))
        difT = p2.enter_context(nc.sbuf_tensor("difT", [128, 128], F32))
        xb2 = [p2.enter_context(nc.sbuf_tensor(f"x2_{i}", [128, D], F32)) for i in range(2)]
        kvs = [p2.enter_context(nc.sbuf_tensor(f"kvs{i}", [128, 3 * D], BF16))
               for i in range(2)]
        kTb = [kvs[i][:, 0:D] for i in range(2)]
        vbb = [kvs[i][:, D:2 * D] for i in range(2)]
        Sbb = [kvs[i][:, 2 * D:3 * D] for i in range(2)]
        h_bf2 = p2.enter_context(nc.sbuf_tensor("h_bf2", [128, D], BF16))
        hT2 = p2.enter_context(nc.sbuf_tensor("hT2", [128, D], BF16))
        q_tok = p2.enter_context(nc.sbuf_tensor("q_tok", [128, D], BF16))
        q_dec = p2.enter_context(nc.sbuf_tensor("q_dec", [128, D], BF16))
        qT = p2.enter_context(nc.sbuf_tensor("qT", [128, D], BF16))
        qTd = p2.enter_context(nc.sbuf_tensor("qTd", [128, D], BF16))
        g1T = p2.enter_context(nc.sbuf_tensor("g1T", [128, 128], BF16))
        sig = p2.enter_context(nc.sbuf_tensor("sig", [128, D], F32))
        scm = p2.enter_context(nc.sbuf_tensor("scm", [128, D], BF16))
        o_sb = p2.enter_context(nc.sbuf_tensor("o_sb", [128, D], F32))
        tmpf = p2.enter_context(nc.sbuf_tensor("tmpf", [128, D], F32))
        orms = p2.enter_context(nc.sbuf_tensor("orms", [128, D], BF16))
        oT_sb = p2.enter_context(nc.sbuf_tensor("oT_sb", [128, D], BF16))
        ms16 = p2.enter_context(nc.sbuf_tensor("ms16", [128, H], F32))
        rmss = p2.enter_context(nc.sbuf_tensor("rmss", [128, H], F32))
        rrr = p2.enter_context(nc.sbuf_tensor("rrr", [128, H], F32))
        pt2 = p2.enter_context(nc.psum_tensor("pt2", [128, D], BF16))
        pq = p2.enter_context(nc.psum_tensor("pq", [128, D], F32))
        psc = p2.enter_context(nc.psum_tensor("psc", [128, 1024], F32))

        @blk.gpsimd
        def _(g):
            i1 = g.dma_start(out=_b3(wq_sb[:, :], D), in_=_w3(wq))
            c.inc("wbig", i1, 16)
            i1 = g.dma_start(out=_b3(wg1_sb[:, :], 128), in_=_w3(wg1))
            c.inc("wbig", i1, 16)
            i1 = g.dma_start(out=wg2_sb[:, :], in_=wg2)
            c.inc("wbig", i1, 16)
            mk = c.inc("gp", g.iota(difT[:, :], pattern=[[1, 128]], base=0,
                                    channel_multiplier=-1,
                                    allow_small_or_imprecise_dtypes=True), 1)
            c.wait(g, "gp", mk)
            i1 = g.affine_select(out=difT[:, :], in_=difT[:, :],
                                 compare_op=ALU.is_ge, fill=0.0,
                                 base=0, pattern=[[1, 128]], channel_multiplier=-1)
            c.inc("gp", i1, 1)

        m_gp_p2w = c.mark("wbig")
        m_gp_p2i = c.mark("gp")

        @blk.scalar
        def _(s):
            c.wait(s, "s", m_gp_p2w)
            c.wait(s, "s", m_gp_p2i)
            for h in range(H):
                i1 = s.activation(Dt[:, h * 128:(h + 1) * 128], difT[:, :], AF.Exp,
                                  scale=ldb[:, h:h + 1])
            c.inc("s", i1, 1)

        m_s_dtexp = c.mark("s")

        @blk.gpsimd
        def _(g):
            c.wait(g, "gp", m_s_dtexp)
            for h in range(H):
                i1 = g.affine_select(out=Dt[:, h * 128:(h + 1) * 128],
                                     in_=Dt[:, h * 128:(h + 1) * 128],
                                     compare_op=ALU.is_ge, fill=0.0,
                                     base=0, pattern=[[1, 128]], channel_multiplier=-1)
            c.inc("gp", i1, 1)

        m_gp_dt = c.mark("gp")

        (m2_x, m2_ld, m2_sq, m2_var, m2_rstds, m2_hbf, m2_tph, m2_hTe,
         m2_q, m2_qe, m2_qdec, m2_tpq, m2_qTe, m2_tpqd, m2_qTde, m2_g1, m2_g1e,
         m2_gate, m2_sig, m2_sc, m2_scm, m2_o, m2_oe, m2_ms, m2_rmss, m2_orms,
         m2_tpo, m2_oTe, m2_oTst) = ({} for _ in range(29))

        rholder = {}

        for i in range(OWN):
            par = i % 2

            @blk.gpsimd
            def _(g, i=i, par=par):
                if i == 0:
                    c.wait(g, "gp", m_sy0)
                    rholder["r"] = g.register("r_off").__enter__()
                    g.reg_load(rholder["r"], own0_sb[0:1, 0:1])
                    g.reg_alu(rholder["r"], rholder["r"], 3 * CHUNK_ELEMS, ALU.mult)
                if i >= 2:
                    c.wait(g, "gp", m2_o[(i - 2, 3)])
                r = rholder["r"]
                i1 = g.dma_start(
                    out=kvs[par][:, :].rearrange("p (j e) -> p j e", e=D),
                    in_=bass.AP(sc_kvs.tensor, r, [[D, 128], [CH * D, 3], [1, D]]))
                m2_ld[i] = c.inc("kvA" if par == 0 else "kvB", i1, 16)
                g.reg_alu(r, r, 3 * CHUNK_ELEMS, ALU.add)

            @blk.sync
            def _(sy, i=i, par=par):
                if i >= 2:
                    c.wait(sy, "sy", m2_hbf[i - 2])
                    c.wait(sy, "sy", m2_sq[i - 2])
                i1 = sy.dma_start(out=xb2[par][:, :], in_=x_own[i * CH:(i + 1) * CH, :])
                m2_x[i] = c.inc("xA" if par == 0 else "xB", i1, 16)

            @blk.scalar
            def _(s, i=i, par=par):
                c.wait(s, "s", m2_x[i])
                i1 = s.activation(junk[:, :], xb2[par][:, :], AF.Square,
                                  accum_out=ssumsq[:, :])
                m2_sq[i] = c.inc("s", i1, 1)

            @blk.vector
            def _(v, i=i, par=par):
                c.wait(v, "v", m2_x[i])
                mk = c.inc("v", v.reduce_sum(smalls[:, 0:1], xb2[par][:, :], axis=AX.X,
                                             negate=True), 1)
                c.wait(v, "v", mk)
                mk = c.inc("v", v.tensor_scalar(smalls[:, 1:2], smalls[:, 0:1],
                                                1.0 / D, None, op0=ALU.mult), 1)
                c.wait(v, "v", mk)
                c.wait(v, "v", m2_sq[i])
                mk = c.inc("v", v.tensor_scalar(smalls[:, 2:3], ssumsq[:, :],
                                                1.0 / D, None, op0=ALU.mult), 1)
                c.wait(v, "v", mk)
                mk = c.inc("v", v.tensor_mul(smalls[:, 3:4], smalls[:, 1:2],
                                             smalls[:, 1:2]), 1)
                c.wait(v, "v", mk)
                i1 = v.tensor_sub(smalls[:, 4:5], smalls[:, 2:3], smalls[:, 3:4])
                m2_var[i] = c.inc("v", i1, 1)

            @blk.scalar
            def _(s, i=i):
                c.wait(s, "s", m2_var[i])
                i1 = s.activation(smalls[:, 5:6], smalls[:, 4:5], AF.Sqrt, bias=eps_sb[:, 0:1])
                m2_rstds[i] = c.inc("s", i1, 1)

            @blk.vector
            def _(v, i=i, par=par):
                c.wait(v, "v", m2_rstds[i])
                v.tensor_scalar(xb2[par][:, :], xb2[par][:, :], smalls[:, 1:2], None,
                                op0=ALU.add)
                i1 = v.reciprocal(smalls[:, 6:7], smalls[:, 5:6])
                mr = c.inc("v", i1, 1)
                c.wait(v, "v", mr)
                mk = c.inc("v", v.scalar_tensor_tensor(xb2[par][:, :], ca1_rep[:, :],
                                                       smalls[:, 6:7], xb2[par][:, :],
                                                       op0=ALU.mult, op1=ALU.mult), 1)
                c.wait(v, "v", mk)
                if i >= 1:
                    c.wait(v, "v", m2_tph[i - 1])
                i1 = v.tensor_add(h_bf2[:, :], xb2[par][:, :], sa_rep[:, :])
                m2_hbf[i] = c.inc("v", i1, 1)

            @blk.tensor
            def _(t, i=i):
                c.wait(t, "t", m2_hbf[i])
                if i >= 1:
                    c.wait(t, "t", m2_oTe[i - 1])
                for dk in range(DK):
                    i1 = t.transpose(pt2[:, dk * 128:(dk + 1) * 128],
                                     h_bf2[:, dk * 128:(dk + 1) * 128], ident[:, :])
                m2_tph[i] = c.inc("t", i1, 1)

            @blk.vector
            def _(v, i=i):
                c.wait(v, "v", m2_tph[i])
                if i >= 1:
                    c.wait(v, "v", m2_g1[i - 1])
                    c.wait(v, "v", m2_q[i - 1])
                for eb in range(4):
                    i1 = v.tensor_copy(hT2[:, eb * 512:(eb + 1) * 512],
                                       pt2[:, eb * 512:(eb + 1) * 512])
                m2_hTe[i] = c.inc("v", i1, 1)

            @blk.tensor
            def _(t, i=i):
                c.wait(t, "t", m2_hTe[i])
                c.wait(t, "t", m_gp_p2w)
                if i >= 1:
                    c.wait(t, "t", m2_oe[(i - 1, 3)])
                for dk in range(DK):
                    for eb in range(4):
                        i1 = t.matmul(pq[:, eb * 512:(eb + 1) * 512],
                                      hT2[:, dk * 128:(dk + 1) * 128],
                                      wq_sb[:, dk * D + eb * 512:dk * D + (eb + 1) * 512],
                                      start=(dk == 0), stop=(dk == DK - 1))
                m2_q[i] = c.inc("t", i1, 1)

            @blk.scalar
            def _(s, i=i):
                c.wait(s, "s", m2_q[i])
                if i >= 1:
                    c.wait(s, "s", m2_tpq[i - 1])
                    c.wait(s, "s", m2_qdec[i - 1])
                for eb in range(4):
                    i1 = s.activation(q_tok[:, eb * 512:(eb + 1) * 512],
                                      pq[:, eb * 512:(eb + 1) * 512], AF.Silu)
                m2_qe[i] = c.inc("s", i1, 1)

            @blk.vector
            def _(v, i=i):
                c.wait(v, "v", m2_qe[i])
                if i >= 1:
                    c.wait(v, "v", m2_tpqd[i - 1])
                for h in range(H):
                    i1 = v.tensor_scalar(q_dec[:, h * 128:(h + 1) * 128],
                                         q_tok[:, h * 128:(h + 1) * 128],
                                         qdec16[:, h:h + 1], None, op0=ALU.mult)
                m2_qdec[i] = c.inc("v", i1, 1)

            @blk.tensor
            def _(t, i=i):
                c.wait(t, "t", m2_qe[i])
                c.wait(t, "t", m2_hTe[i])
                for h in range(H):
                    i1 = t.transpose(pt2[:, h * 128:(h + 1) * 128],
                                     q_tok[:, h * 128:(h + 1) * 128], ident[:, :])
                m2_tpq[i] = c.inc("t", i1, 1)

            @blk.vector
            def _(v, i=i):
                c.wait(v, "v", m2_tpq[i])
                if i >= 1:
                    c.wait(v, "v", m2_sc[(i - 1, 3)])
                for eb in range(4):
                    i1 = v.tensor_copy(qT[:, eb * 512:(eb + 1) * 512],
                                       pt2[:, eb * 512:(eb + 1) * 512])
                m2_qTe[i] = c.inc("v", i1, 1)

            @blk.tensor
            def _(t, i=i):
                c.wait(t, "t", m2_qdec[i])
                c.wait(t, "t", m2_qTe[i])
                for h in range(H):
                    i1 = t.transpose(pt2[:, h * 128:(h + 1) * 128],
                                     q_dec[:, h * 128:(h + 1) * 128], ident[:, :])
                m2_tpqd[i] = c.inc("t", i1, 1)

            @blk.vector
            def _(v, i=i):
                c.wait(v, "v", m2_tpqd[i])
                if i >= 1:
                    c.wait(v, "v", m2_o[(i - 1, 3)])
                for eb in range(4):
                    i1 = v.tensor_copy(qTd[:, eb * 512:(eb + 1) * 512],
                                       pt2[:, eb * 512:(eb + 1) * 512])
                m2_qTde[i] = c.inc("v", i1, 1)

            @blk.tensor
            def _(t, i=i):
                c.wait(t, "t", m2_hTe[i])
                guard = m2_scm.get((i - 1, 2), 0)
                c.wait(t, "t", guard)
                for dk in range(DK):
                    i1 = t.matmul(psc[:, 0:128],
                                  wg1_sb[:, dk * 128:(dk + 1) * 128],
                                  hT2[:, dk * 128:(dk + 1) * 128],
                                  start=(dk == 0), stop=(dk == DK - 1))
                m2_g1[i] = c.inc("t", i1, 1)

            @blk.vector
            def _(v, i=i):
                c.wait(v, "v", m2_g1[i])
                if i >= 1:
                    c.wait(v, "v", m2_gate[i - 1])
                i1 = v.tensor_copy(g1T[:, :], psc[:, 0:128])
                m2_g1e[i] = c.inc("v", i1, 1)

            @blk.tensor
            def _(t, i=i):
                c.wait(t, "t", m2_g1e[i])
                c.wait(t, "t", m2_qe[i])
                for eb in range(4):
                    i1 = t.matmul(pq[:, eb * 512:(eb + 1) * 512], g1T[:, :],
                                  wg2_sb[:, eb * 512:(eb + 1) * 512],
                                  start=True, stop=True)
                m2_gate[i] = c.inc("t", i1, 1)

            @blk.scalar
            def _(s, i=i):
                c.wait(s, "s", m2_gate[i])
                if i >= 1:
                    c.wait(s, "s", m2_ms[i - 1])
                for eb in range(4):
                    i1 = s.activation(sig[:, eb * 512:(eb + 1) * 512],
                                      pq[:, eb * 512:(eb + 1) * 512], AF.Sigmoid)
                m2_sig[i] = c.inc("s", i1, 1)

            for hg in range(4):
                @blk.tensor
                def _(t, i=i, hg=hg, par=par):
                    c.wait(t, "t", m2_qTe[i])
                    c.wait(t, "t", m2_ld[i])
                    guard = m2_scm.get((i, hg - 2), m2_scm.get((i - 1, hg + 2), 0))
                    c.wait(t, "t", guard)
                    if hg == 0:
                        c.wait(t, "t", m2_g1e[i])
                    for h in range(hg * 4, hg * 4 + 4):
                        i1 = t.matmul(psc[:, (hg % 2) * 512 + (h % 4) * 128:
                                         (hg % 2) * 512 + (h % 4 + 1) * 128],
                                      kTb[par][:, h * 128:(h + 1) * 128],
                                      qT[:, h * 128:(h + 1) * 128],
                                      start=True, stop=True)
                    m2_sc[(i, hg)] = c.inc("t", i1, 1)

                @blk.vector
                def _(v, i=i, hg=hg):
                    c.wait(v, "v", m2_sc[(i, hg)])
                    c.wait(v, "v", m_gp_dt)
                    if i >= 1:
                        c.wait(v, "v", m2_o[(i - 1, hg)])
                    i1 = v.tensor_mul(scm[:, hg * 512:(hg + 1) * 512],
                                      psc[:, (hg % 2) * 512:(hg % 2 + 1) * 512],
                                      Dt[:, hg * 512:(hg + 1) * 512])
                    m2_scm[(i, hg)] = c.inc("v", i1, 1)

                @blk.tensor
                def _(t, i=i, hg=hg, par=par):
                    c.wait(t, "t", m2_scm[(i, hg)])
                    c.wait(t, "t", m2_qTde[i])
                    c.wait(t, "t", m2_sig[i])
                    guard = m2_oe.get((i, hg - 2), 0)
                    c.wait(t, "t", guard)
                    for h in range(hg * 4, hg * 4 + 4):
                        sl = slice((hg % 2) * 512 + (h % 4) * 128,
                                   (hg % 2) * 512 + (h % 4 + 1) * 128)
                        t.matmul(pq[:, sl], scm[:, h * 128:(h + 1) * 128],
                                 vbb[par][:, h * 128:(h + 1) * 128],
                                 start=True, stop=False)
                        i1 = t.matmul(pq[:, sl], qTd[:, h * 128:(h + 1) * 128],
                                      Sbb[par][:, h * 128:(h + 1) * 128],
                                      start=False, stop=True)
                    m2_o[(i, hg)] = c.inc("t", i1, 1)

                @blk.vector
                def _(v, i=i, hg=hg):
                    c.wait(v, "v", m2_o[(i, hg)])
                    i1 = v.tensor_copy(o_sb[:, hg * 512:(hg + 1) * 512],
                                       pq[:, (hg % 2) * 512:(hg % 2 + 1) * 512])
                    m2_oe[(i, hg)] = c.inc("v", i1, 1)

            @blk.vector
            def _(v, i=i):
                c.wait(v, "v", m2_sig[i])
                c.wait(v, "v", m2_oe[(i, 3)])
                mk = c.inc("v", v.tensor_mul(o_sb[:, :], o_sb[:, :], sig[:, :]), 1)
                c.wait(v, "v", mk)
                mk = c.inc("v", v.tensor_mul(tmpf[:, :], o_sb[:, :], o_sb[:, :]), 1)
                c.wait(v, "v", mk)
                i1 = v.tensor_reduce(ms16[:, :],
                                     tmpf[:, :].rearrange("p (h q) -> p h q", q=128),
                                     axis=AX.X, op=ALU.add)
                m2_ms[i] = c.inc("v", i1, 1)

            @blk.scalar
            def _(s, i=i):
                c.wait(s, "s", m2_ms[i])
                i1 = s.activation(rmss[:, :], ms16[:, :], AF.Sqrt,
                                  scale=1.0 / HD, bias=eps_sb[:, 0:1])
                m2_rmss[i] = c.inc("s", i1, 1)

            @blk.vector
            def _(v, i=i):
                c.wait(v, "v", m2_rmss[i])
                i1 = v.reciprocal(rrr[:, :], rmss[:, :])
                mr = c.inc("v", i1, 1)
                c.wait(v, "v", mr)
                if i >= 1:
                    c.wait(v, "v", m2_tpo[i - 1])
                for h in range(H):
                    i1 = v.tensor_scalar(orms[:, h * 128:(h + 1) * 128],
                                         o_sb[:, h * 128:(h + 1) * 128],
                                         rrr[:, h:h + 1], None, op0=ALU.mult)
                m2_orms[i] = c.inc("v", i1, 1)

            @blk.tensor
            def _(t, i=i):
                c.wait(t, "t", m2_orms[i])
                c.wait(t, "t", m2_qTde[i])
                for dk in range(DK):
                    i1 = t.transpose(pt2[:, dk * 128:(dk + 1) * 128],
                                     orms[:, dk * 128:(dk + 1) * 128], ident[:, :])
                m2_tpo[i] = c.inc("t", i1, 1)

            @blk.scalar
            def _(s, i=i):
                c.wait(s, "s", m2_tpo[i])
                if i >= 1:
                    c.wait(s, "s", m2_oTst[i - 1])
                for dk in range(DK):
                    i1 = s.activation(oT_sb[:, dk * 128:(dk + 1) * 128],
                                      pt2[:, dk * 128:(dk + 1) * 128], AF.Copy,
                                      scale=normw_fc[:, dk:dk + 1])
                m2_oTe[i] = c.inc("s", i1, 1)

            @blk.sync
            def _(sy, i=i):
                c.wait(sy, "sy", m2_oTe[i])
                i1 = sy.dma_start(out=sc_oT[i * CH:(i + 1) * CH, :], in_=oT_sb[:, :])
                m2_oTst[i] = c.inc("st", i1, 16)

        if upto == 2:
            @blk.gpsimd
            def _(g):
                for sn in ("st", "v", "s", "t"):
                    c.wait(g, "gp", c.mark(sn))
            p2.close()
            return nc
        p2.close()

        bar2 = c.barrier_marks()
        _emit_barrier(blk, c, bar2)

        p3 = ExitStack()
        wo_sb = p3.enter_context(nc.sbuf_tensor("wo_sb", [128, DK * D], BF16))
        ga_rep = p3.enter_context(nc.sbuf_tensor("ga_rep", [128, D], F32))
        oTb = [p3.enter_context(nc.sbuf_tensor(f"oTb{i}", [128, D], BF16)) for i in range(2)]
        xb3 = [p3.enter_context(nc.sbuf_tensor(f"x3_{i}", [128, D], F32)) for i in range(2)]
        xm_sb = p3.enter_context(nc.sbuf_tensor("xm_sb", [128, D], F32))
        tmp3 = p3.enter_context(nc.sbuf_tensor("tmp3", [128, D], F32))
        po = p3.enter_context(nc.psum_tensor("po", [128, D], F32))

        @blk.gpsimd
        def _(g):
            i1 = g.dma_start(out=_b3(wo_sb[:, :], D), in_=_w3(wo))
            c.inc("wbig", i1, 16)

        m3_wo = c.mark("wbig")

        @blk.sync
        def _(sy):
            i1 = sy.dma_start(out=ga_rep[:, :], in_=sc_mod[2 * CH:3 * CH, :])
            c.inc("ld1", i1, 16)

        m3_ga = c.mark("ld1")
        m3_oT, m3_x, m3_mm, m3_xm, m3_st = {}, {}, {}, {}, {}

        for i in range(OWN):
            par = i % 2

            @blk.gpsimd
            def _(g, i=i, par=par):
                if i >= 2:
                    c.wait(g, "gp", m3_mm[i - 2])
                i1 = g.dma_start(out=oTb[par][:, :], in_=sc_oT[i * CH:(i + 1) * CH, :])
                m3_oT[i] = c.inc("oTA" if par == 0 else "oTB", i1, 16)

            @blk.sync
            def _(sy, i=i, par=par):
                if i >= 2:
                    c.wait(sy, "sy", m3_xm[i - 2])
                i1 = sy.dma_start(out=xb3[par][:, :], in_=x_own[i * CH:(i + 1) * CH, :])
                m3_x[i] = c.inc("xA" if par == 0 else "xB", i1, 16)

            @blk.tensor
            def _(t, i=i, par=par):
                c.wait(t, "t", m3_oT[i])
                c.wait(t, "t", m3_wo)
                if i >= 1:
                    c.wait(t, "t", m3_xm[i - 1])
                for dk in range(DK):
                    for eb in range(4):
                        i1 = t.matmul(po[:, eb * 512:(eb + 1) * 512],
                                      oTb[par][:, dk * 128:(dk + 1) * 128],
                                      wo_sb[:, dk * D + eb * 512:dk * D + (eb + 1) * 512],
                                      start=(dk == 0), stop=(dk == DK - 1))
                m3_mm[i] = c.inc("t", i1, 1)

            if dbg and upto == 3 and i == 0:
                @blk.vector
                def _(v):
                    c.wait(v, "v", m3_mm[0])
                    i1 = v.tensor_copy(tmp3[:, :], po[:, :])
                    c.inc("v", i1, 1)

                @blk.sync
                def _(sy):
                    c.wait(sy, "sy", c.mark("v"))
                    i1 = sy.dma_start(out=dbg_att, in_=tmp3[:, :])
                    c.inc("st", i1, 16)
                    i1 = sy.dma_start(out=dbg_oTb, in_=oTb[0][:, :])
                    c.inc("st", i1, 16)

            @blk.vector
            def _(v, i=i, par=par):
                c.wait(v, "v", m3_mm[i])
                if dbg and upto == 3 and i == 0:
                    c.wait(v, "v", c.mark("st"))
                c.wait(v, "v", m3_x[i])
                c.wait(v, "v", m3_ga)
                if i >= 1:
                    c.wait(v, "v", m3_st[i - 1])
                mk = c.inc("v", v.tensor_mul(tmp3[:, :], po[:, :], ga_rep[:, :]), 1)
                c.wait(v, "v", mk)
                i1 = v.tensor_add(xm_sb[:, :], tmp3[:, :], xb3[par][:, :])
                m3_xm[i] = c.inc("v", i1, 1)

            @blk.sync
            def _(sy, i=i):
                c.wait(sy, "sy", m3_xm[i])
                i1 = sy.dma_start(out=sc_xmid[i * CH:(i + 1) * CH, :], in_=xm_sb[:, :])
                m3_st[i] = c.inc("st", i1, 16)

        if upto == 3:
            @blk.gpsimd
            def _(g):
                for sn in ("st", "v", "s", "t"):
                    c.wait(g, "gp", c.mark(sn))
            p3.close()
            return nc
        p3.close()

        bar3 = c.barrier_marks()
        _emit_barrier(blk, c, bar3)

        p4 = ExitStack()
        h2T = p4.enter_context(nc.sbuf_tensor("h2T", [128, DK * TILE], BF16))
        zT = p4.enter_context(nc.sbuf_tensor("zT", [128, ICN * TILE], BF16))
        acc = p4.enter_context(nc.sbuf_tensor("acc", [128, TSUB * D], F32))
        wup = [p4.enter_context(nc.sbuf_tensor(f"wup{i}", [128, DK * 128], BF16))
               for i in range(2)]
        wdn = p4.enter_context(nc.sbuf_tensor("wdn", [128, 4 * D], BF16))
        modb = [p4.enter_context(nc.sbuf_tensor(f"modb{i}", [128, D], F32))
                for i in range(2)]
        gwT_sb = p4.enter_context(nc.sbuf_tensor("gwT_sb", [128, DK * 8], BF16))
        xm4 = p4.enter_context(nc.sbuf_tensor("xm4", [128, D], F32))
        h2bf = p4.enter_context(nc.sbuf_tensor("h2bf", [128, D], BF16))
        silt = p4.enter_context(nc.sbuf_tensor("silt", [128, TILE], F32))
        lgT = p4.enter_context(nc.sbuf_tensor("lgT", [8, TILE], BF16))
        lgt = p4.enter_context(nc.sbuf_tensor("lgt", [128, TSUB * 8], F32))
        rt0 = p4.enter_context(nc.sbuf_tensor("rt0", [128, TSUB * 8], F32))
        rt1 = p4.enter_context(nc.sbuf_tensor("rt1", [128, TSUB * 8], F32))
        rt2 = p4.enter_context(nc.sbuf_tensor("rt2", [128, TSUB * 8], F32))
        rts = p4.enter_context(nc.sbuf_tensor("rts", [128, TSUB], F32))
        comb = p4.enter_context(nc.sbuf_tensor("comb", [128, TSUB * 8], F32))

        @blk.gpsimd
        def _(g):
            i1 = g.dma_start(out=_b3(gwT_sb[:, :], 8), in_=_w3(gwT))
            c.inc("wbig", i1, 16)

        m4_gw = c.mark("wbig")

        def up_src(e10, mat, ic):
            if e10 < 8:
                base = (egT if mat == 0 else euT)[e10]
            else:
                base = (sgT if mat == 0 else suT)[:, (e10 - 8) * ISH:(e10 - 7) * ISH]
            return _w3(base[:, ic * 128:(ic + 1) * 128])

        def dn_src(e10, sw_i):
            r0 = sum(DSW[:sw_i]) * 128
            r1 = r0 + DSW[sw_i] * 128
            if e10 < 8:
                base = edT[e10]
            else:
                base = sdT[(e10 - 8) * ISH:(e10 - 7) * ISH, :]
            return _w3(base[r0:r1, :])

        for tt in range(NTILE):
            bar_t = c.barrier_marks()
            _emit_barrier(blk, c, bar_t)

            t4 = ExitStack()
            pscope = ExitStack()
            pup = t4.enter_context(nc.psum_tensor("pup_%d" % tt, [128, D], F32))
            pt4 = pscope.enter_context(nc.psum_tensor("pt4_%d" % tt, [128, D], BF16))

            m4_xm, m4_sq, m4_var, m4_rstds, m4_hbf = {}, {}, {}, {}, {}
            m4_tph, m4_he = {}, {}

            @blk.sync
            def _(sy, tt=tt):
                i1 = sy.dma_start(out=modb[0][:, :], in_=sc_mod[4 * CH:5 * CH, :])
                c.inc("ld1", i1, 16)
                i1 = sy.dma_start(out=modb[1][:, :], in_=sc_mod[3 * CH:4 * CH, :])
                c.inc("ld1", i1, 16)

            m4_mods = c.mark("ld1")

            for cc in range(TSUB):
                ich = tt * TSUB + cc

                @blk.sync
                def _(sy, ich=ich, cc=cc):
                    c.wait(sy, "sy", m4_hbf.get(cc - 1, 0))
                    c.wait(sy, "sy", m4_sq.get(cc - 1, 0))
                    i1 = sy.dma_start(out=xm4[:, :],
                                      in_=sc_xmid[ich * CH:(ich + 1) * CH, :])
                    m4_xm[cc] = c.inc("xA", i1, 16)

                @blk.scalar
                def _(s, cc=cc):
                    c.wait(s, "s", m4_xm[cc])
                    i1 = s.activation(junk[:, :], xm4[:, :], AF.Square,
                                      accum_out=ssumsq[:, :])
                    m4_sq[cc] = c.inc("s", i1, 1)

                @blk.vector
                def _(v, cc=cc):
                    c.wait(v, "v", m4_xm[cc])
                    mk = c.inc("v", v.reduce_sum(smalls[:, 0:1], xm4[:, :], axis=AX.X,
                                                 negate=True), 1)
                    c.wait(v, "v", mk)
                    mk = c.inc("v", v.tensor_scalar(smalls[:, 1:2], smalls[:, 0:1],
                                                    1.0 / D, None, op0=ALU.mult), 1)
                    c.wait(v, "v", mk)
                    c.wait(v, "v", m4_sq[cc])
                    mk = c.inc("v", v.tensor_scalar(smalls[:, 2:3], ssumsq[:, :],
                                                    1.0 / D, None, op0=ALU.mult), 1)
                    c.wait(v, "v", mk)
                    mk = c.inc("v", v.tensor_mul(smalls[:, 3:4], smalls[:, 1:2],
                                                 smalls[:, 1:2]), 1)
                    c.wait(v, "v", mk)
                    i1 = v.tensor_sub(smalls[:, 4:5], smalls[:, 2:3], smalls[:, 3:4])
                    m4_var[cc] = c.inc("v", i1, 1)

                @blk.scalar
                def _(s, cc=cc):
                    c.wait(s, "s", m4_var[cc])
                    i1 = s.activation(smalls[:, 5:6], smalls[:, 4:5], AF.Sqrt, bias=eps_sb[:, 0:1])
                    m4_rstds[cc] = c.inc("s", i1, 1)

                @blk.vector
                def _(v, cc=cc):
                    c.wait(v, "v", m4_rstds[cc])
                    v.tensor_scalar(xm4[:, :], xm4[:, :], smalls[:, 1:2], None,
                                    op0=ALU.add)
                    i1 = v.reciprocal(smalls[:, 6:7], smalls[:, 5:6])
                    mr = c.inc("v", i1, 1)
                    c.wait(v, "v", mr)
                    c.wait(v, "v", m4_mods)
                    mk = c.inc("v", v.scalar_tensor_tensor(xm4[:, :], modb[0][:, :],
                                                           smalls[:, 6:7], xm4[:, :],
                                                           op0=ALU.mult, op1=ALU.mult), 1)
                    c.wait(v, "v", mk)
                    c.wait(v, "v", m4_tph.get(cc - 1, 0))
                    i1 = v.tensor_add(h2bf[:, :], xm4[:, :], modb[1][:, :])
                    m4_hbf[cc] = c.inc("v", i1, 1)

                @blk.tensor
                def _(t, cc=cc):
                    c.wait(t, "t", m4_hbf[cc])
                    c.wait(t, "t", m4_he.get(cc - 1, 0))
                    for dk in range(DK):
                        i1 = t.transpose(pt4[:, dk * 128:(dk + 1) * 128],
                                         h2bf[:, dk * 128:(dk + 1) * 128], ident[:, :])
                    m4_tph[cc] = c.inc("t", i1, 1)

                @blk.scalar
                def _(s, cc=cc):
                    c.wait(s, "s", m4_tph[cc])
                    for dk in range(DK):
                        i1 = s.copy(h2T[:, dk * TILE + cc * 128:dk * TILE + (cc + 1) * 128],
                                    pt4[:, dk * 128:(dk + 1) * 128])
                    m4_he[cc] = c.inc("s", i1, 1)

            m4_hall = c.mark("s")

            @blk.tensor
            def _(t):
                c.wait(t, "t", m4_hall)
                c.wait(t, "t", m4_gw)
                for th in range(2):
                    for dk in range(DK):
                        i1 = t.matmul(pup[0:8, th * 512:(th + 1) * 512],
                                      gwT_sb[:, dk * 8:(dk + 1) * 8],
                                      h2T[:, dk * TILE + th * 512:dk * TILE + (th + 1) * 512],
                                      start=(dk == 0), stop=(dk == DK - 1))
                c.inc("t", i1, 1)

            m4_rmm = c.mark("t")

            @blk.scalar
            def _(s):
                c.wait(s, "s", m4_rmm)
                i1 = s.copy(lgT[:, :], pup[0:8, 0:1024])
                c.inc("s", i1, 1)

            m4_lge = c.mark("s")

            @blk.tensor
            def _(t):
                c.wait(t, "t", m4_lge)
                for sub in range(TSUB):
                    i1 = t.transpose(pt4[:, sub * 8:(sub + 1) * 8],
                                     lgT[:, sub * 128:(sub + 1) * 128],
                                     ident[0:8, 0:8])
                c.inc("t", i1, 1)

            m4_ltp = c.mark("t")

            @blk.vector
            def _(v):
                c.wait(v, "v", m4_ltp)
                r3 = lambda tns: tns[:, :].rearrange("p (s e) -> p s e", e=8)

                def bc(tns):
                    return tns[:, :].unsqueeze(2).broadcast_to([128, TSUB, 8])

                def sw(i1):
                    mk = c.inc("v", i1, 1)
                    c.wait(v, "v", mk)

                sw(v.tensor_copy(lgt[:, :], pt4[:, 0:TSUB * 8]))
                sw(v.tensor_reduce(rts[:, :], r3(lgt), axis=AX.X, op=ALU.max))
                sw(v.tensor_tensor(r3(rt0), r3(lgt), bc(rts), op=ALU.subtract))

            @blk.scalar
            def _(s):
                c.wait(s, "s", c.mark("v"))
                i1 = s.activation(rt1[:, :], rt0[:, :], AF.Exp)
                c.inc("s", i1, 1)

            m4_exp = c.mark("s")

            @blk.vector
            def _(v):
                c.wait(v, "v", m4_exp)
                r3 = lambda tns: tns[:, :].rearrange("p (s e) -> p s e", e=8)

                def bc(tns):
                    return tns[:, :].unsqueeze(2).broadcast_to([128, TSUB, 8])

                def sw(i1):
                    mk = c.inc("v", i1, 1)
                    c.wait(v, "v", mk)

                sw(v.tensor_reduce(rts[:, :], r3(rt1), axis=AX.X, op=ALU.add))
                sw(v.reciprocal(rts[:, :], rts[:, :]))
                sw(v.tensor_tensor(r3(rt1), r3(rt1), bc(rts), op=ALU.mult))
                sw(v.tensor_reduce(rts[:, :], r3(rt1), axis=AX.X, op=ALU.max))
                sw(v.tensor_tensor(r3(rt2), r3(rt1), bc(rts), op=ALU.subtract))
                sw(v.tensor_scalar(rt2[:, :], rt2[:, :], 0.0, None, op0=ALU.is_ge))
                sw(v.tensor_mul(rt0[:, :], rt1[:, :], rt2[:, :]))
                sw(v.tensor_sub(rt0[:, :], rt1[:, :], rt0[:, :]))
                sw(v.tensor_reduce(rts[:, :], r3(rt0), axis=AX.X, op=ALU.max))
                sw(v.tensor_tensor(r3(rt0), r3(rt0), bc(rts), op=ALU.subtract))
                sw(v.tensor_scalar(rt0[:, :], rt0[:, :], 0.0, None, op0=ALU.is_ge))
                sw(v.tensor_add(rt2[:, :], rt2[:, :], rt0[:, :]))
                sw(v.tensor_mul(rt2[:, :], rt1[:, :], rt2[:, :]))
                sw(v.tensor_reduce(rts[:, :], r3(rt2), axis=AX.X, op=ALU.add))
                sw(v.tensor_scalar(rts[:, :], rts[:, :], 1e-20, None, op0=ALU.add))
                sw(v.reciprocal(rts[:, :], rts[:, :]))
                i1 = v.tensor_tensor(r3(comb), r3(rt2), bc(rts), op=ALU.mult)
                c.inc("v", i1, 1)

            m4_comb = c.mark("v")
            pscope.close()
            pdn = t4.enter_context(nc.psum_tensor("pdn_%d" % tt, [128, D], F32))

            m4_wup, m4_wdn, m4_y, m4_z, m4_dn, m4_ae = {}, {}, {}, {}, {}, {}
            up_items = [(e10, ic, mat) for e10 in range(NE10)
                        for ic in range(ICN) for mat in range(2)]
            up_ptr = [0]

            def emit_up_loads(upto_item, m4_wup=m4_wup, m4_y=m4_y,
                              up_items=up_items, up_ptr=up_ptr):
                while up_ptr[0] <= upto_item and up_ptr[0] < len(up_items):
                    e10, ic, mat = up_items[up_ptr[0]]
                    kidx = up_ptr[0]

                    @blk.gpsimd
                    def _(g, e10=e10, ic=ic, mat=mat, kidx=kidx):
                        if kidx >= 2:
                            pe, pc, pm = up_items[kidx - 2]
                            c.wait(g, "gp", m4_y.get((pe, pc, pm), 0))
                        i1 = g.dma_start(out=_b3(wup[mat][:, :], 128), in_=up_src(e10, mat, ic))
                        m4_wup[(e10, ic, mat)] = c.inc("upA" if mat == 0 else "upB", i1, 16)
                    up_ptr[0] += 1

            for e10 in range(NE10):
                @blk.gpsimd
                def _(g, e10=e10):
                    c.wait(g, "gp", m4_dn.get((e10 - 1, 2, TSUB - 1), 0))
                    i1 = g.dma_start(out=_b3(wdn[:, 0:DSW[0] * D], D),
                                     in_=dn_src(e10, 0))
                    m4_wdn[(e10, 0)] = c.inc("dnw", i1, 16)

                for ic in range(ICN):
                    emit_up_loads(e10 * ICN * 2 + ic * 2 + 1)
                    for mat in range(2):
                        @blk.tensor
                        def _(t, e10=e10, ic=ic, mat=mat):
                            c.wait(t, "t", m4_wup[(e10, ic, mat)])
                            c.wait(t, "t", m4_lge)
                            guard = m4_z.get((e10, ic - 1),
                                             m4_z.get((e10 - 1, ICN - 1), 0))
                            c.wait(t, "t", guard)
                            for dk in range(DK):
                                for th in range(2):
                                    i1 = t.matmul(
                                        pup[:, mat * 1024 + th * 512:
                                            mat * 1024 + (th + 1) * 512],
                                        wup[mat][:, dk * 128:(dk + 1) * 128],
                                        h2T[:, dk * TILE + th * 512:
                                            dk * TILE + (th + 1) * 512],
                                        start=(dk == 0), stop=(dk == DK - 1))
                            m4_y[(e10, ic, mat)] = c.inc("t", i1, 1)

                    @blk.scalar
                    def _(s, e10=e10, ic=ic):
                        c.wait(s, "s", m4_y[(e10, ic, 0)])
                        c.wait(s, "s", m4_z.get((e10, ic - 1),
                                                     m4_z.get((e10 - 1, ICN - 1), 0)))
                        i1 = s.activation(silt[:, :], pup[:, 0:1024], AF.Silu)
                        m4_z[(e10, ic, "s")] = c.inc("s", i1, 1)

                    @blk.vector
                    def _(v, e10=e10, ic=ic):
                        c.wait(v, "v", m4_z[(e10, ic, "s")])
                        c.wait(v, "v", m4_y[(e10, ic, 1)])
                        c.wait(v, "v", m4_dn.get((e10 - 1, 2, TSUB - 1), 0))
                        i1 = v.tensor_mul(zT[:, ic * TILE:(ic + 1) * TILE],
                                          silt[:, :], pup[:, 1024:2048])
                        m4_z[(e10, ic)] = c.inc("v", i1, 1)

                for sw_i in range(3):
                    nic = DSW[sw_i]
                    ic0_base = sum(DSW[:sw_i])
                    if sw_i >= 1:
                        @blk.gpsimd
                        def _(g, e10=e10, sw_i=sw_i, nic=nic):
                            c.wait(g, "gp", m4_dn[(e10, sw_i - 1, TSUB - 1)])
                            i1 = g.dma_start(out=_b3(wdn[:, 0:nic * D], D),
                                             in_=dn_src(e10, sw_i))
                            m4_wdn[(e10, sw_i)] = c.inc("dnw", i1, 16)

                    for sub in range(TSUB):
                        @blk.tensor
                        def _(t, e10=e10, sw_i=sw_i, sub=sub, nic=nic, ic0_base=ic0_base):
                            c.wait(t, "t", m4_wdn[(e10, sw_i)])
                            for ic0 in range(nic):
                                c.wait(t, "t", m4_z[(e10, ic0_base + ic0)])
                            prev = m4_ae.get((e10, sw_i, sub - 1),
                                             m4_ae.get((e10, sw_i - 1, TSUB - 1),
                                                       m4_ae.get((e10 - 1, 2, TSUB - 1), 0)))
                            c.wait(t, "t", prev)
                            for ic0 in range(nic):
                                ic = ic0_base + ic0
                                for eb in range(4):
                                    i1 = t.matmul(
                                        pdn[:, eb * 512:(eb + 1) * 512],
                                        zT[:, ic * TILE + sub * 128:
                                           ic * TILE + (sub + 1) * 128],
                                        wdn[:, ic0 * D + eb * 512:ic0 * D + (eb + 1) * 512],
                                        start=(ic0 == 0), stop=(ic0 == nic - 1))
                            m4_dn[(e10, sw_i, sub)] = c.inc("t", i1, 1)

                        @blk.vector
                        def _(v, e10=e10, sw_i=sw_i, sub=sub):
                            c.wait(v, "v", m4_dn[(e10, sw_i, sub)])
                            c.wait(v, "v", m4_comb)
                            asl = acc[:, sub * D:(sub + 1) * D]
                            if e10 == 0 and sw_i == 0:
                                i1 = v.tensor_scalar(
                                    asl, pdn[:, :],
                                    comb[:, sub * 8 + e10:sub * 8 + e10 + 1],
                                    None, op0=ALU.mult)
                            else:
                                sc = (comb[:, sub * 8 + e10:sub * 8 + e10 + 1]
                                      if e10 < 8 else ones_f[:, 0:1])
                                i1 = v.scalar_tensor_tensor(asl, pdn[:, :], sc, asl,
                                                            op0=ALU.mult, op1=ALU.add)
                            m4_ae[(e10, sw_i, sub)] = c.inc("v", i1, 1)

            @blk.sync
            def _(sy, tt=tt):
                for sub in range(TSUB):
                    c.wait(sy, "sy", m4_ae[(NE10 - 1, 2, sub)])
                i1 = sy.dma_start(out=modb[1][:, :], in_=sc_mod[5 * CH:6 * CH, :])
                c.inc("ld1", i1, 16)

            m4_gb = c.mark("ld1")
            m4_ep, m4_yst = {}, {}

            for cc in range(TSUB):
                ich = tt * TSUB + cc

                @blk.sync
                def _(sy, ich=ich, cc=cc):
                    c.wait(sy, "sy", m4_ep.get(cc - 1, 0))
                    c.wait(sy, "sy", m4_yst.get(cc - 1, 0))
                    i1 = sy.dma_start(out=xm4[:, :],
                                      in_=sc_xmid[ich * CH:(ich + 1) * CH, :])
                    m4_xm[(cc, "ep")] = c.inc("xA", i1, 16)

                @blk.vector
                def _(v, ich=ich, cc=cc):
                    c.wait(v, "v", m4_xm[(cc, "ep")])
                    c.wait(v, "v", m4_gb)
                    for hf in range(2):
                        mk = c.inc("v", v.tensor_mul(
                            silt[:, :],
                            acc[:, cc * D + hf * TILE:cc * D + (hf + 1) * TILE],
                            modb[1][:, hf * TILE:(hf + 1) * TILE]), 1)
                        c.wait(v, "v", mk)
                        mk = c.inc("v", v.tensor_add(
                            xm4[:, hf * TILE:(hf + 1) * TILE], silt[:, :],
                            xm4[:, hf * TILE:(hf + 1) * TILE]), 1)
                        c.wait(v, "v", mk)
                    m4_ep[cc] = mk

                @blk.sync
                def _(sy, ich=ich, cc=cc):
                    c.wait(sy, "sy", m4_ep[cc])
                    i1 = sy.dma_start(out=y_out[ich * CH:(ich + 1) * CH, :],
                                      in_=xm4[:, :])
                    m4_yst[cc] = c.inc("st", i1, 16)

            t4.close()

        @blk.gpsimd
        def _(g):
            for sn in ("st", "v", "s", "t", "ld1"):
                c.wait(g, "gp", c.mark(sn))

        p4.close()

    return nc


def _prep_inputs(inputs):
    a = {k: np.asarray(v) for k, v in inputs.items()}
    w = {}
    w["x"] = np.ascontiguousarray(a["L0_attn_x"], dtype=np.float32)
    w["c"] = np.ascontiguousarray(a["L0_c"], dtype=np.float32)
    w["ld"] = np.ascontiguousarray(a["L0_attn_log_decay"], dtype=np.float32)
    w["wmod"] = np.ascontiguousarray(a["L0_wmod"], dtype=np.float32)
    w["wq"] = np.ascontiguousarray(a["L0_attn_Wq"], dtype=np.float32)
    w["wk"] = np.ascontiguousarray(a["L0_attn_Wk"], dtype=np.float32)
    w["wv"] = np.ascontiguousarray(a["L0_attn_Wv"], dtype=np.float32)
    w["wo"] = np.ascontiguousarray(a["L0_attn_Wo"], dtype=np.float32)
    w["wg1"] = np.ascontiguousarray(a["L0_attn_Wg1"], dtype=np.float32)
    w["wg2"] = np.ascontiguousarray(a["L0_attn_Wg2"], dtype=np.float32)
    w["normw"] = np.ascontiguousarray(a["L0_attn_norm_w"], dtype=np.float32)
    w["gwT"] = np.ascontiguousarray(a["L0_ffn_gate_w"].T, dtype=np.float32)
    w["sgT"] = np.ascontiguousarray(a["L0_ffn_shared_gate"].T, dtype=np.float32)
    w["suT"] = np.ascontiguousarray(a["L0_ffn_shared_up"].T, dtype=np.float32)
    w["sdT"] = np.ascontiguousarray(a["L0_ffn_shared_down"].T, dtype=np.float32)
    w["egT"] = np.ascontiguousarray(a["L0_ffn_expert_gate"].transpose(0, 2, 1),
                                    dtype=np.float32)
    w["euT"] = np.ascontiguousarray(a["L0_ffn_expert_up"].transpose(0, 2, 1),
                                    dtype=np.float32)
    w["edT"] = np.ascontiguousarray(a["L0_ffn_expert_down"].transpose(0, 2, 1),
                                    dtype=np.float32)
    return w


def _core_in_map(w, core, upto=4):
    b, j = core // 2, core % 2
    im = {
        "x_scan": w["x"][b],
        "c_row": w["c"][b],
        "ld": w["ld"],
        "wmod": w["wmod"],
        "wk": w["wk"],
        "wv": w["wv"],
        "own0": np.array([[OWN * j]], dtype=np.int32),
    }
    if upto >= 2:
        im["x_own"] = np.ascontiguousarray(w["x"][b, j * 2048:(j + 1) * 2048])
        im["wq"] = w["wq"]
        im["wg1"] = w["wg1"]
        im["wg2"] = w["wg2"]
        im["normw"] = w["normw"]
    if upto >= 3:
        im["wo"] = w["wo"]
    if upto >= 4:
        im["gwT"] = w["gwT"]
        im["sgT"] = w["sgT"]
        im["suT"] = w["suT"]
        im["sdT"] = w["sdT"]
        im["egT"] = w["egT"]
        im["euT"] = w["euT"]
        im["edT"] = w["edT"]
    return im


def kernel(**inputs):
    w = _prep_inputs(inputs)
    nc = build_program(upto=4)
    in_maps = [_core_in_map(w, core) for core in range(8)]
    res = run_bass_kernel_spmd(nc, in_maps, core_ids=list(range(8)))
    out = np.empty((4, 4096, 2048), dtype=np.float32)
    for core in range(8):
        b, j = core // 2, core % 2
        out[b, j * 2048:(j + 1) * 2048, :] = res.results[core]["y"]
    return out


# version 03
# speedup vs baseline: 1.0358x; 1.0358x over previous
import numpy as np
import concourse.bass as bass
import concourse.mybir as mybir
from concourse.bass_utils import run_bass_kernel_spmd
from contextlib import ExitStack

F32 = mybir.dt.float32
BF16 = mybir.dt.bfloat16
I32 = mybir.dt.int32
AF = mybir.ActivationFunctionType
ALU = mybir.AluOpType
AX = mybir.AxisListType

D = 2048
DK = 16
H = 16
HD = 128
CH = 128
NCH = 32
OWN = 16
EPS = 1e-5
ISH = 1408
ICN = 11
NE10 = 10
TSUB = 8
TILE = 1024
NTILE = 2
CHUNK_ELEMS = CH * D
DSW = (4, 4, 3)

ENGS = ("sy", "gp", "t", "v", "s",
        "xA", "xB", "st", "ld1", "wA", "wB", "wbig",
        "kvst", "Sst", "kvA", "kvB", "oTA", "oTB", "upA", "upB", "dnw")


def _w3(dram2d, p=128):
    return dram2d.rearrange("(k p) e -> p k e", p=p)


def _b3(buf2d, e):
    return buf2d.rearrange("p (k e) -> p k e", e=e)


class Ctx:
    def __init__(self, nc, stack):
        self.nc = nc
        self.h = {}
        self.cnt = {}
        self.lastw = {}
        for n in ENGS:
            self.h[n] = stack.enter_context(nc.semaphore("sem_" + n))
            self.cnt[n] = 0

    def inc(self, name, inst, k=1):
        inst.then_inc(self.h[name], k)
        self.cnt[name] += k
        return (name, self.cnt[name])

    def mark(self, name):
        return (name, self.cnt[name])

    def wait(self, eng, engname, tok):
        if not tok:
            return
        semname, val = tok
        if val <= 0:
            return
        key = (engname, semname)
        if self.lastw.get(key, -1) >= val:
            return
        eng.wait_ge(self.h[semname], val)
        self.lastw[key] = val

    def barrier_marks(self):
        return dict(self.cnt)


def _emit_barrier(blk, c, marks):
    for en, sect in (("sy", blk.sync), ("gp", blk.gpsimd), ("t", blk.tensor),
                     ("v", blk.vector), ("s", blk.scalar)):
        def f(eng, en=en):
            for sn, v in marks.items():
                if sn != en:
                    c.wait(eng, en, (sn, v))
        sect(f)


def build_program(upto=4, dbg=False, nch=NCH, nmb=48):
    nc = bass.Bass("TRN2", target_bir_lowering=False)

    x_scan = nc.dram_tensor("x_scan", [4096, D], F32, kind="ExternalInput").ap()
    c_row = nc.dram_tensor("c_row", [D], F32, kind="ExternalInput").ap()
    ld_in = nc.dram_tensor("ld", [H], F32, kind="ExternalInput").ap()
    wmod = nc.dram_tensor("wmod", [D, 6 * D], F32, kind="ExternalInput").ap()
    wk = nc.dram_tensor("wk", [D, D], F32, kind="ExternalInput").ap()
    wv = nc.dram_tensor("wv", [D, D], F32, kind="ExternalInput").ap()
    own0 = nc.dram_tensor("own0", [1, 1], I32, kind="ExternalInput").ap()
    if upto >= 2:
        x_own = nc.dram_tensor("x_own", [OWN * CH, D], F32, kind="ExternalInput").ap()
        wq = nc.dram_tensor("wq", [D, D], F32, kind="ExternalInput").ap()
        wg1 = nc.dram_tensor("wg1", [D, HD], F32, kind="ExternalInput").ap()
        wg2 = nc.dram_tensor("wg2", [HD, D], F32, kind="ExternalInput").ap()
        normw = nc.dram_tensor("normw", [D], F32, kind="ExternalInput").ap()
    if upto >= 3:
        wo = nc.dram_tensor("wo", [D, D], F32, kind="ExternalInput").ap()
    if upto >= 4:
        gwT = nc.dram_tensor("gwT", [D, 8], F32, kind="ExternalInput").ap()
        sgT = nc.dram_tensor("sgT", [D, 2 * ISH], F32, kind="ExternalInput").ap()
        suT = nc.dram_tensor("suT", [D, 2 * ISH], F32, kind="ExternalInput").ap()
        sdT = nc.dram_tensor("sdT", [2 * ISH, D], F32, kind="ExternalInput").ap()
        egT = nc.dram_tensor("egT", [8, D, ISH], F32, kind="ExternalInput").ap()
        euT = nc.dram_tensor("euT", [8, D, ISH], F32, kind="ExternalInput").ap()
        edT = nc.dram_tensor("edT", [8, ISH, D], F32, kind="ExternalInput").ap()
        y_out = nc.dram_tensor("y", [OWN * CH, D], F32, kind="ExternalOutput").ap()

    def scratch(name, shape, dt, out_at):
        kind = "ExternalOutput" if (dbg and upto == out_at) else "Internal"
        return nc.dram_tensor(name, shape, dt, kind=kind).ap()

    sc_kvs = scratch("sc_kvs", [NCH * 3 * CH, D], BF16, 1)
    sc_mod = scratch("sc_mod", [6 * CH, D], F32, 1)
    dbg_h = (nc.dram_tensor("dbg_h", [CH, D], BF16, kind="ExternalOutput").ap()
             if (dbg and upto == 1) else None)
    dbg_sm = (nc.dram_tensor("dbg_sm", [CH, 8], F32, kind="ExternalOutput").ap()
              if (dbg and upto == 1) else None)
    dbg_x = (nc.dram_tensor("dbg_x", [CH, D], F32, kind="ExternalOutput").ap()
             if (dbg and upto == 1) else None)
    dbg_ss = (nc.dram_tensor("dbg_ss", [CH, 1], F32, kind="ExternalOutput").ap()
              if (dbg and upto == 1) else None)
    sc_oT = scratch("sc_oT", [OWN * CH, D], BF16, 2) if upto >= 2 else None
    sc_xmid = scratch("sc_xmid", [OWN * CH, D], F32, 3) if upto >= 3 else None
    dbg_oTb = (nc.dram_tensor("dbg_oTb", [CH, D], BF16, kind="ExternalOutput").ap()
               if (dbg and upto == 3) else None)
    dbg_att = (nc.dram_tensor("dbg_att", [CH, D], F32, kind="ExternalOutput").ap()
               if (dbg and upto == 3) else None)

    stack = ExitStack()
    with stack:
        c = Ctx(nc, stack)

        ident = stack.enter_context(nc.sbuf_tensor("ident", [128, 128], BF16))
        ones_f = stack.enter_context(nc.sbuf_tensor("ones_f", [128, 128], F32))
        ones_1p = stack.enter_context(nc.sbuf_tensor("ones_1p", [1, 128], F32))
        ldb = stack.enter_context(nc.sbuf_tensor("ldb", [128, H], F32))
        qdec16 = stack.enter_context(nc.sbuf_tensor("qdec16", [128, H], F32))
        kdec16 = stack.enter_context(nc.sbuf_tensor("kdec16", [128, H], F32))
        cdec16 = stack.enter_context(nc.sbuf_tensor("cdec16", [128, H], F32))
        normw_fc = (stack.enter_context(nc.sbuf_tensor("normw_fc", [128, DK], F32))
                    if upto >= 2 else None)
        it_p1 = stack.enter_context(nc.sbuf_tensor("it_p1", [128, 1], F32))
        it_km = stack.enter_context(nc.sbuf_tensor("it_km", [128, 1], F32))
        c128 = stack.enter_context(nc.sbuf_tensor("c128", [128, 1], F32))
        eps_sb = stack.enter_context(nc.sbuf_tensor("eps_sb", [128, 1], F32))
        ld_sb = stack.enter_context(nc.sbuf_tensor("ld_sb", [1, H], F32))
        own0_sb = stack.enter_context(nc.sbuf_tensor("own0_sb", [1, 1], I32))
        smalls = stack.enter_context(nc.sbuf_tensor("smalls", [128, 8], F32))
        ssumsq = stack.enter_context(nc.sbuf_tensor("ssumsq", [128, 1], F32))
        junk = stack.enter_context(nc.sbuf_tensor("junk", [128, D], BF16))
        sa_rep = stack.enter_context(nc.sbuf_tensor("sa_rep", [128, D], F32))
        ca1_rep = stack.enter_context(nc.sbuf_tensor("ca1_rep", [128, D], F32))

        blk = stack.enter_context(nc.Block())

        p1 = ExitStack()
        wk_sb = p1.enter_context(nc.sbuf_tensor("wk_sb", [128, DK * D], BF16))
        wv_sb = p1.enter_context(nc.sbuf_tensor("wv_sb", [128, DK * D], BF16))

        p0 = ExitStack()
        crep = p0.enter_context(nc.sbuf_tensor("crep", [128, D], BF16))
        c_fc = p0.enter_context(nc.sbuf_tensor("c_fc", [128, DK], F32))
        csil = p0.enter_context(nc.sbuf_tensor("csil", [128, DK], F32))
        wmbuf = [p0.enter_context(nc.sbuf_tensor(f"wmbuf{i}", [128, DK * 256], BF16))
                 for i in range(2)]
        modv = p0.enter_context(nc.sbuf_tensor("modv", [128, D], F32))
        modt = [p0.enter_context(nc.sbuf_tensor(f"modt{i}", [128, D], F32))
                for i in range(2)]
        ps_m = p0.enter_context(nc.psum_tensor("ps_m", [128, 1024], F32))
        ps_sm = p0.enter_context(nc.psum_tensor("ps_sm", [128, 16], F32))

        @blk.sync
        def _(sy):
            i1 = sy.dma_start(out=ld_sb[:, :], in_=ld_in[None, :])
            c.inc("ld1", i1, 16)
            i1 = sy.dma_start(out=own0_sb[:, :], in_=own0)
            c.inc("ld1", i1, 16)
            i1 = sy.dma_start(out=c_fc[:, :], in_=c_row.rearrange("(k p) -> p k", p=128),
                              allow_slow_non_contiguous=True)
            c.inc("ld1", i1, 16)
            if upto >= 2:
                i1 = sy.dma_start(out=normw_fc[:, :],
                                  in_=normw.rearrange("(k p) -> p k", p=128),
                                  allow_slow_non_contiguous=True)
                c.inc("ld1", i1, 16)

        m_sy0 = c.mark("ld1")

        @blk.gpsimd
        def _(g):
            mk = c.inc("gp", g.memset(ident[:, :], 0.0), 1)
            g.memset(ones_f[:, :], 1.0)
            c.wait(g, "gp", mk)
            g.affine_select(out=ident[:, :], in_=ident[:, :],
                            compare_op=ALU.not_equal, fill=1.0,
                            base=0, pattern=[[-1, 128]], channel_multiplier=1)
            g.memset(ones_1p[:, :], 1.0)
            g.memset(c128[:, :], 128.0)
            g.memset(eps_sb[:, :], EPS)
            g.iota(it_p1[:, :], pattern=[[0, 1]], base=1, channel_multiplier=1,
                   allow_small_or_imprecise_dtypes=True)
            i1 = g.iota(it_km[:, :], pattern=[[0, 1]], base=127, channel_multiplier=-1,
                        allow_small_or_imprecise_dtypes=True)
            c.inc("gp", i1, 1)

        m_gp_const = c.mark("gp")

        @blk.tensor
        def _(t):
            c.wait(t, "t", m_sy0)
            c.wait(t, "t", m_gp_const)
            i1 = t.matmul(ps_sm[:, :], ones_1p[:, :], ld_sb[:, :], start=True, stop=True)
            c.inc("t", i1, 1)

        m_t_ldb = c.mark("t")

        @blk.vector
        def _(v):
            c.wait(v, "v", m_t_ldb)
            i1 = v.tensor_copy(ldb[:, :], ps_sm[:, :])
            c.inc("v", i1, 1)

        m_v_ldb = c.mark("v")

        @blk.scalar
        def _(s):
            c.wait(s, "s", m_v_ldb)
            for h in range(H):
                s.activation(qdec16[:, h:h + 1], it_p1[:, :], AF.Exp,
                             scale=ldb[:, h:h + 1])
                s.activation(kdec16[:, h:h + 1], it_km[:, :], AF.Exp,
                             scale=ldb[:, h:h + 1])
                s.activation(cdec16[:, h:h + 1], c128[:, :], AF.Exp,
                             scale=ldb[:, h:h + 1])
            i1 = s.activation(csil[:, :], c_fc[:, :], AF.Silu)
            c.inc("s", i1, 1)

        m_s_dec = c.mark("s")

        @blk.vector
        def _(v):
            c.wait(v, "v", m_s_dec)
            for ch in range(DK):
                i1 = v.tensor_scalar(crep[:, ch * 128:(ch + 1) * 128], ones_f[:, :],
                                     csil[:, ch:ch + 1], None, op0=ALU.mult)
            c.inc("v", i1, 1)

        m_v_crep = c.mark("v")

        NMB = nmb
        m_wm, m_mm_mod, m_modev, m_modst = {}, {}, {}, {}

        for nb in range(NMB):
            vec, q = nb // 8, nb % 8

            @blk.gpsimd
            def _(g, nb=nb):
                if nb >= 2:
                    c.wait(g, "gp", m_mm_mod[nb - 2])
                i1 = g.dma_start(
                    out=_b3(wmbuf[nb % 2][:, :], 256),
                    in_=_w3(wmod[:, nb * 256:(nb + 1) * 256]))
                m_wm[nb] = c.inc("wA" if nb % 2 == 0 else "wB", i1, 16)

            @blk.tensor
            def _(t, nb=nb):
                c.wait(t, "t", m_v_crep)
                c.wait(t, "t", m_wm[nb])
                if nb >= 2:
                    c.wait(t, "t", m_modev[nb - 2])
                for dk in range(DK):
                    i1 = t.matmul(ps_m[:, (nb % 2) * 512:(nb % 2) * 512 + 256],
                                  crep[:, dk * 128:(dk + 1) * 128],
                                  wmbuf[nb % 2][:, dk * 256:(dk + 1) * 256],
                                  start=(dk == 0), stop=(dk == DK - 1))
                m_mm_mod[nb] = c.inc("t", i1, 1)

            @blk.vector
            def _(v, nb=nb, vec=vec, q=q):
                c.wait(v, "v", m_mm_mod[nb])
                if q == 0 and nb >= 8:
                    c.wait(v, "v", m_modst[nb - 1])
                src = ps_m[:, (nb % 2) * 512:(nb % 2) * 512 + 256]
                dst = modv[:, q * 256:(q + 1) * 256]
                if vec in (1, 4):
                    i1 = v.tensor_scalar(dst, src, 1.0, None, op0=ALU.add)
                else:
                    i1 = v.tensor_copy(dst, src)
                m_modev[nb] = c.inc("v", i1, 1)
                if q == 7 and vec in (2, 5):
                    c.wait(v, "v", m_modev[nb])
                    v.tensor_scalar(modt[0][:, :], modv[:, :], 0.25, None, op0=ALU.max)
                    mk = c.inc("v", v.tensor_scalar(modt[1][:, :], modv[:, :], -0.25,
                                                    None, op0=ALU.min), 1)
                    c.wait(v, "v", mk)
                    v.tensor_scalar(modv[:, :], modv[:, :], 0.0, None, op0=ALU.is_ge)
                    mk = c.inc("v", v.tensor_sub(modt[0][:, :], modt[0][:, :],
                                                 modt[1][:, :]), 1)
                    c.wait(v, "v", mk)
                    mk = c.inc("v", v.tensor_mul(modv[:, :], modv[:, :],
                                                 modt[0][:, :]), 1)
                    c.wait(v, "v", mk)
                    i1 = v.tensor_add(modv[:, :], modv[:, :], modt[1][:, :])
                    m_modev[nb] = c.inc("v", i1, 1)

            if q == 7:
                @blk.sync
                def _(sy, nb=nb, vec=vec):
                    c.wait(sy, "sy", m_modev[nb])
                    i1 = sy.dma_start(out=sc_mod[vec * CH:(vec + 1) * CH, :],
                                      in_=modv[:, :])
                    m_modst[nb] = c.inc("st", i1, 16)

        @blk.gpsimd
        def _(g):
            i1 = g.dma_start(out=_b3(wk_sb[:, :], D), in_=_w3(wk))
            c.inc("wbig", i1, 16)
            i1 = g.dma_start(out=_b3(wv_sb[:, :], D), in_=_w3(wv))
            c.inc("wbig", i1, 16)

        m_gp_wkv = c.mark("wbig")

        @blk.sync
        def _(sy):
            c.wait(sy, "sy", c.mark("st"))
            i1 = sy.dma_start(out=sa_rep[:, :], in_=sc_mod[0:CH, :])
            c.inc("ld1", i1, 16)
            i1 = sy.dma_start(out=ca1_rep[:, :], in_=sc_mod[CH:2 * CH, :])
            c.inc("ld1", i1, 16)

        p0.close()

        bar0 = c.barrier_marks()
        _emit_barrier(blk, c, bar0)

        xb = [p1.enter_context(nc.sbuf_tensor(f"xb{i}", [128, D], F32)) for i in range(2)]
        h_bf = p1.enter_context(nc.sbuf_tensor("h_bf", [128, D], BF16))
        hT = p1.enter_context(nc.sbuf_tensor("hT", [128, D], BF16))
        k_tok = p1.enter_context(nc.sbuf_tensor("k_tok", [128, D], BF16))
        v_tok = p1.enter_context(nc.sbuf_tensor("v_tok", [128, D], BF16))
        kT_sb = p1.enter_context(nc.sbuf_tensor("kT_sb", [128, D], BF16))
        S_sb = p1.enter_context(nc.sbuf_tensor("S_sb", [128, D], F32))
        stmp = p1.enter_context(nc.sbuf_tensor("stmp", [128, 128], F32))
        pt1 = p1.enter_context(nc.psum_tensor("pt1", [128, D], BF16))
        pkv = p1.enter_context(nc.psum_tensor("pkv", [128, D], F32))
        pS = p1.enter_context(nc.psum_tensor("pS", [128, 1024], F32))

        m_x, m_sq, m_var, m_rstds, m_hbf = {}, {}, {}, {}, {}
        m_tph, m_hTe, m_kv, m_kve, m_kdec = {}, {}, {}, {}, {}
        m_tpk, m_kTe, m_kTst, m_vst = {}, {}, {}, {}
        m_sst, m_dS, m_supd = {}, {}, {}

        @blk.vector
        def _(v):
            i1 = v.memset(S_sb[:, :], 0.0)
            c.inc("v", i1, 1)

        m_v_sinit = c.mark("v")

        for g_ch in range(nch):
            par = g_ch % 2

            @blk.gpsimd
            def _(g, g_ch=g_ch):
                if g_ch >= 1:
                    c.wait(g, "gp", m_supd[(g_ch - 1, 3)])
                else:
                    c.wait(g, "gp", m_v_sinit)
                i1 = g.dma_start(out=sc_kvs[(g_ch * 3 + 2) * CH:(g_ch * 3 + 3) * CH, :],
                                 in_=S_sb[:, :])
                m_sst[g_ch] = c.inc("Sst", i1, 16)

            @blk.sync
            def _(sy, g_ch=g_ch, par=par):
                if g_ch >= 2:
                    c.wait(sy, "sy", m_hbf[g_ch - 2])
                    c.wait(sy, "sy", m_sq[g_ch - 2])
                i1 = sy.dma_start(out=xb[par][:, :],
                                  in_=x_scan[g_ch * CH:(g_ch + 1) * CH, :])
                m_x[g_ch] = c.inc("xA" if par == 0 else "xB", i1, 16)

            @blk.scalar
            def _(s, g_ch=g_ch, par=par):
                c.wait(s, "s", m_x[g_ch])
                i1 = s.activation(junk[:, :], xb[par][:, :], AF.Square,
                                  accum_out=ssumsq[:, :])
                m_sq[g_ch] = c.inc("s", i1, 1)

            @blk.vector
            def _(v, g_ch=g_ch, par=par):
                c.wait(v, "v", m_x[g_ch])
                mk = c.inc("v", v.reduce_sum(smalls[:, 0:1], xb[par][:, :], axis=AX.X,
                                             negate=True), 1)
                c.wait(v, "v", mk)
                mk = c.inc("v", v.tensor_scalar(smalls[:, 1:2], smalls[:, 0:1],
                                                1.0 / D, None, op0=ALU.mult), 1)
                c.wait(v, "v", mk)
                c.wait(v, "v", m_sq[g_ch])
                mk = c.inc("v", v.tensor_scalar(smalls[:, 2:3], ssumsq[:, :],
                                                1.0 / D, None, op0=ALU.mult), 1)
                c.wait(v, "v", mk)
                mk = c.inc("v", v.tensor_mul(smalls[:, 3:4], smalls[:, 1:2],
                                             smalls[:, 1:2]), 1)
                c.wait(v, "v", mk)
                i1 = v.tensor_sub(smalls[:, 4:5], smalls[:, 2:3], smalls[:, 3:4])
                m_var[g_ch] = c.inc("v", i1, 1)

            @blk.scalar
            def _(s, g_ch=g_ch):
                c.wait(s, "s", m_var[g_ch])
                i1 = s.activation(smalls[:, 5:6], smalls[:, 4:5], AF.Sqrt, bias=eps_sb[:, 0:1])
                m_rstds[g_ch] = c.inc("s", i1, 1)

            @blk.vector
            def _(v, g_ch=g_ch, par=par):
                c.wait(v, "v", m_rstds[g_ch])
                v.tensor_scalar(xb[par][:, :], xb[par][:, :], smalls[:, 1:2], None,
                                op0=ALU.add)
                i1 = v.reciprocal(smalls[:, 6:7], smalls[:, 5:6])
                mr = c.inc("v", i1, 1)
                c.wait(v, "v", mr)
                mk = c.inc("v", v.scalar_tensor_tensor(xb[par][:, :], ca1_rep[:, :],
                                                       smalls[:, 6:7], xb[par][:, :],
                                                       op0=ALU.mult, op1=ALU.mult), 1)
                c.wait(v, "v", mk)
                if g_ch >= 1:
                    c.wait(v, "v", m_tph[g_ch - 1])
                i1 = v.tensor_add(h_bf[:, :], xb[par][:, :], sa_rep[:, :])
                m_hbf[g_ch] = c.inc("v", i1, 1)

            if dbg and upto == 1 and g_ch == 0:
                @blk.sync
                def _(sy):
                    c.wait(sy, "sy", m_var[0])
                    i1 = sy.dma_start(out=dbg_x, in_=xb[0][:, :])
                    c.inc("st", i1, 16)
                    i1 = sy.dma_start(out=dbg_ss, in_=ssumsq[:, :])
                    c.inc("st", i1, 16)
                    c.wait(sy, "sy", m_hbf[0])
                    i1 = sy.dma_start(out=dbg_h, in_=h_bf[:, :])
                    c.inc("st", i1, 16)
                    i1 = sy.dma_start(out=dbg_sm, in_=smalls[:, :])
                    c.inc("st", i1, 16)

            @blk.tensor
            def _(t, g_ch=g_ch):
                c.wait(t, "t", m_hbf[g_ch])
                if g_ch >= 1:
                    c.wait(t, "t", m_kTe[g_ch - 1])
                for dk in range(DK):
                    i1 = t.transpose(pt1[:, dk * 128:(dk + 1) * 128],
                                     h_bf[:, dk * 128:(dk + 1) * 128], ident[:, :])
                m_tph[g_ch] = c.inc("t", i1, 1)

            @blk.vector
            def _(v, g_ch=g_ch):
                c.wait(v, "v", m_tph[g_ch])
                if g_ch >= 1:
                    c.wait(v, "v", m_kv[(g_ch - 1, 3)])
                for eb in range(4):
                    i1 = v.tensor_copy(hT[:, eb * 512:(eb + 1) * 512],
                                       pt1[:, eb * 512:(eb + 1) * 512])
                m_hTe[g_ch] = c.inc("v", i1, 1)

            for eb in range(4):
                pp = eb % 2

                @blk.tensor
                def _(t, g_ch=g_ch, eb=eb, pp=pp):
                    c.wait(t, "t", m_hTe[g_ch])
                    c.wait(t, "t", m_gp_wkv)
                    guard = m_kve.get((g_ch, eb - 2), m_kve.get((g_ch - 1, eb + 2), 0))
                    c.wait(t, "t", guard)
                    for dk in range(DK):
                        t.matmul(pkv[:, pp * 1024:pp * 1024 + 512],
                                 hT[:, dk * 128:(dk + 1) * 128],
                                 wk_sb[:, dk * D + eb * 512:dk * D + (eb + 1) * 512],
                                 start=(dk == 0), stop=(dk == DK - 1))
                        i1 = t.matmul(pkv[:, pp * 1024 + 512:pp * 1024 + 1024],
                                      hT[:, dk * 128:(dk + 1) * 128],
                                      wv_sb[:, dk * D + eb * 512:dk * D + (eb + 1) * 512],
                                      start=(dk == 0), stop=(dk == DK - 1))
                    m_kv[(g_ch, eb)] = c.inc("t", i1, 1)

                @blk.scalar
                def _(s, g_ch=g_ch, eb=eb, pp=pp):
                    c.wait(s, "s", m_kv[(g_ch, eb)])
                    if g_ch >= 1 and eb == 0:
                        c.wait(s, "s", m_dS[(g_ch - 1, 3)])
                        c.wait(s, "s", m_vst[g_ch - 1])
                        c.wait(s, "s", m_tpk[g_ch - 1])
                        c.wait(s, "s", m_kdec[g_ch - 1])
                    s.activation(k_tok[:, eb * 512:(eb + 1) * 512],
                                 pkv[:, pp * 1024:pp * 1024 + 512], AF.Silu)
                    i1 = s.activation(v_tok[:, eb * 512:(eb + 1) * 512],
                                      pkv[:, pp * 1024 + 512:pp * 1024 + 1024], AF.Silu)
                    m_kve[(g_ch, eb)] = c.inc("s", i1, 1)

            @blk.tensor
            def _(t, g_ch=g_ch):
                c.wait(t, "t", m_kve[(g_ch, 3)])
                c.wait(t, "t", m_hTe[g_ch])
                for h in range(H):
                    i1 = t.transpose(pt1[:, h * 128:(h + 1) * 128],
                                     k_tok[:, h * 128:(h + 1) * 128], ident[:, :])
                m_tpk[g_ch] = c.inc("t", i1, 1)

            @blk.vector
            def _(v, g_ch=g_ch):
                c.wait(v, "v", m_tpk[g_ch])
                if g_ch >= 1:
                    c.wait(v, "v", m_vst[g_ch - 1])
                for eb in range(4):
                    i1 = v.tensor_copy(kT_sb[:, eb * 512:(eb + 1) * 512],
                                       pt1[:, eb * 512:(eb + 1) * 512])
                m_kTe[g_ch] = c.inc("v", i1, 1)

            @blk.vector
            def _(v, g_ch=g_ch):
                c.wait(v, "v", m_tpk[g_ch])
                if g_ch >= 1:
                    c.wait(v, "v", m_dS[(g_ch - 1, 3)])
                for h in range(H):
                    i1 = v.tensor_scalar(k_tok[:, h * 128:(h + 1) * 128],
                                         k_tok[:, h * 128:(h + 1) * 128],
                                         kdec16[:, h:h + 1], None, op0=ALU.mult)
                m_kdec[g_ch] = c.inc("v", i1, 1)

            @blk.gpsimd
            def _(g, g_ch=g_ch):
                c.wait(g, "gp", m_kTe[g_ch])
                i1 = g.dma_start(out=sc_kvs[(g_ch * 3 + 0) * CH:(g_ch * 3 + 1) * CH, :],
                                 in_=kT_sb[:, :])
                m_kTst[g_ch] = c.inc("kvst", i1, 16)
                c.wait(g, "gp", m_kve[(g_ch, 3)])
                i1 = g.dma_start(out=sc_kvs[(g_ch * 3 + 1) * CH:(g_ch * 3 + 2) * CH, :],
                                 in_=v_tok[:, :])
                m_vst[g_ch] = c.inc("kvst", i1, 16)

            for hg in range(4):
                @blk.tensor
                def _(t, g_ch=g_ch, hg=hg):
                    c.wait(t, "t", m_kdec[g_ch])
                    guard = m_supd.get((g_ch, hg - 2), m_supd.get((g_ch - 1, hg + 2), 0))
                    c.wait(t, "t", guard)
                    for h in range(hg * 4, hg * 4 + 4):
                        i1 = t.matmul(pS[:, (hg % 2) * 512 + (h % 4) * 128:
                                         (hg % 2) * 512 + (h % 4 + 1) * 128],
                                      k_tok[:, h * 128:(h + 1) * 128],
                                      v_tok[:, h * 128:(h + 1) * 128],
                                      start=True, stop=True)
                    m_dS[(g_ch, hg)] = c.inc("t", i1, 1)

                @blk.vector
                def _(v, g_ch=g_ch, hg=hg):
                    c.wait(v, "v", m_dS[(g_ch, hg)])
                    c.wait(v, "v", m_sst[g_ch])
                    for h in range(hg * 4, hg * 4 + 4):
                        i1 = v.scalar_tensor_tensor(
                            S_sb[:, h * 128:(h + 1) * 128],
                            S_sb[:, h * 128:(h + 1) * 128],
                            cdec16[:, h:h + 1],
                            pS[:, (hg % 2) * 512 + (h % 4) * 128:
                               (hg % 2) * 512 + (h % 4 + 1) * 128],
                            op0=ALU.mult, op1=ALU.add)
                    m_supd[(g_ch, hg)] = c.inc("v", i1, 1)

        if upto == 1:
            @blk.gpsimd
            def _(g):
                for sn in ("st", "Sst", "kvst", "v", "s", "t"):
                    c.wait(g, "gp", c.mark(sn))
            p1.close()
            return nc
        p1.close()

        bar1 = c.barrier_marks()
        _emit_barrier(blk, c, bar1)

        p2 = ExitStack()
        wq_sb = p2.enter_context(nc.sbuf_tensor("wq_sb", [128, DK * D], BF16))
        wg1_sb = p2.enter_context(nc.sbuf_tensor("wg1_sb", [128, DK * 128], BF16))
        wg2_sb = p2.enter_context(nc.sbuf_tensor("wg2_sb", [128, D], BF16))
        Dt = p2.enter_context(nc.sbuf_tensor("Dt", [128, H * 128], F32))
        difT = p2.enter_context(nc.sbuf_tensor("difT", [128, 128], F32))
        xb2 = [p2.enter_context(nc.sbuf_tensor(f"x2_{i}", [128, D], F32)) for i in range(2)]
        kvs = [p2.enter_context(nc.sbuf_tensor(f"kvs{i}", [128, 3 * D], BF16))
               for i in range(2)]
        kTb = [kvs[i][:, 0:D] for i in range(2)]
        vbb = [kvs[i][:, D:2 * D] for i in range(2)]
        Sbb = [kvs[i][:, 2 * D:3 * D] for i in range(2)]
        h_bf2 = p2.enter_context(nc.sbuf_tensor("h_bf2", [128, D], BF16))
        hT2 = p2.enter_context(nc.sbuf_tensor("hT2", [128, D], BF16))
        q_tok = p2.enter_context(nc.sbuf_tensor("q_tok", [128, D], BF16))
        q_dec = p2.enter_context(nc.sbuf_tensor("q_dec", [128, D], BF16))
        qT = p2.enter_context(nc.sbuf_tensor("qT", [128, D], BF16))
        qTd = p2.enter_context(nc.sbuf_tensor("qTd", [128, D], BF16))
        g1T = p2.enter_context(nc.sbuf_tensor("g1T", [128, 128], BF16))
        sig = p2.enter_context(nc.sbuf_tensor("sig", [128, D], F32))
        scm = p2.enter_context(nc.sbuf_tensor("scm", [128, D], BF16))
        o_sb = p2.enter_context(nc.sbuf_tensor("o_sb", [128, D], F32))
        tmpf = p2.enter_context(nc.sbuf_tensor("tmpf", [128, D], F32))
        orms = p2.enter_context(nc.sbuf_tensor("orms", [128, D], BF16))
        oT_sb = p2.enter_context(nc.sbuf_tensor("oT_sb", [128, D], BF16))
        ms16 = p2.enter_context(nc.sbuf_tensor("ms16", [128, H], F32))
        rmss = p2.enter_context(nc.sbuf_tensor("rmss", [128, H], F32))
        rrr = p2.enter_context(nc.sbuf_tensor("rrr", [128, H], F32))
        pt2 = p2.enter_context(nc.psum_tensor("pt2", [128, D], BF16))
        pq = p2.enter_context(nc.psum_tensor("pq", [128, D], F32))
        psc = p2.enter_context(nc.psum_tensor("psc", [128, 1024], F32))

        @blk.gpsimd
        def _(g):
            i1 = g.dma_start(out=_b3(wq_sb[:, :], D), in_=_w3(wq))
            c.inc("wbig", i1, 16)
            i1 = g.dma_start(out=_b3(wg1_sb[:, :], 128), in_=_w3(wg1))
            c.inc("wbig", i1, 16)
            i1 = g.dma_start(out=wg2_sb[:, :], in_=wg2)
            c.inc("wbig", i1, 16)
            mk = c.inc("gp", g.iota(difT[:, :], pattern=[[1, 128]], base=0,
                                    channel_multiplier=-1,
                                    allow_small_or_imprecise_dtypes=True), 1)
            c.wait(g, "gp", mk)
            i1 = g.affine_select(out=difT[:, :], in_=difT[:, :],
                                 compare_op=ALU.is_ge, fill=0.0,
                                 base=0, pattern=[[1, 128]], channel_multiplier=-1)
            c.inc("gp", i1, 1)

        m_gp_p2w = c.mark("wbig")
        m_gp_p2i = c.mark("gp")

        @blk.scalar
        def _(s):
            c.wait(s, "s", m_gp_p2w)
            c.wait(s, "s", m_gp_p2i)
            for h in range(H):
                i1 = s.activation(Dt[:, h * 128:(h + 1) * 128], difT[:, :], AF.Exp,
                                  scale=ldb[:, h:h + 1])
            c.inc("s", i1, 1)

        m_s_dtexp = c.mark("s")

        @blk.gpsimd
        def _(g):
            c.wait(g, "gp", m_s_dtexp)
            for h in range(H):
                i1 = g.affine_select(out=Dt[:, h * 128:(h + 1) * 128],
                                     in_=Dt[:, h * 128:(h + 1) * 128],
                                     compare_op=ALU.is_ge, fill=0.0,
                                     base=0, pattern=[[1, 128]], channel_multiplier=-1)
            c.inc("gp", i1, 1)

        m_gp_dt = c.mark("gp")

        (m2_x, m2_ld, m2_sq, m2_var, m2_rstds, m2_hbf, m2_tph, m2_hTe,
         m2_q, m2_qe, m2_qdec, m2_tpq, m2_qTe, m2_tpqd, m2_qTde, m2_g1, m2_g1e,
         m2_gate, m2_sig, m2_sc, m2_scm, m2_o, m2_oe, m2_ms, m2_rmss, m2_orms,
         m2_tpo, m2_oTe, m2_oTst) = ({} for _ in range(29))

        rholder = {}

        for i in range(OWN):
            par = i % 2

            @blk.gpsimd
            def _(g, i=i, par=par):
                if i == 0:
                    c.wait(g, "gp", m_sy0)
                    rholder["r"] = g.register("r_off").__enter__()
                    g.reg_load(rholder["r"], own0_sb[0:1, 0:1])
                    g.reg_alu(rholder["r"], rholder["r"], 3 * CHUNK_ELEMS, ALU.mult)
                if i >= 2:
                    c.wait(g, "gp", m2_o[(i - 2, 1)])
                r = rholder["r"]
                i1 = g.dma_start(
                    out=kvs[par][:, :].rearrange("p (j e) -> p j e", e=D),
                    in_=bass.AP(sc_kvs.tensor, r, [[D, 128], [CH * D, 3], [1, D]]))
                m2_ld[i] = c.inc("kvA" if par == 0 else "kvB", i1, 16)
                g.reg_alu(r, r, 3 * CHUNK_ELEMS, ALU.add)

            @blk.sync
            def _(sy, i=i, par=par):
                if i >= 2:
                    c.wait(sy, "sy", m2_hbf[i - 2])
                    c.wait(sy, "sy", m2_sq[i - 2])
                i1 = sy.dma_start(out=xb2[par][:, :], in_=x_own[i * CH:(i + 1) * CH, :])
                m2_x[i] = c.inc("xA" if par == 0 else "xB", i1, 16)

            @blk.scalar
            def _(s, i=i, par=par):
                c.wait(s, "s", m2_x[i])
                i1 = s.activation(junk[:, :], xb2[par][:, :], AF.Square,
                                  accum_out=ssumsq[:, :])
                m2_sq[i] = c.inc("s", i1, 1)

            @blk.vector
            def _(v, i=i, par=par):
                c.wait(v, "v", m2_x[i])
                mk = c.inc("v", v.reduce_sum(smalls[:, 0:1], xb2[par][:, :], axis=AX.X,
                                             negate=True), 1)
                c.wait(v, "v", mk)
                mk = c.inc("v", v.tensor_scalar(smalls[:, 1:2], smalls[:, 0:1],
                                                1.0 / D, None, op0=ALU.mult), 1)
                c.wait(v, "v", mk)
                c.wait(v, "v", m2_sq[i])
                mk = c.inc("v", v.tensor_scalar(smalls[:, 2:3], ssumsq[:, :],
                                                1.0 / D, None, op0=ALU.mult), 1)
                c.wait(v, "v", mk)
                mk = c.inc("v", v.tensor_mul(smalls[:, 3:4], smalls[:, 1:2],
                                             smalls[:, 1:2]), 1)
                c.wait(v, "v", mk)
                i1 = v.tensor_sub(smalls[:, 4:5], smalls[:, 2:3], smalls[:, 3:4])
                m2_var[i] = c.inc("v", i1, 1)

            @blk.scalar
            def _(s, i=i):
                c.wait(s, "s", m2_var[i])
                i1 = s.activation(smalls[:, 5:6], smalls[:, 4:5], AF.Sqrt, bias=eps_sb[:, 0:1])
                m2_rstds[i] = c.inc("s", i1, 1)

            @blk.vector
            def _(v, i=i, par=par):
                c.wait(v, "v", m2_rstds[i])
                v.tensor_scalar(xb2[par][:, :], xb2[par][:, :], smalls[:, 1:2], None,
                                op0=ALU.add)
                i1 = v.reciprocal(smalls[:, 6:7], smalls[:, 5:6])
                mr = c.inc("v", i1, 1)
                c.wait(v, "v", mr)
                mk = c.inc("v", v.scalar_tensor_tensor(xb2[par][:, :], ca1_rep[:, :],
                                                       smalls[:, 6:7], xb2[par][:, :],
                                                       op0=ALU.mult, op1=ALU.mult), 1)
                c.wait(v, "v", mk)
                if i >= 1:
                    c.wait(v, "v", m2_tph[i - 1])
                i1 = v.tensor_add(h_bf2[:, :], xb2[par][:, :], sa_rep[:, :])
                m2_hbf[i] = c.inc("v", i1, 1)

            @blk.tensor
            def _(t, i=i):
                c.wait(t, "t", m2_hbf[i])
                if i >= 1:
                    c.wait(t, "t", m2_oTe[i - 1])
                for dk in range(DK):
                    i1 = t.transpose(pt2[:, dk * 128:(dk + 1) * 128],
                                     h_bf2[:, dk * 128:(dk + 1) * 128], ident[:, :])
                m2_tph[i] = c.inc("t", i1, 1)

            @blk.vector
            def _(v, i=i):
                c.wait(v, "v", m2_tph[i])
                if i >= 1:
                    c.wait(v, "v", m2_g1[i - 1])
                    c.wait(v, "v", m2_q[i - 1])
                for eb in range(4):
                    i1 = v.tensor_copy(hT2[:, eb * 512:(eb + 1) * 512],
                                       pt2[:, eb * 512:(eb + 1) * 512])
                m2_hTe[i] = c.inc("v", i1, 1)

            @blk.tensor
            def _(t, i=i):
                c.wait(t, "t", m2_hTe[i])
                c.wait(t, "t", m_gp_p2w)
                if i >= 1:
                    c.wait(t, "t", m2_oe[(i - 1, 1)])
                for dk in range(DK):
                    for eb in range(4):
                        i1 = t.matmul(pq[:, eb * 512:(eb + 1) * 512],
                                      hT2[:, dk * 128:(dk + 1) * 128],
                                      wq_sb[:, dk * D + eb * 512:dk * D + (eb + 1) * 512],
                                      start=(dk == 0), stop=(dk == DK - 1))
                m2_q[i] = c.inc("t", i1, 1)

            @blk.scalar
            def _(s, i=i):
                c.wait(s, "s", m2_q[i])
                if i >= 1:
                    c.wait(s, "s", m2_tpq[i - 1])
                    c.wait(s, "s", m2_qdec[i - 1])
                for eb in range(4):
                    i1 = s.activation(q_tok[:, eb * 512:(eb + 1) * 512],
                                      pq[:, eb * 512:(eb + 1) * 512], AF.Silu)
                m2_qe[i] = c.inc("s", i1, 1)

            @blk.vector
            def _(v, i=i):
                c.wait(v, "v", m2_qe[i])
                if i >= 1:
                    c.wait(v, "v", m2_tpqd[i - 1])
                for h in range(H):
                    i1 = v.tensor_scalar(q_dec[:, h * 128:(h + 1) * 128],
                                         q_tok[:, h * 128:(h + 1) * 128],
                                         qdec16[:, h:h + 1], None, op0=ALU.mult)
                m2_qdec[i] = c.inc("v", i1, 1)

            @blk.tensor
            def _(t, i=i):
                c.wait(t, "t", m2_qe[i])
                c.wait(t, "t", m2_hTe[i])
                for h in range(H):
                    i1 = t.transpose(pt2[:, h * 128:(h + 1) * 128],
                                     q_tok[:, h * 128:(h + 1) * 128], ident[:, :])
                m2_tpq[i] = c.inc("t", i1, 1)

            @blk.vector
            def _(v, i=i):
                c.wait(v, "v", m2_tpq[i])
                if i >= 1:
                    c.wait(v, "v", m2_sc[(i - 1, 1)])
                for eb in range(4):
                    i1 = v.tensor_copy(qT[:, eb * 512:(eb + 1) * 512],
                                       pt2[:, eb * 512:(eb + 1) * 512])
                m2_qTe[i] = c.inc("v", i1, 1)

            @blk.tensor
            def _(t, i=i):
                c.wait(t, "t", m2_qdec[i])
                c.wait(t, "t", m2_qTe[i])
                for h in range(H):
                    i1 = t.transpose(pt2[:, h * 128:(h + 1) * 128],
                                     q_dec[:, h * 128:(h + 1) * 128], ident[:, :])
                m2_tpqd[i] = c.inc("t", i1, 1)

            @blk.vector
            def _(v, i=i):
                c.wait(v, "v", m2_tpqd[i])
                if i >= 1:
                    c.wait(v, "v", m2_o[(i - 1, 1)])
                for eb in range(4):
                    i1 = v.tensor_copy(qTd[:, eb * 512:(eb + 1) * 512],
                                       pt2[:, eb * 512:(eb + 1) * 512])
                m2_qTde[i] = c.inc("v", i1, 1)

            @blk.tensor
            def _(t, i=i):
                c.wait(t, "t", m2_hTe[i])
                guard = m2_scm.get((i - 1, 1), 0)
                c.wait(t, "t", guard)
                for dk in range(DK):
                    i1 = t.matmul(psc[:, 0:128],
                                  wg1_sb[:, dk * 128:(dk + 1) * 128],
                                  hT2[:, dk * 128:(dk + 1) * 128],
                                  start=(dk == 0), stop=(dk == DK - 1))
                m2_g1[i] = c.inc("t", i1, 1)

            @blk.vector
            def _(v, i=i):
                c.wait(v, "v", m2_g1[i])
                if i >= 1:
                    c.wait(v, "v", m2_gate[i - 1])
                i1 = v.tensor_copy(g1T[:, :], psc[:, 0:128])
                m2_g1e[i] = c.inc("v", i1, 1)

            @blk.tensor
            def _(t, i=i):
                c.wait(t, "t", m2_g1e[i])
                c.wait(t, "t", m2_qe[i])
                for eb in range(4):
                    i1 = t.matmul(pq[:, eb * 512:(eb + 1) * 512], g1T[:, :],
                                  wg2_sb[:, eb * 512:(eb + 1) * 512],
                                  start=True, stop=True)
                m2_gate[i] = c.inc("t", i1, 1)

            @blk.scalar
            def _(s, i=i):
                c.wait(s, "s", m2_gate[i])
                if i >= 1:
                    c.wait(s, "s", m2_ms[i - 1])
                for eb in range(4):
                    i1 = s.activation(sig[:, eb * 512:(eb + 1) * 512],
                                      pq[:, eb * 512:(eb + 1) * 512], AF.Sigmoid)
                m2_sig[i] = c.inc("s", i1, 1)

            for hg in range(2):
                @blk.tensor
                def _(t, i=i, hg=hg, par=par):
                    c.wait(t, "t", m2_qTe[i])
                    c.wait(t, "t", m2_ld[i])
                    guard = m2_scm.get((i, hg - 1), m2_scm.get((i - 1, 1), 0))
                    c.wait(t, "t", guard)
                    if hg == 0:
                        c.wait(t, "t", m2_g1e[i])
                    for h in range(hg * 8, hg * 8 + 8):
                        i1 = t.matmul(psc[:, (h % 8) * 128:(h % 8 + 1) * 128],
                                      kTb[par][:, h * 128:(h + 1) * 128],
                                      qT[:, h * 128:(h + 1) * 128],
                                      start=True, stop=True)
                    m2_sc[(i, hg)] = c.inc("t", i1, 1)

                @blk.vector
                def _(v, i=i, hg=hg):
                    c.wait(v, "v", m2_sc[(i, hg)])
                    c.wait(v, "v", m_gp_dt)
                    if i >= 1:
                        c.wait(v, "v", m2_o[(i - 1, hg)])
                    i1 = v.tensor_mul(scm[:, hg * 1024:(hg + 1) * 1024],
                                      psc[:, :],
                                      Dt[:, hg * 1024:(hg + 1) * 1024])
                    m2_scm[(i, hg)] = c.inc("v", i1, 1)

                @blk.tensor
                def _(t, i=i, hg=hg, par=par):
                    c.wait(t, "t", m2_scm[(i, hg)])
                    c.wait(t, "t", m2_qTde[i])
                    c.wait(t, "t", m2_sig[i])
                    guard = m2_oe.get((i - 1, hg), 0)
                    c.wait(t, "t", guard)
                    for h in range(hg * 8, hg * 8 + 8):
                        sl = slice((hg % 2) * 1024 + (h % 8) * 128,
                                   (hg % 2) * 1024 + (h % 8 + 1) * 128)
                        t.matmul(pq[:, sl], scm[:, h * 128:(h + 1) * 128],
                                 vbb[par][:, h * 128:(h + 1) * 128],
                                 start=True, stop=False)
                        i1 = t.matmul(pq[:, sl], qTd[:, h * 128:(h + 1) * 128],
                                      Sbb[par][:, h * 128:(h + 1) * 128],
                                      start=False, stop=True)
                    m2_o[(i, hg)] = c.inc("t", i1, 1)

                @blk.vector
                def _(v, i=i, hg=hg):
                    c.wait(v, "v", m2_o[(i, hg)])
                    i1 = v.tensor_copy(o_sb[:, hg * 1024:(hg + 1) * 1024],
                                       pq[:, (hg % 2) * 1024:(hg % 2 + 1) * 1024])
                    m2_oe[(i, hg)] = c.inc("v", i1, 1)

            @blk.vector
            def _(v, i=i):
                c.wait(v, "v", m2_sig[i])
                c.wait(v, "v", m2_oe[(i, 1)])
                mk = c.inc("v", v.tensor_mul(o_sb[:, :], o_sb[:, :], sig[:, :]), 1)
                c.wait(v, "v", mk)
                mk = c.inc("v", v.tensor_mul(tmpf[:, :], o_sb[:, :], o_sb[:, :]), 1)
                c.wait(v, "v", mk)
                i1 = v.tensor_reduce(ms16[:, :],
                                     tmpf[:, :].rearrange("p (h q) -> p h q", q=128),
                                     axis=AX.X, op=ALU.add)
                m2_ms[i] = c.inc("v", i1, 1)

            @blk.scalar
            def _(s, i=i):
                c.wait(s, "s", m2_ms[i])
                i1 = s.activation(rmss[:, :], ms16[:, :], AF.Sqrt,
                                  scale=1.0 / HD, bias=eps_sb[:, 0:1])
                m2_rmss[i] = c.inc("s", i1, 1)

            @blk.vector
            def _(v, i=i):
                c.wait(v, "v", m2_rmss[i])
                i1 = v.reciprocal(rrr[:, :], rmss[:, :])
                mr = c.inc("v", i1, 1)
                c.wait(v, "v", mr)
                if i >= 1:
                    c.wait(v, "v", m2_tpo[i - 1])
                for h in range(H):
                    i1 = v.tensor_scalar(orms[:, h * 128:(h + 1) * 128],
                                         o_sb[:, h * 128:(h + 1) * 128],
                                         rrr[:, h:h + 1], None, op0=ALU.mult)
                m2_orms[i] = c.inc("v", i1, 1)

            @blk.tensor
            def _(t, i=i):
                c.wait(t, "t", m2_orms[i])
                c.wait(t, "t", m2_qTde[i])
                for dk in range(DK):
                    i1 = t.transpose(pt2[:, dk * 128:(dk + 1) * 128],
                                     orms[:, dk * 128:(dk + 1) * 128], ident[:, :])
                m2_tpo[i] = c.inc("t", i1, 1)

            @blk.scalar
            def _(s, i=i):
                c.wait(s, "s", m2_tpo[i])
                if i >= 1:
                    c.wait(s, "s", m2_oTst[i - 1])
                for dk in range(DK):
                    i1 = s.activation(oT_sb[:, dk * 128:(dk + 1) * 128],
                                      pt2[:, dk * 128:(dk + 1) * 128], AF.Copy,
                                      scale=normw_fc[:, dk:dk + 1])
                m2_oTe[i] = c.inc("s", i1, 1)

            @blk.sync
            def _(sy, i=i):
                c.wait(sy, "sy", m2_oTe[i])
                i1 = sy.dma_start(out=sc_oT[i * CH:(i + 1) * CH, :], in_=oT_sb[:, :])
                m2_oTst[i] = c.inc("st", i1, 16)

        if upto == 2:
            @blk.gpsimd
            def _(g):
                for sn in ("st", "v", "s", "t"):
                    c.wait(g, "gp", c.mark(sn))
            p2.close()
            return nc
        p2.close()

        bar2 = c.barrier_marks()
        _emit_barrier(blk, c, bar2)

        p3 = ExitStack()
        wo_sb = p3.enter_context(nc.sbuf_tensor("wo_sb", [128, DK * D], BF16))
        ga_rep = p3.enter_context(nc.sbuf_tensor("ga_rep", [128, D], F32))
        oTb = [p3.enter_context(nc.sbuf_tensor(f"oTb{i}", [128, D], BF16)) for i in range(2)]
        xb3 = [p3.enter_context(nc.sbuf_tensor(f"x3_{i}", [128, D], F32)) for i in range(2)]
        xm_sb = p3.enter_context(nc.sbuf_tensor("xm_sb", [128, D], F32))
        tmp3 = p3.enter_context(nc.sbuf_tensor("tmp3", [128, D], F32))
        po = p3.enter_context(nc.psum_tensor("po", [128, D], F32))

        @blk.gpsimd
        def _(g):
            i1 = g.dma_start(out=_b3(wo_sb[:, :], D), in_=_w3(wo))
            c.inc("wbig", i1, 16)

        m3_wo = c.mark("wbig")

        @blk.sync
        def _(sy):
            i1 = sy.dma_start(out=ga_rep[:, :], in_=sc_mod[2 * CH:3 * CH, :])
            c.inc("ld1", i1, 16)

        m3_ga = c.mark("ld1")
        m3_oT, m3_x, m3_mm, m3_xm, m3_st = {}, {}, {}, {}, {}

        for i in range(OWN):
            par = i % 2

            @blk.gpsimd
            def _(g, i=i, par=par):
                if i >= 2:
                    c.wait(g, "gp", m3_mm[i - 2])
                i1 = g.dma_start(out=oTb[par][:, :], in_=sc_oT[i * CH:(i + 1) * CH, :])
                m3_oT[i] = c.inc("oTA" if par == 0 else "oTB", i1, 16)

            @blk.sync
            def _(sy, i=i, par=par):
                if i >= 2:
                    c.wait(sy, "sy", m3_xm[i - 2])
                i1 = sy.dma_start(out=xb3[par][:, :], in_=x_own[i * CH:(i + 1) * CH, :])
                m3_x[i] = c.inc("xA" if par == 0 else "xB", i1, 16)

            @blk.tensor
            def _(t, i=i, par=par):
                c.wait(t, "t", m3_oT[i])
                c.wait(t, "t", m3_wo)
                if i >= 1:
                    c.wait(t, "t", m3_xm[i - 1])
                for dk in range(DK):
                    for eb in range(4):
                        i1 = t.matmul(po[:, eb * 512:(eb + 1) * 512],
                                      oTb[par][:, dk * 128:(dk + 1) * 128],
                                      wo_sb[:, dk * D + eb * 512:dk * D + (eb + 1) * 512],
                                      start=(dk == 0), stop=(dk == DK - 1))
                m3_mm[i] = c.inc("t", i1, 1)

            if dbg and upto == 3 and i == 0:
                @blk.vector
                def _(v):
                    c.wait(v, "v", m3_mm[0])
                    i1 = v.tensor_copy(tmp3[:, :], po[:, :])
                    c.inc("v", i1, 1)

                @blk.sync
                def _(sy):
                    c.wait(sy, "sy", c.mark("v"))
                    i1 = sy.dma_start(out=dbg_att, in_=tmp3[:, :])
                    c.inc("st", i1, 16)
                    i1 = sy.dma_start(out=dbg_oTb, in_=oTb[0][:, :])
                    c.inc("st", i1, 16)

            @blk.vector
            def _(v, i=i, par=par):
                c.wait(v, "v", m3_mm[i])
                if dbg and upto == 3 and i == 0:
                    c.wait(v, "v", c.mark("st"))
                c.wait(v, "v", m3_x[i])
                c.wait(v, "v", m3_ga)
                if i >= 1:
                    c.wait(v, "v", m3_st[i - 1])
                mk = c.inc("v", v.tensor_mul(tmp3[:, :], po[:, :], ga_rep[:, :]), 1)
                c.wait(v, "v", mk)
                i1 = v.tensor_add(xm_sb[:, :], tmp3[:, :], xb3[par][:, :])
                m3_xm[i] = c.inc("v", i1, 1)

            @blk.sync
            def _(sy, i=i):
                c.wait(sy, "sy", m3_xm[i])
                i1 = sy.dma_start(out=sc_xmid[i * CH:(i + 1) * CH, :], in_=xm_sb[:, :])
                m3_st[i] = c.inc("st", i1, 16)

        if upto == 3:
            @blk.gpsimd
            def _(g):
                for sn in ("st", "v", "s", "t"):
                    c.wait(g, "gp", c.mark(sn))
            p3.close()
            return nc
        p3.close()

        bar3 = c.barrier_marks()
        _emit_barrier(blk, c, bar3)

        p4 = ExitStack()
        h2T = p4.enter_context(nc.sbuf_tensor("h2T", [128, DK * TILE], BF16))
        zT = p4.enter_context(nc.sbuf_tensor("zT", [128, ICN * TILE], BF16))
        acc = p4.enter_context(nc.sbuf_tensor("acc", [128, TSUB * D], F32))
        wup = [p4.enter_context(nc.sbuf_tensor(f"wup{i}", [128, DK * 128], BF16))
               for i in range(2)]
        wdn = p4.enter_context(nc.sbuf_tensor("wdn", [128, 4 * D], BF16))
        modb = [p4.enter_context(nc.sbuf_tensor(f"modb{i}", [128, D], F32))
                for i in range(2)]
        gwT_sb = p4.enter_context(nc.sbuf_tensor("gwT_sb", [128, DK * 8], BF16))
        xm4 = p4.enter_context(nc.sbuf_tensor("xm4", [128, D], F32))
        h2bf = p4.enter_context(nc.sbuf_tensor("h2bf", [128, D], BF16))
        silt = p4.enter_context(nc.sbuf_tensor("silt", [128, TILE], F32))
        lgT = p4.enter_context(nc.sbuf_tensor("lgT", [8, TILE], BF16))
        lgt = p4.enter_context(nc.sbuf_tensor("lgt", [128, TSUB * 8], F32))
        rt0 = p4.enter_context(nc.sbuf_tensor("rt0", [128, TSUB * 8], F32))
        rt1 = p4.enter_context(nc.sbuf_tensor("rt1", [128, TSUB * 8], F32))
        rt2 = p4.enter_context(nc.sbuf_tensor("rt2", [128, TSUB * 8], F32))
        rts = p4.enter_context(nc.sbuf_tensor("rts", [128, TSUB], F32))
        comb = p4.enter_context(nc.sbuf_tensor("comb", [128, TSUB * 8], F32))

        @blk.gpsimd
        def _(g):
            i1 = g.dma_start(out=_b3(gwT_sb[:, :], 8), in_=_w3(gwT))
            c.inc("wbig", i1, 16)

        m4_gw = c.mark("wbig")

        def up_src(e10, mat, ic):
            if e10 < 8:
                base = (egT if mat == 0 else euT)[e10]
            else:
                base = (sgT if mat == 0 else suT)[:, (e10 - 8) * ISH:(e10 - 7) * ISH]
            return _w3(base[:, ic * 128:(ic + 1) * 128])

        def dn_src(e10, sw_i):
            r0 = sum(DSW[:sw_i]) * 128
            r1 = r0 + DSW[sw_i] * 128
            if e10 < 8:
                base = edT[e10]
            else:
                base = sdT[(e10 - 8) * ISH:(e10 - 7) * ISH, :]
            return _w3(base[r0:r1, :])

        for tt in range(NTILE):
            bar_t = c.barrier_marks()
            _emit_barrier(blk, c, bar_t)

            t4 = ExitStack()
            pscope = ExitStack()
            pup = t4.enter_context(nc.psum_tensor("pup_%d" % tt, [128, D], F32))
            pt4 = pscope.enter_context(nc.psum_tensor("pt4_%d" % tt, [128, D], BF16))

            m4_xm, m4_sq, m4_var, m4_rstds, m4_hbf = {}, {}, {}, {}, {}
            m4_tph, m4_he = {}, {}

            @blk.sync
            def _(sy, tt=tt):
                i1 = sy.dma_start(out=modb[0][:, :], in_=sc_mod[4 * CH:5 * CH, :])
                c.inc("ld1", i1, 16)
                i1 = sy.dma_start(out=modb[1][:, :], in_=sc_mod[3 * CH:4 * CH, :])
                c.inc("ld1", i1, 16)

            m4_mods = c.mark("ld1")

            for cc in range(TSUB):
                ich = tt * TSUB + cc

                @blk.sync
                def _(sy, ich=ich, cc=cc):
                    c.wait(sy, "sy", m4_hbf.get(cc - 1, 0))
                    c.wait(sy, "sy", m4_sq.get(cc - 1, 0))
                    i1 = sy.dma_start(out=xm4[:, :],
                                      in_=sc_xmid[ich * CH:(ich + 1) * CH, :])
                    m4_xm[cc] = c.inc("xA", i1, 16)

                @blk.scalar
                def _(s, cc=cc):
                    c.wait(s, "s", m4_xm[cc])
                    i1 = s.activation(junk[:, :], xm4[:, :], AF.Square,
                                      accum_out=ssumsq[:, :])
                    m4_sq[cc] = c.inc("s", i1, 1)

                @blk.vector
                def _(v, cc=cc):
                    c.wait(v, "v", m4_xm[cc])
                    mk = c.inc("v", v.reduce_sum(smalls[:, 0:1], xm4[:, :], axis=AX.X,
                                                 negate=True), 1)
                    c.wait(v, "v", mk)
                    mk = c.inc("v", v.tensor_scalar(smalls[:, 1:2], smalls[:, 0:1],
                                                    1.0 / D, None, op0=ALU.mult), 1)
                    c.wait(v, "v", mk)
                    c.wait(v, "v", m4_sq[cc])
                    mk = c.inc("v", v.tensor_scalar(smalls[:, 2:3], ssumsq[:, :],
                                                    1.0 / D, None, op0=ALU.mult), 1)
                    c.wait(v, "v", mk)
                    mk = c.inc("v", v.tensor_mul(smalls[:, 3:4], smalls[:, 1:2],
                                                 smalls[:, 1:2]), 1)
                    c.wait(v, "v", mk)
                    i1 = v.tensor_sub(smalls[:, 4:5], smalls[:, 2:3], smalls[:, 3:4])
                    m4_var[cc] = c.inc("v", i1, 1)

                @blk.scalar
                def _(s, cc=cc):
                    c.wait(s, "s", m4_var[cc])
                    i1 = s.activation(smalls[:, 5:6], smalls[:, 4:5], AF.Sqrt, bias=eps_sb[:, 0:1])
                    m4_rstds[cc] = c.inc("s", i1, 1)

                @blk.vector
                def _(v, cc=cc):
                    c.wait(v, "v", m4_rstds[cc])
                    v.tensor_scalar(xm4[:, :], xm4[:, :], smalls[:, 1:2], None,
                                    op0=ALU.add)
                    i1 = v.reciprocal(smalls[:, 6:7], smalls[:, 5:6])
                    mr = c.inc("v", i1, 1)
                    c.wait(v, "v", mr)
                    c.wait(v, "v", m4_mods)
                    mk = c.inc("v", v.scalar_tensor_tensor(xm4[:, :], modb[0][:, :],
                                                           smalls[:, 6:7], xm4[:, :],
                                                           op0=ALU.mult, op1=ALU.mult), 1)
                    c.wait(v, "v", mk)
                    c.wait(v, "v", m4_tph.get(cc - 1, 0))
                    i1 = v.tensor_add(h2bf[:, :], xm4[:, :], modb[1][:, :])
                    m4_hbf[cc] = c.inc("v", i1, 1)

                @blk.tensor
                def _(t, cc=cc):
                    c.wait(t, "t", m4_hbf[cc])
                    c.wait(t, "t", m4_he.get(cc - 1, 0))
                    for dk in range(DK):
                        i1 = t.transpose(pt4[:, dk * 128:(dk + 1) * 128],
                                         h2bf[:, dk * 128:(dk + 1) * 128], ident[:, :])
                    m4_tph[cc] = c.inc("t", i1, 1)

                @blk.scalar
                def _(s, cc=cc):
                    c.wait(s, "s", m4_tph[cc])
                    for dk in range(DK):
                        i1 = s.copy(h2T[:, dk * TILE + cc * 128:dk * TILE + (cc + 1) * 128],
                                    pt4[:, dk * 128:(dk + 1) * 128])
                    m4_he[cc] = c.inc("s", i1, 1)

            m4_hall = c.mark("s")

            @blk.tensor
            def _(t):
                c.wait(t, "t", m4_hall)
                c.wait(t, "t", m4_gw)
                for th in range(2):
                    for dk in range(DK):
                        i1 = t.matmul(pup[0:8, th * 512:(th + 1) * 512],
                                      gwT_sb[:, dk * 8:(dk + 1) * 8],
                                      h2T[:, dk * TILE + th * 512:dk * TILE + (th + 1) * 512],
                                      start=(dk == 0), stop=(dk == DK - 1))
                c.inc("t", i1, 1)

            m4_rmm = c.mark("t")

            @blk.scalar
            def _(s):
                c.wait(s, "s", m4_rmm)
                i1 = s.copy(lgT[:, :], pup[0:8, 0:1024])
                c.inc("s", i1, 1)

            m4_lge = c.mark("s")

            @blk.tensor
            def _(t):
                c.wait(t, "t", m4_lge)
                for sub in range(TSUB):
                    i1 = t.transpose(pt4[:, sub * 8:(sub + 1) * 8],
                                     lgT[:, sub * 128:(sub + 1) * 128],
                                     ident[0:8, 0:8])
                c.inc("t", i1, 1)

            m4_ltp = c.mark("t")

            @blk.vector
            def _(v):
                c.wait(v, "v", m4_ltp)
                r3 = lambda tns: tns[:, :].rearrange("p (s e) -> p s e", e=8)

                def bc(tns):
                    return tns[:, :].unsqueeze(2).broadcast_to([128, TSUB, 8])

                def sw(i1):
                    mk = c.inc("v", i1, 1)
                    c.wait(v, "v", mk)

                sw(v.tensor_copy(lgt[:, :], pt4[:, 0:TSUB * 8]))
                sw(v.tensor_reduce(rts[:, :], r3(lgt), axis=AX.X, op=ALU.max))
                sw(v.tensor_tensor(r3(rt0), r3(lgt), bc(rts), op=ALU.subtract))

            @blk.scalar
            def _(s):
                c.wait(s, "s", c.mark("v"))
                i1 = s.activation(rt1[:, :], rt0[:, :], AF.Exp)
                c.inc("s", i1, 1)

            m4_exp = c.mark("s")

            @blk.vector
            def _(v):
                c.wait(v, "v", m4_exp)
                r3 = lambda tns: tns[:, :].rearrange("p (s e) -> p s e", e=8)

                def bc(tns):
                    return tns[:, :].unsqueeze(2).broadcast_to([128, TSUB, 8])

                def sw(i1):
                    mk = c.inc("v", i1, 1)
                    c.wait(v, "v", mk)

                sw(v.tensor_reduce(rts[:, :], r3(rt1), axis=AX.X, op=ALU.add))
                sw(v.reciprocal(rts[:, :], rts[:, :]))
                sw(v.tensor_tensor(r3(rt1), r3(rt1), bc(rts), op=ALU.mult))
                sw(v.tensor_reduce(rts[:, :], r3(rt1), axis=AX.X, op=ALU.max))
                sw(v.tensor_tensor(r3(rt2), r3(rt1), bc(rts), op=ALU.subtract))
                sw(v.tensor_scalar(rt2[:, :], rt2[:, :], 0.0, None, op0=ALU.is_ge))
                sw(v.tensor_mul(rt0[:, :], rt1[:, :], rt2[:, :]))
                sw(v.tensor_sub(rt0[:, :], rt1[:, :], rt0[:, :]))
                sw(v.tensor_reduce(rts[:, :], r3(rt0), axis=AX.X, op=ALU.max))
                sw(v.tensor_tensor(r3(rt0), r3(rt0), bc(rts), op=ALU.subtract))
                sw(v.tensor_scalar(rt0[:, :], rt0[:, :], 0.0, None, op0=ALU.is_ge))
                sw(v.tensor_add(rt2[:, :], rt2[:, :], rt0[:, :]))
                sw(v.tensor_mul(rt2[:, :], rt1[:, :], rt2[:, :]))
                sw(v.tensor_reduce(rts[:, :], r3(rt2), axis=AX.X, op=ALU.add))
                sw(v.tensor_scalar(rts[:, :], rts[:, :], 1e-20, None, op0=ALU.add))
                sw(v.reciprocal(rts[:, :], rts[:, :]))
                i1 = v.tensor_tensor(r3(comb), r3(rt2), bc(rts), op=ALU.mult)
                c.inc("v", i1, 1)

            m4_comb = c.mark("v")
            pscope.close()
            pdn = t4.enter_context(nc.psum_tensor("pdn_%d" % tt, [128, D], F32))

            m4_wup, m4_wdn, m4_y, m4_z, m4_dn, m4_ae = {}, {}, {}, {}, {}, {}
            up_items = [(e10, ic, mat) for e10 in range(NE10)
                        for ic in range(ICN) for mat in range(2)]
            up_ptr = [0]

            def emit_up_loads(upto_item, m4_wup=m4_wup, m4_y=m4_y,
                              up_items=up_items, up_ptr=up_ptr):
                while up_ptr[0] <= upto_item and up_ptr[0] < len(up_items):
                    e10, ic, mat = up_items[up_ptr[0]]
                    kidx = up_ptr[0]

                    @blk.gpsimd
                    def _(g, e10=e10, ic=ic, mat=mat, kidx=kidx):
                        if kidx >= 2:
                            pe, pc, pm = up_items[kidx - 2]
                            c.wait(g, "gp", m4_y.get((pe, pc, pm), 0))
                        i1 = g.dma_start(out=_b3(wup[mat][:, :], 128), in_=up_src(e10, mat, ic))
                        m4_wup[(e10, ic, mat)] = c.inc("upA" if mat == 0 else "upB", i1, 16)
                    up_ptr[0] += 1

            for e10 in range(NE10):
                @blk.gpsimd
                def _(g, e10=e10):
                    c.wait(g, "gp", m4_dn.get((e10 - 1, 2, TSUB - 1), 0))
                    i1 = g.dma_start(out=_b3(wdn[:, 0:DSW[0] * D], D),
                                     in_=dn_src(e10, 0))
                    m4_wdn[(e10, 0)] = c.inc("dnw", i1, 16)

                for ic in range(ICN):
                    emit_up_loads(e10 * ICN * 2 + ic * 2 + 1)
                    for mat in range(2):
                        @blk.tensor
                        def _(t, e10=e10, ic=ic, mat=mat):
                            c.wait(t, "t", m4_wup[(e10, ic, mat)])
                            c.wait(t, "t", m4_lge)
                            guard = m4_z.get((e10, ic - 1),
                                             m4_z.get((e10 - 1, ICN - 1), 0))
                            c.wait(t, "t", guard)
                            c.wait(t, "t", m4_ae.get((e10 - 1, 2, TSUB - 1), 0))
                            for dk in range(DK):
                                for th in range(2):
                                    i1 = t.matmul(
                                        pup[:, mat * 1024 + th * 512:
                                            mat * 1024 + (th + 1) * 512],
                                        wup[mat][:, dk * 128:(dk + 1) * 128],
                                        h2T[:, dk * TILE + th * 512:
                                            dk * TILE + (th + 1) * 512],
                                        start=(dk == 0), stop=(dk == DK - 1))
                            m4_y[(e10, ic, mat)] = c.inc("t", i1, 1)

                    @blk.scalar
                    def _(s, e10=e10, ic=ic):
                        c.wait(s, "s", m4_y[(e10, ic, 0)])
                        c.wait(s, "s", m4_z.get((e10, ic - 1),
                                                     m4_z.get((e10 - 1, ICN - 1), 0)))
                        i1 = s.activation(silt[:, :], pup[:, 0:1024], AF.Silu)
                        m4_z[(e10, ic, "s")] = c.inc("s", i1, 1)

                    @blk.vector
                    def _(v, e10=e10, ic=ic):
                        c.wait(v, "v", m4_z[(e10, ic, "s")])
                        c.wait(v, "v", m4_y[(e10, ic, 1)])
                        c.wait(v, "v", m4_dn.get((e10 - 1, 2, TSUB - 1), 0))
                        i1 = v.tensor_mul(zT[:, ic * TILE:(ic + 1) * TILE],
                                          silt[:, :], pup[:, 1024:2048])
                        m4_z[(e10, ic)] = c.inc("v", i1, 1)

                for sw_i in range(3):
                    nic = DSW[sw_i]
                    ic0_base = sum(DSW[:sw_i])
                    if sw_i >= 1:
                        @blk.gpsimd
                        def _(g, e10=e10, sw_i=sw_i, nic=nic):
                            c.wait(g, "gp", m4_dn[(e10, sw_i - 1, TSUB - 1)])
                            i1 = g.dma_start(out=_b3(wdn[:, 0:nic * D], D),
                                             in_=dn_src(e10, sw_i))
                            m4_wdn[(e10, sw_i)] = c.inc("dnw", i1, 16)

                    for sub in range(TSUB):
                        pbank = pdn if sub % 2 == 0 else pup

                        @blk.tensor
                        def _(t, e10=e10, sw_i=sw_i, sub=sub, nic=nic,
                              ic0_base=ic0_base, pbank=pbank):
                            c.wait(t, "t", m4_wdn[(e10, sw_i)])
                            for ic0 in range(nic):
                                c.wait(t, "t", m4_z[(e10, ic0_base + ic0)])
                            prev = m4_ae.get((e10, sw_i, sub - 2),
                                             m4_ae.get((e10, sw_i - 1, TSUB - 2 + sub),
                                                       m4_ae.get((e10 - 1, 2, TSUB - 2 + sub), 0)))
                            c.wait(t, "t", prev)
                            for ic0 in range(nic):
                                ic = ic0_base + ic0
                                for eb in range(4):
                                    i1 = t.matmul(
                                        pbank[:, eb * 512:(eb + 1) * 512],
                                        zT[:, ic * TILE + sub * 128:
                                           ic * TILE + (sub + 1) * 128],
                                        wdn[:, ic0 * D + eb * 512:ic0 * D + (eb + 1) * 512],
                                        start=(ic0 == 0), stop=(ic0 == nic - 1))
                            m4_dn[(e10, sw_i, sub)] = c.inc("t", i1, 1)

                        @blk.vector
                        def _(v, e10=e10, sw_i=sw_i, sub=sub, pbank=pbank):
                            c.wait(v, "v", m4_dn[(e10, sw_i, sub)])
                            c.wait(v, "v", m4_comb)
                            asl = acc[:, sub * D:(sub + 1) * D]
                            if e10 == 0 and sw_i == 0:
                                i1 = v.tensor_scalar(
                                    asl, pbank[:, :],
                                    comb[:, sub * 8 + e10:sub * 8 + e10 + 1],
                                    None, op0=ALU.mult)
                            else:
                                sc = (comb[:, sub * 8 + e10:sub * 8 + e10 + 1]
                                      if e10 < 8 else ones_f[:, 0:1])
                                i1 = v.scalar_tensor_tensor(asl, pbank[:, :], sc, asl,
                                                            op0=ALU.mult, op1=ALU.add)
                            m4_ae[(e10, sw_i, sub)] = c.inc("v", i1, 1)

            @blk.sync
            def _(sy, tt=tt):
                for sub in range(TSUB):
                    c.wait(sy, "sy", m4_ae[(NE10 - 1, 2, sub)])
                i1 = sy.dma_start(out=modb[1][:, :], in_=sc_mod[5 * CH:6 * CH, :])
                c.inc("ld1", i1, 16)

            m4_gb = c.mark("ld1")
            m4_ep, m4_yst = {}, {}

            for cc in range(TSUB):
                ich = tt * TSUB + cc

                @blk.sync
                def _(sy, ich=ich, cc=cc):
                    c.wait(sy, "sy", m4_ep.get(cc - 1, 0))
                    c.wait(sy, "sy", m4_yst.get(cc - 1, 0))
                    i1 = sy.dma_start(out=xm4[:, :],
                                      in_=sc_xmid[ich * CH:(ich + 1) * CH, :])
                    m4_xm[(cc, "ep")] = c.inc("xA", i1, 16)

                @blk.vector
                def _(v, ich=ich, cc=cc):
                    c.wait(v, "v", m4_xm[(cc, "ep")])
                    c.wait(v, "v", m4_gb)
                    for hf in range(2):
                        mk = c.inc("v", v.tensor_mul(
                            silt[:, :],
                            acc[:, cc * D + hf * TILE:cc * D + (hf + 1) * TILE],
                            modb[1][:, hf * TILE:(hf + 1) * TILE]), 1)
                        c.wait(v, "v", mk)
                        mk = c.inc("v", v.tensor_add(
                            xm4[:, hf * TILE:(hf + 1) * TILE], silt[:, :],
                            xm4[:, hf * TILE:(hf + 1) * TILE]), 1)
                        c.wait(v, "v", mk)
                    m4_ep[cc] = mk

                @blk.sync
                def _(sy, ich=ich, cc=cc):
                    c.wait(sy, "sy", m4_ep[cc])
                    i1 = sy.dma_start(out=y_out[ich * CH:(ich + 1) * CH, :],
                                      in_=xm4[:, :])
                    m4_yst[cc] = c.inc("st", i1, 16)

            t4.close()

        @blk.gpsimd
        def _(g):
            for sn in ("st", "v", "s", "t", "ld1"):
                c.wait(g, "gp", c.mark(sn))

        p4.close()

    return nc


def _prep_inputs(inputs):
    a = {k: np.asarray(v) for k, v in inputs.items()}
    w = {}
    w["x"] = np.ascontiguousarray(a["L0_attn_x"], dtype=np.float32)
    w["c"] = np.ascontiguousarray(a["L0_c"], dtype=np.float32)
    w["ld"] = np.ascontiguousarray(a["L0_attn_log_decay"], dtype=np.float32)
    w["wmod"] = np.ascontiguousarray(a["L0_wmod"], dtype=np.float32)
    w["wq"] = np.ascontiguousarray(a["L0_attn_Wq"], dtype=np.float32)
    w["wk"] = np.ascontiguousarray(a["L0_attn_Wk"], dtype=np.float32)
    w["wv"] = np.ascontiguousarray(a["L0_attn_Wv"], dtype=np.float32)
    w["wo"] = np.ascontiguousarray(a["L0_attn_Wo"], dtype=np.float32)
    w["wg1"] = np.ascontiguousarray(a["L0_attn_Wg1"], dtype=np.float32)
    w["wg2"] = np.ascontiguousarray(a["L0_attn_Wg2"], dtype=np.float32)
    w["normw"] = np.ascontiguousarray(a["L0_attn_norm_w"], dtype=np.float32)
    w["gwT"] = np.ascontiguousarray(a["L0_ffn_gate_w"].T, dtype=np.float32)
    w["sgT"] = np.ascontiguousarray(a["L0_ffn_shared_gate"].T, dtype=np.float32)
    w["suT"] = np.ascontiguousarray(a["L0_ffn_shared_up"].T, dtype=np.float32)
    w["sdT"] = np.ascontiguousarray(a["L0_ffn_shared_down"].T, dtype=np.float32)
    w["egT"] = np.ascontiguousarray(a["L0_ffn_expert_gate"].transpose(0, 2, 1),
                                    dtype=np.float32)
    w["euT"] = np.ascontiguousarray(a["L0_ffn_expert_up"].transpose(0, 2, 1),
                                    dtype=np.float32)
    w["edT"] = np.ascontiguousarray(a["L0_ffn_expert_down"].transpose(0, 2, 1),
                                    dtype=np.float32)
    return w


def _core_in_map(w, core, upto=4):
    b, j = core // 2, core % 2
    im = {
        "x_scan": w["x"][b],
        "c_row": w["c"][b],
        "ld": w["ld"],
        "wmod": w["wmod"],
        "wk": w["wk"],
        "wv": w["wv"],
        "own0": np.array([[OWN * j]], dtype=np.int32),
    }
    if upto >= 2:
        im["x_own"] = np.ascontiguousarray(w["x"][b, j * 2048:(j + 1) * 2048])
        im["wq"] = w["wq"]
        im["wg1"] = w["wg1"]
        im["wg2"] = w["wg2"]
        im["normw"] = w["normw"]
    if upto >= 3:
        im["wo"] = w["wo"]
    if upto >= 4:
        im["gwT"] = w["gwT"]
        im["sgT"] = w["sgT"]
        im["suT"] = w["suT"]
        im["sdT"] = w["sdT"]
        im["egT"] = w["egT"]
        im["euT"] = w["euT"]
        im["edT"] = w["edT"]
    return im


def kernel(**inputs):
    w = _prep_inputs(inputs)
    nc = build_program(upto=4)
    in_maps = [_core_in_map(w, core) for core in range(8)]
    res = run_bass_kernel_spmd(nc, in_maps, core_ids=list(range(8)))
    out = np.empty((4, 4096, 2048), dtype=np.float32)
    for core in range(8):
        b, j = core // 2, core % 2
        out[b, j * 2048:(j + 1) * 2048, :] = res.results[core]["y"]
    return out


# version 04
# speedup vs baseline: 1.1901x; 1.1489x over previous
import numpy as np
import concourse.bass as bass
import concourse.mybir as mybir
from concourse.bass_utils import run_bass_kernel_spmd
from contextlib import ExitStack

F32 = mybir.dt.float32
BF16 = mybir.dt.bfloat16
I32 = mybir.dt.int32
AF = mybir.ActivationFunctionType
ALU = mybir.AluOpType
AX = mybir.AxisListType

D = 2048
DK = 16
H = 16
HD = 128
CH = 128
NCH = 32
OWN = 16
EPS = 1e-5
ISH = 1408
ICN = 11
NE10 = 10
TSUB = 8
TILE = 1024
NTILE = 2
CHUNK_ELEMS = CH * D
DSW = (4, 4, 3)

ENGS = ("sy", "gp", "t", "v", "s",
        "xA", "xB", "st", "ld1", "wA", "wB", "wbig",
        "kvst", "Sst", "kvA", "kvB", "oTA", "oTB", "upA", "upB", "dnw")


def _w3(dram2d, p=128):
    return dram2d.rearrange("(k p) e -> p k e", p=p)


def _b3(buf2d, e):
    return buf2d.rearrange("p (k e) -> p k e", e=e)


class Ctx:
    def __init__(self, nc, stack):
        self.nc = nc
        self.h = {}
        self.cnt = {}
        self.lastw = {}
        for n in ENGS:
            self.h[n] = stack.enter_context(nc.semaphore("sem_" + n))
            self.cnt[n] = 0

    def inc(self, name, inst, k=1):
        inst.then_inc(self.h[name], k)
        self.cnt[name] += k
        return (name, self.cnt[name])

    def mark(self, name):
        return (name, self.cnt[name])

    def wait(self, eng, engname, tok):
        if not tok:
            return
        semname, val = tok
        if val <= 0:
            return
        key = (engname, semname)
        if self.lastw.get(key, -1) >= val:
            return
        eng.wait_ge(self.h[semname], val)
        self.lastw[key] = val

    def barrier_marks(self):
        return dict(self.cnt)


def _emit_barrier(blk, c, marks):
    for en, sect in (("sy", blk.sync), ("gp", blk.gpsimd), ("t", blk.tensor),
                     ("v", blk.vector), ("s", blk.scalar)):
        def f(eng, en=en):
            for sn, v in marks.items():
                if sn != en:
                    c.wait(eng, en, (sn, v))
        sect(f)


def build_program(upto=4, dbg=False, nch=NCH, nmb=48):
    nc = bass.Bass("TRN2", target_bir_lowering=False)

    x_scan = nc.dram_tensor("x_scan", [4096, D], F32, kind="ExternalInput").ap()
    c_row = nc.dram_tensor("c_row", [D], F32, kind="ExternalInput").ap()
    ld_in = nc.dram_tensor("ld", [H], F32, kind="ExternalInput").ap()
    wmod = nc.dram_tensor("wmod", [D, 6 * D], F32, kind="ExternalInput").ap()
    wk = nc.dram_tensor("wk", [D, D], F32, kind="ExternalInput").ap()
    wv = nc.dram_tensor("wv", [D, D], F32, kind="ExternalInput").ap()
    own0 = nc.dram_tensor("own0", [1, 1], I32, kind="ExternalInput").ap()
    if upto >= 2:
        x_own = nc.dram_tensor("x_own", [OWN * CH, D], F32, kind="ExternalInput").ap()
        wq = nc.dram_tensor("wq", [D, D], F32, kind="ExternalInput").ap()
        wg1 = nc.dram_tensor("wg1", [D, HD], F32, kind="ExternalInput").ap()
        wg2 = nc.dram_tensor("wg2", [HD, D], F32, kind="ExternalInput").ap()
        normw = nc.dram_tensor("normw", [D], F32, kind="ExternalInput").ap()
    if upto >= 3:
        wo = nc.dram_tensor("wo", [D, D], F32, kind="ExternalInput").ap()
    if upto >= 4:
        gwT = nc.dram_tensor("gwT", [D, 8], F32, kind="ExternalInput").ap()
        sgT = nc.dram_tensor("sgT", [D, 2 * ISH], F32, kind="ExternalInput").ap()
        suT = nc.dram_tensor("suT", [D, 2 * ISH], F32, kind="ExternalInput").ap()
        sdT = nc.dram_tensor("sdT", [2 * ISH, D], F32, kind="ExternalInput").ap()
        egT = nc.dram_tensor("egT", [8, D, ISH], F32, kind="ExternalInput").ap()
        euT = nc.dram_tensor("euT", [8, D, ISH], F32, kind="ExternalInput").ap()
        edT = nc.dram_tensor("edT", [8, ISH, D], F32, kind="ExternalInput").ap()
        y_out = nc.dram_tensor("y", [OWN * CH, D], F32, kind="ExternalOutput").ap()

    def scratch(name, shape, dt, out_at):
        kind = "ExternalOutput" if (dbg and upto == out_at) else "Internal"
        return nc.dram_tensor(name, shape, dt, kind=kind).ap()

    sc_kvs = scratch("sc_kvs", [NCH * 3 * CH, D], BF16, 1)
    sc_mod = scratch("sc_mod", [6 * CH, D], F32, 1)
    dbg_h = (nc.dram_tensor("dbg_h", [CH, D], BF16, kind="ExternalOutput").ap()
             if (dbg and upto == 1) else None)
    dbg_sm = (nc.dram_tensor("dbg_sm", [CH, 8], F32, kind="ExternalOutput").ap()
              if (dbg and upto == 1) else None)
    dbg_x = (nc.dram_tensor("dbg_x", [CH, D], F32, kind="ExternalOutput").ap()
             if (dbg and upto == 1) else None)
    dbg_ss = (nc.dram_tensor("dbg_ss", [CH, 1], F32, kind="ExternalOutput").ap()
              if (dbg and upto == 1) else None)
    sc_oT = scratch("sc_oT", [OWN * CH, D], BF16, 2) if upto >= 2 else None
    sc_xmid = scratch("sc_xmid", [OWN * CH, D], F32, 3) if upto >= 3 else None
    dbg_oTb = (nc.dram_tensor("dbg_oTb", [CH, D], BF16, kind="ExternalOutput").ap()
               if (dbg and upto == 3) else None)
    dbg_att = (nc.dram_tensor("dbg_att", [CH, D], F32, kind="ExternalOutput").ap()
               if (dbg and upto == 3) else None)

    stack = ExitStack()
    with stack:
        c = Ctx(nc, stack)

        ident = stack.enter_context(nc.sbuf_tensor("ident", [128, 128], BF16))
        ones_f = stack.enter_context(nc.sbuf_tensor("ones_f", [128, 128], F32))
        ones_1p = stack.enter_context(nc.sbuf_tensor("ones_1p", [1, 128], F32))
        ldb = stack.enter_context(nc.sbuf_tensor("ldb", [128, H], F32))
        qdec16 = stack.enter_context(nc.sbuf_tensor("qdec16", [128, H], F32))
        kdec16 = stack.enter_context(nc.sbuf_tensor("kdec16", [128, H], F32))
        cdec16 = stack.enter_context(nc.sbuf_tensor("cdec16", [128, H], F32))
        normw_fc = (stack.enter_context(nc.sbuf_tensor("normw_fc", [128, DK], F32))
                    if upto >= 2 else None)
        it_p1 = stack.enter_context(nc.sbuf_tensor("it_p1", [128, 1], F32))
        it_km = stack.enter_context(nc.sbuf_tensor("it_km", [128, 1], F32))
        c128 = stack.enter_context(nc.sbuf_tensor("c128", [128, 1], F32))
        eps_sb = stack.enter_context(nc.sbuf_tensor("eps_sb", [128, 1], F32))
        ld_sb = stack.enter_context(nc.sbuf_tensor("ld_sb", [1, H], F32))
        own0_sb = stack.enter_context(nc.sbuf_tensor("own0_sb", [1, 1], I32))
        smalls = stack.enter_context(nc.sbuf_tensor("smalls", [128, 8], F32))
        ssumsq = stack.enter_context(nc.sbuf_tensor("ssumsq", [128, 1], F32))
        junk = stack.enter_context(nc.sbuf_tensor("junk", [128, D], BF16))
        sa_rep = stack.enter_context(nc.sbuf_tensor("sa_rep", [128, D], F32))
        ca1_rep = stack.enter_context(nc.sbuf_tensor("ca1_rep", [128, D], F32))

        blk = stack.enter_context(nc.Block())

        p1 = ExitStack()
        wk_sb = p1.enter_context(nc.sbuf_tensor("wk_sb", [128, DK * D], BF16))
        wv_sb = p1.enter_context(nc.sbuf_tensor("wv_sb", [128, DK * D], BF16))

        p0 = ExitStack()
        crep = p0.enter_context(nc.sbuf_tensor("crep", [128, D], BF16))
        c_fc = p0.enter_context(nc.sbuf_tensor("c_fc", [128, DK], F32))
        csil = p0.enter_context(nc.sbuf_tensor("csil", [128, DK], F32))
        wmbuf = [p0.enter_context(nc.sbuf_tensor(f"wmbuf{i}", [128, DK * 256], BF16))
                 for i in range(2)]
        modv = p0.enter_context(nc.sbuf_tensor("modv", [128, D], F32))
        modt = [p0.enter_context(nc.sbuf_tensor(f"modt{i}", [128, D], F32))
                for i in range(2)]
        ps_m = p0.enter_context(nc.psum_tensor("ps_m", [128, 1024], F32))
        ps_sm = p0.enter_context(nc.psum_tensor("ps_sm", [128, 16], F32))

        @blk.sync
        def _(sy):
            i1 = sy.dma_start(out=ld_sb[:, :], in_=ld_in[None, :])
            c.inc("ld1", i1, 16)
            i1 = sy.dma_start(out=own0_sb[:, :], in_=own0)
            c.inc("ld1", i1, 16)
            i1 = sy.dma_start(out=c_fc[:, :], in_=c_row.rearrange("(k p) -> p k", p=128),
                              allow_slow_non_contiguous=True)
            c.inc("ld1", i1, 16)
            if upto >= 2:
                i1 = sy.dma_start(out=normw_fc[:, :],
                                  in_=normw.rearrange("(k p) -> p k", p=128),
                                  allow_slow_non_contiguous=True)
                c.inc("ld1", i1, 16)

        m_sy0 = c.mark("ld1")

        @blk.gpsimd
        def _(g):
            mk = c.inc("gp", g.memset(ident[:, :], 0.0), 1)
            g.memset(ones_f[:, :], 1.0)
            c.wait(g, "gp", mk)
            g.affine_select(out=ident[:, :], in_=ident[:, :],
                            compare_op=ALU.not_equal, fill=1.0,
                            base=0, pattern=[[-1, 128]], channel_multiplier=1)
            g.memset(ones_1p[:, :], 1.0)
            g.memset(c128[:, :], 128.0)
            g.memset(eps_sb[:, :], EPS)
            g.iota(it_p1[:, :], pattern=[[0, 1]], base=1, channel_multiplier=1,
                   allow_small_or_imprecise_dtypes=True)
            i1 = g.iota(it_km[:, :], pattern=[[0, 1]], base=127, channel_multiplier=-1,
                        allow_small_or_imprecise_dtypes=True)
            c.inc("gp", i1, 1)

        m_gp_const = c.mark("gp")

        @blk.tensor
        def _(t):
            c.wait(t, "t", m_sy0)
            c.wait(t, "t", m_gp_const)
            i1 = t.matmul(ps_sm[:, :], ones_1p[:, :], ld_sb[:, :], start=True, stop=True)
            c.inc("t", i1, 1)

        m_t_ldb = c.mark("t")

        @blk.vector
        def _(v):
            c.wait(v, "v", m_t_ldb)
            i1 = v.tensor_copy(ldb[:, :], ps_sm[:, :])
            c.inc("v", i1, 1)

        m_v_ldb = c.mark("v")

        @blk.scalar
        def _(s):
            c.wait(s, "s", m_v_ldb)
            for h in range(H):
                s.activation(qdec16[:, h:h + 1], it_p1[:, :], AF.Exp,
                             scale=ldb[:, h:h + 1])
                s.activation(kdec16[:, h:h + 1], it_km[:, :], AF.Exp,
                             scale=ldb[:, h:h + 1])
                s.activation(cdec16[:, h:h + 1], c128[:, :], AF.Exp,
                             scale=ldb[:, h:h + 1])
            i1 = s.activation(csil[:, :], c_fc[:, :], AF.Silu)
            c.inc("s", i1, 1)

        m_s_dec = c.mark("s")

        @blk.vector
        def _(v):
            c.wait(v, "v", m_s_dec)
            for ch in range(DK):
                i1 = v.tensor_scalar(crep[:, ch * 128:(ch + 1) * 128], ones_f[:, :],
                                     csil[:, ch:ch + 1], None, op0=ALU.mult)
            c.inc("v", i1, 1)

        m_v_crep = c.mark("v")

        NMB = nmb
        m_wm, m_mm_mod, m_modev, m_modst = {}, {}, {}, {}

        for nb in range(NMB):
            vec, q = nb // 8, nb % 8

            @blk.gpsimd
            def _(g, nb=nb):
                if nb >= 2:
                    c.wait(g, "gp", m_mm_mod[nb - 2])
                i1 = g.dma_start(
                    out=_b3(wmbuf[nb % 2][:, :], 256),
                    in_=_w3(wmod[:, nb * 256:(nb + 1) * 256]))
                m_wm[nb] = c.inc("wA" if nb % 2 == 0 else "wB", i1, 16)

            @blk.tensor
            def _(t, nb=nb):
                c.wait(t, "t", m_v_crep)
                c.wait(t, "t", m_wm[nb])
                if nb >= 2:
                    c.wait(t, "t", m_modev[nb - 2])
                for dk in range(DK):
                    i1 = t.matmul(ps_m[:, (nb % 2) * 512:(nb % 2) * 512 + 256],
                                  crep[:, dk * 128:(dk + 1) * 128],
                                  wmbuf[nb % 2][:, dk * 256:(dk + 1) * 256],
                                  start=(dk == 0), stop=(dk == DK - 1))
                m_mm_mod[nb] = c.inc("t", i1, 1)

            @blk.vector
            def _(v, nb=nb, vec=vec, q=q):
                c.wait(v, "v", m_mm_mod[nb])
                if q == 0 and nb >= 8:
                    c.wait(v, "v", m_modst[nb - 1])
                src = ps_m[:, (nb % 2) * 512:(nb % 2) * 512 + 256]
                dst = modv[:, q * 256:(q + 1) * 256]
                if vec in (1, 4):
                    i1 = v.tensor_scalar(dst, src, 1.0, None, op0=ALU.add)
                else:
                    i1 = v.tensor_copy(dst, src)
                m_modev[nb] = c.inc("v", i1, 1)
                if q == 7 and vec in (2, 5):
                    c.wait(v, "v", m_modev[nb])
                    v.tensor_scalar(modt[0][:, :], modv[:, :], 0.25, None, op0=ALU.max)
                    mk = c.inc("v", v.tensor_scalar(modt[1][:, :], modv[:, :], -0.25,
                                                    None, op0=ALU.min), 1)
                    c.wait(v, "v", mk)
                    v.tensor_scalar(modv[:, :], modv[:, :], 0.0, None, op0=ALU.is_ge)
                    mk = c.inc("v", v.tensor_sub(modt[0][:, :], modt[0][:, :],
                                                 modt[1][:, :]), 1)
                    c.wait(v, "v", mk)
                    mk = c.inc("v", v.tensor_mul(modv[:, :], modv[:, :],
                                                 modt[0][:, :]), 1)
                    c.wait(v, "v", mk)
                    i1 = v.tensor_add(modv[:, :], modv[:, :], modt[1][:, :])
                    m_modev[nb] = c.inc("v", i1, 1)

            if q == 7:
                @blk.sync
                def _(sy, nb=nb, vec=vec):
                    c.wait(sy, "sy", m_modev[nb])
                    i1 = sy.dma_start(out=sc_mod[vec * CH:(vec + 1) * CH, :],
                                      in_=modv[:, :])
                    m_modst[nb] = c.inc("st", i1, 16)

        @blk.gpsimd
        def _(g):
            i1 = g.dma_start(out=_b3(wk_sb[:, :], D), in_=_w3(wk))
            c.inc("wbig", i1, 16)
            i1 = g.dma_start(out=_b3(wv_sb[:, :], D), in_=_w3(wv))
            c.inc("wbig", i1, 16)

        m_gp_wkv = c.mark("wbig")

        @blk.sync
        def _(sy):
            c.wait(sy, "sy", c.mark("st"))
            i1 = sy.dma_start(out=sa_rep[:, :], in_=sc_mod[0:CH, :])
            c.inc("ld1", i1, 16)
            i1 = sy.dma_start(out=ca1_rep[:, :], in_=sc_mod[CH:2 * CH, :])
            c.inc("ld1", i1, 16)

        p0.close()

        bar0 = c.barrier_marks()
        _emit_barrier(blk, c, bar0)

        xb = [p1.enter_context(nc.sbuf_tensor(f"xb{i}", [128, D], F32)) for i in range(2)]
        h_bf = p1.enter_context(nc.sbuf_tensor("h_bf", [128, D], BF16))
        hT = p1.enter_context(nc.sbuf_tensor("hT", [128, D], BF16))
        k_tok = p1.enter_context(nc.sbuf_tensor("k_tok", [128, D], BF16))
        v_tok = p1.enter_context(nc.sbuf_tensor("v_tok", [128, D], BF16))
        kT_sb = p1.enter_context(nc.sbuf_tensor("kT_sb", [128, D], BF16))
        S_sb = p1.enter_context(nc.sbuf_tensor("S_sb", [128, D], F32))
        stmp = p1.enter_context(nc.sbuf_tensor("stmp", [128, 128], F32))
        pt1 = p1.enter_context(nc.psum_tensor("pt1", [128, D], BF16))
        pkv = p1.enter_context(nc.psum_tensor("pkv", [128, D], F32))
        pS = p1.enter_context(nc.psum_tensor("pS", [128, 1024], F32))

        m_x, m_sq, m_var, m_rstds, m_hbf = {}, {}, {}, {}, {}
        m_tph, m_hTe, m_kv, m_kve, m_kdec = {}, {}, {}, {}, {}
        m_tpk, m_kTe, m_kTst, m_vst = {}, {}, {}, {}
        m_sst, m_dS, m_supd = {}, {}, {}

        @blk.vector
        def _(v):
            i1 = v.memset(S_sb[:, :], 0.0)
            c.inc("v", i1, 1)

        m_v_sinit = c.mark("v")

        def p1_store(g_ch):
            @blk.gpsimd
            def _(g, g_ch=g_ch):
                if g_ch >= 1:
                    c.wait(g, "gp", m_supd[(g_ch - 1, 3)])
                else:
                    c.wait(g, "gp", m_v_sinit)
                i1 = g.dma_start(out=sc_kvs[(g_ch * 3 + 2) * CH:(g_ch * 3 + 3) * CH, :],
                                 in_=S_sb[:, :])
                m_sst[g_ch] = c.inc("Sst", i1, 16)

        def p1_xload(g_ch):
            par = g_ch % 2

            @blk.sync
            def _(sy, g_ch=g_ch, par=par):
                if g_ch >= 2:
                    c.wait(sy, "sy", m_hbf[g_ch - 2])
                    c.wait(sy, "sy", m_sq[g_ch - 2])
                i1 = sy.dma_start(out=xb[par][:, :],
                                  in_=x_scan[g_ch * CH:(g_ch + 1) * CH, :])
                m_x[g_ch] = c.inc("xA" if par == 0 else "xB", i1, 16)

        def p1_ln(g_ch):
            par = g_ch % 2

            @blk.scalar
            def _(s, g_ch=g_ch, par=par):
                c.wait(s, "s", m_x[g_ch])
                i1 = s.activation(junk[:, :], xb[par][:, :], AF.Square,
                                  accum_out=ssumsq[:, :])
                m_sq[g_ch] = c.inc("s", i1, 1)

            @blk.vector
            def _(v, g_ch=g_ch, par=par):
                c.wait(v, "v", m_x[g_ch])
                mk = c.inc("v", v.reduce_sum(smalls[:, 0:1], xb[par][:, :], axis=AX.X,
                                             negate=True), 1)
                c.wait(v, "v", mk)
                mk = c.inc("v", v.tensor_scalar(smalls[:, 1:2], smalls[:, 0:1],
                                                1.0 / D, None, op0=ALU.mult), 1)
                c.wait(v, "v", mk)
                c.wait(v, "v", m_sq[g_ch])
                mk = c.inc("v", v.tensor_scalar(smalls[:, 2:3], ssumsq[:, :],
                                                1.0 / D, None, op0=ALU.mult), 1)
                c.wait(v, "v", mk)
                mk = c.inc("v", v.tensor_mul(smalls[:, 3:4], smalls[:, 1:2],
                                             smalls[:, 1:2]), 1)
                c.wait(v, "v", mk)
                i1 = v.tensor_sub(smalls[:, 4:5], smalls[:, 2:3], smalls[:, 3:4])
                m_var[g_ch] = c.inc("v", i1, 1)

            @blk.scalar
            def _(s, g_ch=g_ch):
                c.wait(s, "s", m_var[g_ch])
                i1 = s.activation(smalls[:, 5:6], smalls[:, 4:5], AF.Sqrt,
                                  bias=eps_sb[:, 0:1])
                m_rstds[g_ch] = c.inc("s", i1, 1)

            @blk.vector
            def _(v, g_ch=g_ch, par=par):
                c.wait(v, "v", m_rstds[g_ch])
                v.tensor_scalar(xb[par][:, :], xb[par][:, :], smalls[:, 1:2], None,
                                op0=ALU.add)
                i1 = v.reciprocal(smalls[:, 6:7], smalls[:, 5:6])
                mr = c.inc("v", i1, 1)
                c.wait(v, "v", mr)
                mk = c.inc("v", v.scalar_tensor_tensor(xb[par][:, :], ca1_rep[:, :],
                                                       smalls[:, 6:7], xb[par][:, :],
                                                       op0=ALU.mult, op1=ALU.mult), 1)
                c.wait(v, "v", mk)
                if g_ch >= 1:
                    c.wait(v, "v", m_tph[g_ch - 1])
                i1 = v.tensor_add(h_bf[:, :], xb[par][:, :], sa_rep[:, :])
                m_hbf[g_ch] = c.inc("v", i1, 1)

        def p1_rest(g_ch):
            par = g_ch % 2

            @blk.tensor
            def _(t, g_ch=g_ch):
                c.wait(t, "t", m_hbf[g_ch])
                if g_ch >= 1:
                    c.wait(t, "t", m_kTe[g_ch - 1])
                for dk in range(DK):
                    i1 = t.transpose(pt1[:, dk * 128:(dk + 1) * 128],
                                     h_bf[:, dk * 128:(dk + 1) * 128], ident[:, :])
                m_tph[g_ch] = c.inc("t", i1, 1)

            @blk.vector
            def _(v, g_ch=g_ch):
                c.wait(v, "v", m_tph[g_ch])
                if g_ch >= 1:
                    c.wait(v, "v", m_kv[(g_ch - 1, 3)])
                for eb in range(4):
                    i1 = v.tensor_copy(hT[:, eb * 512:(eb + 1) * 512],
                                       pt1[:, eb * 512:(eb + 1) * 512])
                m_hTe[g_ch] = c.inc("v", i1, 1)

            if g_ch + 1 < nch:
                p1_xload(g_ch + 1)
                p1_ln(g_ch + 1)

            for eb in range(4):
                pp = eb % 2

                @blk.tensor
                def _(t, g_ch=g_ch, eb=eb, pp=pp):
                    c.wait(t, "t", m_hTe[g_ch])
                    c.wait(t, "t", m_gp_wkv)
                    guard = m_kve.get((g_ch, eb - 2), m_kve.get((g_ch - 1, eb + 2), 0))
                    c.wait(t, "t", guard)
                    for dk in range(DK):
                        t.matmul(pkv[:, pp * 1024:pp * 1024 + 512],
                                 hT[:, dk * 128:(dk + 1) * 128],
                                 wk_sb[:, dk * D + eb * 512:dk * D + (eb + 1) * 512],
                                 start=(dk == 0), stop=(dk == DK - 1))
                        i1 = t.matmul(pkv[:, pp * 1024 + 512:pp * 1024 + 1024],
                                      hT[:, dk * 128:(dk + 1) * 128],
                                      wv_sb[:, dk * D + eb * 512:dk * D + (eb + 1) * 512],
                                      start=(dk == 0), stop=(dk == DK - 1))
                    m_kv[(g_ch, eb)] = c.inc("t", i1, 1)

                @blk.scalar
                def _(s, g_ch=g_ch, eb=eb, pp=pp):
                    c.wait(s, "s", m_kv[(g_ch, eb)])
                    if g_ch >= 1 and eb == 0:
                        c.wait(s, "s", m_dS[(g_ch - 1, 3)])
                        c.wait(s, "s", m_vst[g_ch - 1])
                        c.wait(s, "s", m_tpk[g_ch - 1])
                        c.wait(s, "s", m_kdec[g_ch - 1])
                    s.activation(k_tok[:, eb * 512:(eb + 1) * 512],
                                 pkv[:, pp * 1024:pp * 1024 + 512], AF.Silu)
                    i1 = s.activation(v_tok[:, eb * 512:(eb + 1) * 512],
                                      pkv[:, pp * 1024 + 512:pp * 1024 + 1024], AF.Silu)
                    m_kve[(g_ch, eb)] = c.inc("s", i1, 1)

            @blk.tensor
            def _(t, g_ch=g_ch):
                c.wait(t, "t", m_kve[(g_ch, 3)])
                c.wait(t, "t", m_hTe[g_ch])
                for h in range(H):
                    i1 = t.transpose(pt1[:, h * 128:(h + 1) * 128],
                                     k_tok[:, h * 128:(h + 1) * 128], ident[:, :])
                m_tpk[g_ch] = c.inc("t", i1, 1)

            @blk.vector
            def _(v, g_ch=g_ch):
                c.wait(v, "v", m_tpk[g_ch])
                if g_ch >= 1:
                    c.wait(v, "v", m_vst[g_ch - 1])
                for eb in range(4):
                    i1 = v.tensor_copy(kT_sb[:, eb * 512:(eb + 1) * 512],
                                       pt1[:, eb * 512:(eb + 1) * 512])
                m_kTe[g_ch] = c.inc("v", i1, 1)

            @blk.vector
            def _(v, g_ch=g_ch):
                c.wait(v, "v", m_tpk[g_ch])
                if g_ch >= 1:
                    c.wait(v, "v", m_dS[(g_ch - 1, 3)])
                for h in range(H):
                    i1 = v.tensor_scalar(k_tok[:, h * 128:(h + 1) * 128],
                                         k_tok[:, h * 128:(h + 1) * 128],
                                         kdec16[:, h:h + 1], None, op0=ALU.mult)
                m_kdec[g_ch] = c.inc("v", i1, 1)

            @blk.gpsimd
            def _(g, g_ch=g_ch):
                c.wait(g, "gp", m_kTe[g_ch])
                i1 = g.dma_start(out=sc_kvs[(g_ch * 3 + 0) * CH:(g_ch * 3 + 1) * CH, :],
                                 in_=kT_sb[:, :])
                m_kTst[g_ch] = c.inc("kvst", i1, 16)
                c.wait(g, "gp", m_kve[(g_ch, 3)])
                i1 = g.dma_start(out=sc_kvs[(g_ch * 3 + 1) * CH:(g_ch * 3 + 2) * CH, :],
                                 in_=v_tok[:, :])
                m_vst[g_ch] = c.inc("kvst", i1, 16)

            for hg in range(4):
                @blk.tensor
                def _(t, g_ch=g_ch, hg=hg):
                    c.wait(t, "t", m_kdec[g_ch])
                    guard = m_supd.get((g_ch, hg - 2), m_supd.get((g_ch - 1, hg + 2), 0))
                    c.wait(t, "t", guard)
                    for h in range(hg * 4, hg * 4 + 4):
                        i1 = t.matmul(pS[:, (hg % 2) * 512 + (h % 4) * 128:
                                         (hg % 2) * 512 + (h % 4 + 1) * 128],
                                      k_tok[:, h * 128:(h + 1) * 128],
                                      v_tok[:, h * 128:(h + 1) * 128],
                                      start=True, stop=True)
                    m_dS[(g_ch, hg)] = c.inc("t", i1, 1)

                @blk.vector
                def _(v, g_ch=g_ch, hg=hg):
                    c.wait(v, "v", m_dS[(g_ch, hg)])
                    c.wait(v, "v", m_sst[g_ch])
                    for h in range(hg * 4, hg * 4 + 4):
                        i1 = v.scalar_tensor_tensor(
                            S_sb[:, h * 128:(h + 1) * 128],
                            S_sb[:, h * 128:(h + 1) * 128],
                            cdec16[:, h:h + 1],
                            pS[:, (hg % 2) * 512 + (h % 4) * 128:
                               (hg % 2) * 512 + (h % 4 + 1) * 128],
                            op0=ALU.mult, op1=ALU.add)
                    m_supd[(g_ch, hg)] = c.inc("v", i1, 1)

        p1_xload(0)
        p1_ln(0)
        for g_ch in range(nch):
            p1_store(g_ch)
            p1_rest(g_ch)

        if upto == 1:
            @blk.gpsimd
            def _(g):
                for sn in ("st", "Sst", "kvst", "v", "s", "t"):
                    c.wait(g, "gp", c.mark(sn))
            p1.close()
            return nc
        p1.close()

        bar1 = c.barrier_marks()
        _emit_barrier(blk, c, bar1)

        p2 = ExitStack()
        wq_sb = p2.enter_context(nc.sbuf_tensor("wq_sb", [128, DK * D], BF16))
        wg1_sb = p2.enter_context(nc.sbuf_tensor("wg1_sb", [128, DK * 128], BF16))
        wg2_sb = p2.enter_context(nc.sbuf_tensor("wg2_sb", [128, D], BF16))
        Dt = p2.enter_context(nc.sbuf_tensor("Dt", [128, H * 128], F32))
        difT = p2.enter_context(nc.sbuf_tensor("difT", [128, 128], F32))
        xb2 = [p2.enter_context(nc.sbuf_tensor(f"x2_{i}", [128, D], F32)) for i in range(2)]
        kvs = [p2.enter_context(nc.sbuf_tensor(f"kvs{i}", [128, 3 * D], BF16))
               for i in range(2)]
        kTb = [kvs[i][:, 0:D] for i in range(2)]
        vbb = [kvs[i][:, D:2 * D] for i in range(2)]
        Sbb = [kvs[i][:, 2 * D:3 * D] for i in range(2)]
        h_bf2 = p2.enter_context(nc.sbuf_tensor("h_bf2", [128, D], BF16))
        hT2 = p2.enter_context(nc.sbuf_tensor("hT2", [128, D], BF16))
        q_tok = p2.enter_context(nc.sbuf_tensor("q_tok", [128, D], BF16))
        q_dec = p2.enter_context(nc.sbuf_tensor("q_dec", [128, D], BF16))
        qT = p2.enter_context(nc.sbuf_tensor("qT", [128, D], BF16))
        qTd = p2.enter_context(nc.sbuf_tensor("qTd", [128, D], BF16))
        g1T = p2.enter_context(nc.sbuf_tensor("g1T", [128, 128], BF16))
        sig = p2.enter_context(nc.sbuf_tensor("sig", [128, D], F32))
        scm = p2.enter_context(nc.sbuf_tensor("scm", [128, D], BF16))
        o_sb = p2.enter_context(nc.sbuf_tensor("o_sb", [128, D], F32))
        tmpf = p2.enter_context(nc.sbuf_tensor("tmpf", [128, D], F32))
        orms = p2.enter_context(nc.sbuf_tensor("orms", [128, D], BF16))
        oT_sb = p2.enter_context(nc.sbuf_tensor("oT_sb", [128, D], BF16))
        ms16 = p2.enter_context(nc.sbuf_tensor("ms16", [128, H], F32))
        rmss = p2.enter_context(nc.sbuf_tensor("rmss", [128, H], F32))
        rrr = p2.enter_context(nc.sbuf_tensor("rrr", [128, H], F32))
        pt2 = p2.enter_context(nc.psum_tensor("pt2", [128, D], BF16))
        pq = p2.enter_context(nc.psum_tensor("pq", [128, D], F32))
        psc = p2.enter_context(nc.psum_tensor("psc", [128, 1024], F32))

        @blk.gpsimd
        def _(g):
            i1 = g.dma_start(out=_b3(wq_sb[:, :], D), in_=_w3(wq))
            c.inc("wbig", i1, 16)
            i1 = g.dma_start(out=_b3(wg1_sb[:, :], 128), in_=_w3(wg1))
            c.inc("wbig", i1, 16)
            i1 = g.dma_start(out=wg2_sb[:, :], in_=wg2)
            c.inc("wbig", i1, 16)
            mk = c.inc("gp", g.iota(difT[:, :], pattern=[[1, 128]], base=0,
                                    channel_multiplier=-1,
                                    allow_small_or_imprecise_dtypes=True), 1)
            c.wait(g, "gp", mk)
            i1 = g.affine_select(out=difT[:, :], in_=difT[:, :],
                                 compare_op=ALU.is_ge, fill=0.0,
                                 base=0, pattern=[[1, 128]], channel_multiplier=-1)
            c.inc("gp", i1, 1)

        m_gp_p2w = c.mark("wbig")
        m_gp_p2i = c.mark("gp")

        @blk.scalar
        def _(s):
            c.wait(s, "s", m_gp_p2w)
            c.wait(s, "s", m_gp_p2i)
            for h in range(H):
                i1 = s.activation(Dt[:, h * 128:(h + 1) * 128], difT[:, :], AF.Exp,
                                  scale=ldb[:, h:h + 1])
            c.inc("s", i1, 1)

        m_s_dtexp = c.mark("s")

        @blk.gpsimd
        def _(g):
            c.wait(g, "gp", m_s_dtexp)
            for h in range(H):
                i1 = g.affine_select(out=Dt[:, h * 128:(h + 1) * 128],
                                     in_=Dt[:, h * 128:(h + 1) * 128],
                                     compare_op=ALU.is_ge, fill=0.0,
                                     base=0, pattern=[[1, 128]], channel_multiplier=-1)
            c.inc("gp", i1, 1)

        m_gp_dt = c.mark("gp")

        (m2_x, m2_ld, m2_sq, m2_var, m2_rstds, m2_hbf, m2_tph, m2_hTe,
         m2_q, m2_qe, m2_qdec, m2_tpq, m2_qTe, m2_tpqd, m2_qTde, m2_g1, m2_g1e,
         m2_gate, m2_sig, m2_sc, m2_scm, m2_o, m2_oe, m2_ms, m2_rmss, m2_orms,
         m2_tpo, m2_oTe, m2_oTst) = ({} for _ in range(29))

        rholder = {}

        def p2_ln_all(i2):
            i = i2
            par = i % 2

            @blk.sync
            def _(sy, i=i, par=par):
                if i >= 2:
                    c.wait(sy, "sy", m2_hbf[i - 2])
                    c.wait(sy, "sy", m2_sq[i - 2])
                i1 = sy.dma_start(out=xb2[par][:, :], in_=x_own[i * CH:(i + 1) * CH, :])
                m2_x[i] = c.inc("xA" if par == 0 else "xB", i1, 16)

            @blk.scalar
            def _(s, i=i, par=par):
                c.wait(s, "s", m2_x[i])
                i1 = s.activation(junk[:, :], xb2[par][:, :], AF.Square,
                                  accum_out=ssumsq[:, :])
                m2_sq[i] = c.inc("s", i1, 1)

            @blk.vector
            def _(v, i=i, par=par):
                c.wait(v, "v", m2_x[i])
                mk = c.inc("v", v.reduce_sum(smalls[:, 0:1], xb2[par][:, :], axis=AX.X,
                                             negate=True), 1)
                c.wait(v, "v", mk)
                mk = c.inc("v", v.tensor_scalar(smalls[:, 1:2], smalls[:, 0:1],
                                                1.0 / D, None, op0=ALU.mult), 1)
                c.wait(v, "v", mk)
                c.wait(v, "v", m2_sq[i])
                mk = c.inc("v", v.tensor_scalar(smalls[:, 2:3], ssumsq[:, :],
                                                1.0 / D, None, op0=ALU.mult), 1)
                c.wait(v, "v", mk)
                mk = c.inc("v", v.tensor_mul(smalls[:, 3:4], smalls[:, 1:2],
                                             smalls[:, 1:2]), 1)
                c.wait(v, "v", mk)
                i1 = v.tensor_sub(smalls[:, 4:5], smalls[:, 2:3], smalls[:, 3:4])
                m2_var[i] = c.inc("v", i1, 1)

            @blk.scalar
            def _(s, i=i):
                c.wait(s, "s", m2_var[i])
                i1 = s.activation(smalls[:, 5:6], smalls[:, 4:5], AF.Sqrt, bias=eps_sb[:, 0:1])
                m2_rstds[i] = c.inc("s", i1, 1)

            @blk.vector
            def _(v, i=i, par=par):
                c.wait(v, "v", m2_rstds[i])
                v.tensor_scalar(xb2[par][:, :], xb2[par][:, :], smalls[:, 1:2], None,
                                op0=ALU.add)
                i1 = v.reciprocal(smalls[:, 6:7], smalls[:, 5:6])
                mr = c.inc("v", i1, 1)
                c.wait(v, "v", mr)
                mk = c.inc("v", v.scalar_tensor_tensor(xb2[par][:, :], ca1_rep[:, :],
                                                       smalls[:, 6:7], xb2[par][:, :],
                                                       op0=ALU.mult, op1=ALU.mult), 1)
                c.wait(v, "v", mk)
                if i >= 1:
                    c.wait(v, "v", m2_tph[i - 1])
                i1 = v.tensor_add(h_bf2[:, :], xb2[par][:, :], sa_rep[:, :])
                m2_hbf[i] = c.inc("v", i1, 1)



        p2_ln_all(0)

        for i in range(OWN):
            par = i % 2

            @blk.gpsimd
            def _(g, i=i, par=par):
                if i == 0:
                    c.wait(g, "gp", m_sy0)
                    rholder["r"] = g.register("r_off").__enter__()
                    g.reg_load(rholder["r"], own0_sb[0:1, 0:1])
                    g.reg_alu(rholder["r"], rholder["r"], 3 * CHUNK_ELEMS, ALU.mult)
                if i >= 2:
                    c.wait(g, "gp", m2_o[(i - 2, 1)])
                r = rholder["r"]
                i1 = g.dma_start(
                    out=kvs[par][:, :].rearrange("p (j e) -> p j e", e=D),
                    in_=bass.AP(sc_kvs.tensor, r, [[D, 128], [CH * D, 3], [1, D]]))
                m2_ld[i] = c.inc("kvA" if par == 0 else "kvB", i1, 16)
                g.reg_alu(r, r, 3 * CHUNK_ELEMS, ALU.add)

            @blk.tensor
            def _(t, i=i):
                c.wait(t, "t", m2_hbf[i])
                if i >= 1:
                    c.wait(t, "t", m2_oTe[i - 1])
                for dk in range(DK):
                    i1 = t.transpose(pt2[:, dk * 128:(dk + 1) * 128],
                                     h_bf2[:, dk * 128:(dk + 1) * 128], ident[:, :])
                m2_tph[i] = c.inc("t", i1, 1)

            @blk.vector
            def _(v, i=i):
                c.wait(v, "v", m2_tph[i])
                if i >= 1:
                    c.wait(v, "v", m2_g1[i - 1])
                    c.wait(v, "v", m2_q[i - 1])
                for eb in range(4):
                    i1 = v.tensor_copy(hT2[:, eb * 512:(eb + 1) * 512],
                                       pt2[:, eb * 512:(eb + 1) * 512])
                m2_hTe[i] = c.inc("v", i1, 1)

            if i + 1 < OWN:
                p2_ln_all(i + 1)

            @blk.tensor
            def _(t, i=i):
                c.wait(t, "t", m2_hTe[i])
                c.wait(t, "t", m_gp_p2w)
                if i >= 1:
                    c.wait(t, "t", m2_oe[(i - 1, 1)])
                for dk in range(DK):
                    for eb in range(4):
                        i1 = t.matmul(pq[:, eb * 512:(eb + 1) * 512],
                                      hT2[:, dk * 128:(dk + 1) * 128],
                                      wq_sb[:, dk * D + eb * 512:dk * D + (eb + 1) * 512],
                                      start=(dk == 0), stop=(dk == DK - 1))
                m2_q[i] = c.inc("t", i1, 1)

            @blk.scalar
            def _(s, i=i):
                c.wait(s, "s", m2_q[i])
                if i >= 1:
                    c.wait(s, "s", m2_tpq[i - 1])
                    c.wait(s, "s", m2_qdec[i - 1])
                for eb in range(4):
                    i1 = s.activation(q_tok[:, eb * 512:(eb + 1) * 512],
                                      pq[:, eb * 512:(eb + 1) * 512], AF.Silu)
                m2_qe[i] = c.inc("s", i1, 1)

            @blk.vector
            def _(v, i=i):
                c.wait(v, "v", m2_qe[i])
                if i >= 1:
                    c.wait(v, "v", m2_tpqd[i - 1])
                for h in range(H):
                    i1 = v.tensor_scalar(q_dec[:, h * 128:(h + 1) * 128],
                                         q_tok[:, h * 128:(h + 1) * 128],
                                         qdec16[:, h:h + 1], None, op0=ALU.mult)
                m2_qdec[i] = c.inc("v", i1, 1)

            @blk.tensor
            def _(t, i=i):
                c.wait(t, "t", m2_qe[i])
                c.wait(t, "t", m2_hTe[i])
                for h in range(H):
                    i1 = t.transpose(pt2[:, h * 128:(h + 1) * 128],
                                     q_tok[:, h * 128:(h + 1) * 128], ident[:, :])
                m2_tpq[i] = c.inc("t", i1, 1)

            @blk.vector
            def _(v, i=i):
                c.wait(v, "v", m2_tpq[i])
                if i >= 1:
                    c.wait(v, "v", m2_sc[(i - 1, 1)])
                for eb in range(4):
                    i1 = v.tensor_copy(qT[:, eb * 512:(eb + 1) * 512],
                                       pt2[:, eb * 512:(eb + 1) * 512])
                m2_qTe[i] = c.inc("v", i1, 1)

            @blk.tensor
            def _(t, i=i):
                c.wait(t, "t", m2_qdec[i])
                c.wait(t, "t", m2_qTe[i])
                for h in range(H):
                    i1 = t.transpose(pt2[:, h * 128:(h + 1) * 128],
                                     q_dec[:, h * 128:(h + 1) * 128], ident[:, :])
                m2_tpqd[i] = c.inc("t", i1, 1)

            @blk.vector
            def _(v, i=i):
                c.wait(v, "v", m2_tpqd[i])
                if i >= 1:
                    c.wait(v, "v", m2_o[(i - 1, 1)])
                for eb in range(4):
                    i1 = v.tensor_copy(qTd[:, eb * 512:(eb + 1) * 512],
                                       pt2[:, eb * 512:(eb + 1) * 512])
                m2_qTde[i] = c.inc("v", i1, 1)

            @blk.tensor
            def _(t, i=i):
                c.wait(t, "t", m2_hTe[i])
                guard = m2_scm.get((i - 1, 1), 0)
                c.wait(t, "t", guard)
                for dk in range(DK):
                    i1 = t.matmul(psc[:, 0:128],
                                  wg1_sb[:, dk * 128:(dk + 1) * 128],
                                  hT2[:, dk * 128:(dk + 1) * 128],
                                  start=(dk == 0), stop=(dk == DK - 1))
                m2_g1[i] = c.inc("t", i1, 1)

            @blk.vector
            def _(v, i=i):
                c.wait(v, "v", m2_g1[i])
                if i >= 1:
                    c.wait(v, "v", m2_gate[i - 1])
                i1 = v.tensor_copy(g1T[:, :], psc[:, 0:128])
                m2_g1e[i] = c.inc("v", i1, 1)

            @blk.tensor
            def _(t, i=i):
                c.wait(t, "t", m2_g1e[i])
                c.wait(t, "t", m2_qe[i])
                for eb in range(4):
                    i1 = t.matmul(pq[:, eb * 512:(eb + 1) * 512], g1T[:, :],
                                  wg2_sb[:, eb * 512:(eb + 1) * 512],
                                  start=True, stop=True)
                m2_gate[i] = c.inc("t", i1, 1)

            @blk.scalar
            def _(s, i=i):
                c.wait(s, "s", m2_gate[i])
                if i >= 1:
                    c.wait(s, "s", m2_ms[i - 1])
                for eb in range(4):
                    i1 = s.activation(sig[:, eb * 512:(eb + 1) * 512],
                                      pq[:, eb * 512:(eb + 1) * 512], AF.Sigmoid)
                m2_sig[i] = c.inc("s", i1, 1)

            for hg in range(2):
                @blk.tensor
                def _(t, i=i, hg=hg, par=par):
                    c.wait(t, "t", m2_qTe[i])
                    c.wait(t, "t", m2_ld[i])
                    guard = m2_scm.get((i, hg - 1), m2_scm.get((i - 1, 1), 0))
                    c.wait(t, "t", guard)
                    if hg == 0:
                        c.wait(t, "t", m2_g1e[i])
                    for h in range(hg * 8, hg * 8 + 8):
                        i1 = t.matmul(psc[:, (h % 8) * 128:(h % 8 + 1) * 128],
                                      kTb[par][:, h * 128:(h + 1) * 128],
                                      qT[:, h * 128:(h + 1) * 128],
                                      start=True, stop=True)
                    m2_sc[(i, hg)] = c.inc("t", i1, 1)

                @blk.vector
                def _(v, i=i, hg=hg):
                    c.wait(v, "v", m2_sc[(i, hg)])
                    c.wait(v, "v", m_gp_dt)
                    if i >= 1:
                        c.wait(v, "v", m2_o[(i - 1, hg)])
                    i1 = v.tensor_mul(scm[:, hg * 1024:(hg + 1) * 1024],
                                      psc[:, :],
                                      Dt[:, hg * 1024:(hg + 1) * 1024])
                    m2_scm[(i, hg)] = c.inc("v", i1, 1)

                @blk.tensor
                def _(t, i=i, hg=hg, par=par):
                    c.wait(t, "t", m2_scm[(i, hg)])
                    c.wait(t, "t", m2_qTde[i])
                    c.wait(t, "t", m2_sig[i])
                    guard = m2_oe.get((i - 1, hg), 0)
                    c.wait(t, "t", guard)
                    for h in range(hg * 8, hg * 8 + 8):
                        sl = slice((hg % 2) * 1024 + (h % 8) * 128,
                                   (hg % 2) * 1024 + (h % 8 + 1) * 128)
                        t.matmul(pq[:, sl], scm[:, h * 128:(h + 1) * 128],
                                 vbb[par][:, h * 128:(h + 1) * 128],
                                 start=True, stop=False)
                        i1 = t.matmul(pq[:, sl], qTd[:, h * 128:(h + 1) * 128],
                                      Sbb[par][:, h * 128:(h + 1) * 128],
                                      start=False, stop=True)
                    m2_o[(i, hg)] = c.inc("t", i1, 1)

                @blk.vector
                def _(v, i=i, hg=hg):
                    c.wait(v, "v", m2_o[(i, hg)])
                    i1 = v.tensor_copy(o_sb[:, hg * 1024:(hg + 1) * 1024],
                                       pq[:, (hg % 2) * 1024:(hg % 2 + 1) * 1024])
                    m2_oe[(i, hg)] = c.inc("v", i1, 1)

            @blk.vector
            def _(v, i=i):
                c.wait(v, "v", m2_sig[i])
                c.wait(v, "v", m2_oe[(i, 1)])
                mk = c.inc("v", v.tensor_mul(o_sb[:, :], o_sb[:, :], sig[:, :]), 1)
                c.wait(v, "v", mk)
                mk = c.inc("v", v.tensor_mul(tmpf[:, :], o_sb[:, :], o_sb[:, :]), 1)
                c.wait(v, "v", mk)
                i1 = v.tensor_reduce(ms16[:, :],
                                     tmpf[:, :].rearrange("p (h q) -> p h q", q=128),
                                     axis=AX.X, op=ALU.add)
                m2_ms[i] = c.inc("v", i1, 1)

            @blk.scalar
            def _(s, i=i):
                c.wait(s, "s", m2_ms[i])
                i1 = s.activation(rmss[:, :], ms16[:, :], AF.Sqrt,
                                  scale=1.0 / HD, bias=eps_sb[:, 0:1])
                m2_rmss[i] = c.inc("s", i1, 1)

            @blk.vector
            def _(v, i=i):
                c.wait(v, "v", m2_rmss[i])
                i1 = v.reciprocal(rrr[:, :], rmss[:, :])
                mr = c.inc("v", i1, 1)
                c.wait(v, "v", mr)
                if i >= 1:
                    c.wait(v, "v", m2_tpo[i - 1])
                for h in range(H):
                    i1 = v.tensor_scalar(orms[:, h * 128:(h + 1) * 128],
                                         o_sb[:, h * 128:(h + 1) * 128],
                                         rrr[:, h:h + 1], None, op0=ALU.mult)
                m2_orms[i] = c.inc("v", i1, 1)

            @blk.tensor
            def _(t, i=i):
                c.wait(t, "t", m2_orms[i])
                c.wait(t, "t", m2_qTde[i])
                for dk in range(DK):
                    i1 = t.transpose(pt2[:, dk * 128:(dk + 1) * 128],
                                     orms[:, dk * 128:(dk + 1) * 128], ident[:, :])
                m2_tpo[i] = c.inc("t", i1, 1)

            @blk.scalar
            def _(s, i=i):
                c.wait(s, "s", m2_tpo[i])
                if i >= 1:
                    c.wait(s, "s", m2_oTst[i - 1])
                for dk in range(DK):
                    i1 = s.activation(oT_sb[:, dk * 128:(dk + 1) * 128],
                                      pt2[:, dk * 128:(dk + 1) * 128], AF.Copy,
                                      scale=normw_fc[:, dk:dk + 1])
                m2_oTe[i] = c.inc("s", i1, 1)

            @blk.sync
            def _(sy, i=i):
                c.wait(sy, "sy", m2_oTe[i])
                i1 = sy.dma_start(out=sc_oT[i * CH:(i + 1) * CH, :], in_=oT_sb[:, :])
                m2_oTst[i] = c.inc("st", i1, 16)

        if upto == 2:
            @blk.gpsimd
            def _(g):
                for sn in ("st", "v", "s", "t"):
                    c.wait(g, "gp", c.mark(sn))
            p2.close()
            return nc
        p2.close()

        bar2 = c.barrier_marks()
        _emit_barrier(blk, c, bar2)

        p3 = ExitStack()
        wo_sb = p3.enter_context(nc.sbuf_tensor("wo_sb", [128, DK * D], BF16))
        ga_rep = p3.enter_context(nc.sbuf_tensor("ga_rep", [128, D], F32))
        oTb = [p3.enter_context(nc.sbuf_tensor(f"oTb{i}", [128, D], BF16)) for i in range(2)]
        xb3 = [p3.enter_context(nc.sbuf_tensor(f"x3_{i}", [128, D], F32)) for i in range(2)]
        xm_sb = p3.enter_context(nc.sbuf_tensor("xm_sb", [128, D], F32))
        tmp3 = p3.enter_context(nc.sbuf_tensor("tmp3", [128, D], F32))
        po = p3.enter_context(nc.psum_tensor("po", [128, D], F32))

        @blk.gpsimd
        def _(g):
            i1 = g.dma_start(out=_b3(wo_sb[:, :], D), in_=_w3(wo))
            c.inc("wbig", i1, 16)

        m3_wo = c.mark("wbig")

        @blk.sync
        def _(sy):
            i1 = sy.dma_start(out=ga_rep[:, :], in_=sc_mod[2 * CH:3 * CH, :])
            c.inc("ld1", i1, 16)

        m3_ga = c.mark("ld1")
        m3_oT, m3_x, m3_mm, m3_xm, m3_st = {}, {}, {}, {}, {}

        for i in range(OWN):
            par = i % 2

            @blk.gpsimd
            def _(g, i=i, par=par):
                if i >= 2:
                    c.wait(g, "gp", m3_mm[i - 2])
                i1 = g.dma_start(out=oTb[par][:, :], in_=sc_oT[i * CH:(i + 1) * CH, :])
                m3_oT[i] = c.inc("oTA" if par == 0 else "oTB", i1, 16)

            @blk.sync
            def _(sy, i=i, par=par):
                if i >= 2:
                    c.wait(sy, "sy", m3_xm[i - 2])
                i1 = sy.dma_start(out=xb3[par][:, :], in_=x_own[i * CH:(i + 1) * CH, :])
                m3_x[i] = c.inc("xA" if par == 0 else "xB", i1, 16)

            @blk.tensor
            def _(t, i=i, par=par):
                c.wait(t, "t", m3_oT[i])
                c.wait(t, "t", m3_wo)
                if i >= 1:
                    c.wait(t, "t", m3_xm[i - 1])
                for dk in range(DK):
                    for eb in range(4):
                        i1 = t.matmul(po[:, eb * 512:(eb + 1) * 512],
                                      oTb[par][:, dk * 128:(dk + 1) * 128],
                                      wo_sb[:, dk * D + eb * 512:dk * D + (eb + 1) * 512],
                                      start=(dk == 0), stop=(dk == DK - 1))
                m3_mm[i] = c.inc("t", i1, 1)

            if dbg and upto == 3 and i == 0:
                @blk.vector
                def _(v):
                    c.wait(v, "v", m3_mm[0])
                    i1 = v.tensor_copy(tmp3[:, :], po[:, :])
                    c.inc("v", i1, 1)

                @blk.sync
                def _(sy):
                    c.wait(sy, "sy", c.mark("v"))
                    i1 = sy.dma_start(out=dbg_att, in_=tmp3[:, :])
                    c.inc("st", i1, 16)
                    i1 = sy.dma_start(out=dbg_oTb, in_=oTb[0][:, :])
                    c.inc("st", i1, 16)

            @blk.vector
            def _(v, i=i, par=par):
                c.wait(v, "v", m3_mm[i])
                if dbg and upto == 3 and i == 0:
                    c.wait(v, "v", c.mark("st"))
                c.wait(v, "v", m3_x[i])
                c.wait(v, "v", m3_ga)
                if i >= 1:
                    c.wait(v, "v", m3_st[i - 1])
                mk = c.inc("v", v.tensor_mul(tmp3[:, :], po[:, :], ga_rep[:, :]), 1)
                c.wait(v, "v", mk)
                i1 = v.tensor_add(xm_sb[:, :], tmp3[:, :], xb3[par][:, :])
                m3_xm[i] = c.inc("v", i1, 1)

            @blk.sync
            def _(sy, i=i):
                c.wait(sy, "sy", m3_xm[i])
                i1 = sy.dma_start(out=sc_xmid[i * CH:(i + 1) * CH, :], in_=xm_sb[:, :])
                m3_st[i] = c.inc("st", i1, 16)

        if upto == 3:
            @blk.gpsimd
            def _(g):
                for sn in ("st", "v", "s", "t"):
                    c.wait(g, "gp", c.mark(sn))
            p3.close()
            return nc
        p3.close()

        bar3 = c.barrier_marks()
        _emit_barrier(blk, c, bar3)

        p4 = ExitStack()
        h2T = p4.enter_context(nc.sbuf_tensor("h2T", [128, DK * TILE], BF16))
        zT = p4.enter_context(nc.sbuf_tensor("zT", [128, ICN * TILE], BF16))
        acc = p4.enter_context(nc.sbuf_tensor("acc", [128, TSUB * D], F32))
        wup = [p4.enter_context(nc.sbuf_tensor(f"wup{i}", [128, DK * 128], BF16))
               for i in range(2)]
        wdn = p4.enter_context(nc.sbuf_tensor("wdn", [128, 4 * D], BF16))
        modb = [p4.enter_context(nc.sbuf_tensor(f"modb{i}", [128, D], F32))
                for i in range(2)]
        gwT_sb = p4.enter_context(nc.sbuf_tensor("gwT_sb", [128, DK * 8], BF16))
        xm4 = p4.enter_context(nc.sbuf_tensor("xm4", [128, D], F32))
        h2bf = p4.enter_context(nc.sbuf_tensor("h2bf", [128, D], BF16))
        silt = p4.enter_context(nc.sbuf_tensor("silt", [128, TILE], F32))
        lgT = p4.enter_context(nc.sbuf_tensor("lgT", [8, TILE], BF16))
        lgt = p4.enter_context(nc.sbuf_tensor("lgt", [128, TSUB * 8], F32))
        rt0 = p4.enter_context(nc.sbuf_tensor("rt0", [128, TSUB * 8], F32))
        rt1 = p4.enter_context(nc.sbuf_tensor("rt1", [128, TSUB * 8], F32))
        rt2 = p4.enter_context(nc.sbuf_tensor("rt2", [128, TSUB * 8], F32))
        rts = p4.enter_context(nc.sbuf_tensor("rts", [128, TSUB], F32))
        comb = p4.enter_context(nc.sbuf_tensor("comb", [128, TSUB * 8], F32))

        @blk.gpsimd
        def _(g):
            i1 = g.dma_start(out=_b3(gwT_sb[:, :], 8), in_=_w3(gwT))
            c.inc("wbig", i1, 16)

        m4_gw = c.mark("wbig")

        def up_src(e10, mat, ic):
            if e10 < 8:
                base = (egT if mat == 0 else euT)[e10]
            else:
                base = (sgT if mat == 0 else suT)[:, (e10 - 8) * ISH:(e10 - 7) * ISH]
            return _w3(base[:, ic * 128:(ic + 1) * 128])

        def dn_src(e10, sw_i):
            r0 = sum(DSW[:sw_i]) * 128
            r1 = r0 + DSW[sw_i] * 128
            if e10 < 8:
                base = edT[e10]
            else:
                base = sdT[(e10 - 8) * ISH:(e10 - 7) * ISH, :]
            return _w3(base[r0:r1, :])

        for tt in range(NTILE):
            bar_t = c.barrier_marks()
            _emit_barrier(blk, c, bar_t)

            t4 = ExitStack()
            pscope = ExitStack()
            pup = t4.enter_context(nc.psum_tensor("pup_%d" % tt, [128, D], F32))
            pt4 = pscope.enter_context(nc.psum_tensor("pt4_%d" % tt, [128, D], BF16))

            m4_xm, m4_sq, m4_var, m4_rstds, m4_hbf = {}, {}, {}, {}, {}
            m4_tph, m4_he = {}, {}

            @blk.sync
            def _(sy, tt=tt):
                i1 = sy.dma_start(out=modb[0][:, :], in_=sc_mod[4 * CH:5 * CH, :])
                c.inc("ld1", i1, 16)
                i1 = sy.dma_start(out=modb[1][:, :], in_=sc_mod[3 * CH:4 * CH, :])
                c.inc("ld1", i1, 16)

            m4_mods = c.mark("ld1")

            for cc in range(TSUB):
                ich = tt * TSUB + cc

                @blk.sync
                def _(sy, ich=ich, cc=cc):
                    c.wait(sy, "sy", m4_hbf.get(cc - 1, 0))
                    c.wait(sy, "sy", m4_sq.get(cc - 1, 0))
                    i1 = sy.dma_start(out=xm4[:, :],
                                      in_=sc_xmid[ich * CH:(ich + 1) * CH, :])
                    m4_xm[cc] = c.inc("xA", i1, 16)

                @blk.scalar
                def _(s, cc=cc):
                    c.wait(s, "s", m4_xm[cc])
                    i1 = s.activation(junk[:, :], xm4[:, :], AF.Square,
                                      accum_out=ssumsq[:, :])
                    m4_sq[cc] = c.inc("s", i1, 1)

                @blk.vector
                def _(v, cc=cc):
                    c.wait(v, "v", m4_xm[cc])
                    mk = c.inc("v", v.reduce_sum(smalls[:, 0:1], xm4[:, :], axis=AX.X,
                                                 negate=True), 1)
                    c.wait(v, "v", mk)
                    mk = c.inc("v", v.tensor_scalar(smalls[:, 1:2], smalls[:, 0:1],
                                                    1.0 / D, None, op0=ALU.mult), 1)
                    c.wait(v, "v", mk)
                    c.wait(v, "v", m4_sq[cc])
                    mk = c.inc("v", v.tensor_scalar(smalls[:, 2:3], ssumsq[:, :],
                                                    1.0 / D, None, op0=ALU.mult), 1)
                    c.wait(v, "v", mk)
                    mk = c.inc("v", v.tensor_mul(smalls[:, 3:4], smalls[:, 1:2],
                                                 smalls[:, 1:2]), 1)
                    c.wait(v, "v", mk)
                    i1 = v.tensor_sub(smalls[:, 4:5], smalls[:, 2:3], smalls[:, 3:4])
                    m4_var[cc] = c.inc("v", i1, 1)

                @blk.scalar
                def _(s, cc=cc):
                    c.wait(s, "s", m4_var[cc])
                    i1 = s.activation(smalls[:, 5:6], smalls[:, 4:5], AF.Sqrt, bias=eps_sb[:, 0:1])
                    m4_rstds[cc] = c.inc("s", i1, 1)

                @blk.vector
                def _(v, cc=cc):
                    c.wait(v, "v", m4_rstds[cc])
                    v.tensor_scalar(xm4[:, :], xm4[:, :], smalls[:, 1:2], None,
                                    op0=ALU.add)
                    i1 = v.reciprocal(smalls[:, 6:7], smalls[:, 5:6])
                    mr = c.inc("v", i1, 1)
                    c.wait(v, "v", mr)
                    c.wait(v, "v", m4_mods)
                    mk = c.inc("v", v.scalar_tensor_tensor(xm4[:, :], modb[0][:, :],
                                                           smalls[:, 6:7], xm4[:, :],
                                                           op0=ALU.mult, op1=ALU.mult), 1)
                    c.wait(v, "v", mk)
                    c.wait(v, "v", m4_tph.get(cc - 1, 0))
                    i1 = v.tensor_add(h2bf[:, :], xm4[:, :], modb[1][:, :])
                    m4_hbf[cc] = c.inc("v", i1, 1)

                @blk.tensor
                def _(t, cc=cc):
                    c.wait(t, "t", m4_hbf[cc])
                    c.wait(t, "t", m4_he.get(cc - 1, 0))
                    for dk in range(DK):
                        i1 = t.transpose(pt4[:, dk * 128:(dk + 1) * 128],
                                         h2bf[:, dk * 128:(dk + 1) * 128], ident[:, :])
                    m4_tph[cc] = c.inc("t", i1, 1)

                @blk.scalar
                def _(s, cc=cc):
                    c.wait(s, "s", m4_tph[cc])
                    for dk in range(DK):
                        i1 = s.copy(h2T[:, dk * TILE + cc * 128:dk * TILE + (cc + 1) * 128],
                                    pt4[:, dk * 128:(dk + 1) * 128])
                    m4_he[cc] = c.inc("s", i1, 1)

            m4_hall = c.mark("s")

            @blk.tensor
            def _(t):
                c.wait(t, "t", m4_hall)
                c.wait(t, "t", m4_gw)
                for th in range(2):
                    for dk in range(DK):
                        i1 = t.matmul(pup[0:8, th * 512:(th + 1) * 512],
                                      gwT_sb[:, dk * 8:(dk + 1) * 8],
                                      h2T[:, dk * TILE + th * 512:dk * TILE + (th + 1) * 512],
                                      start=(dk == 0), stop=(dk == DK - 1))
                c.inc("t", i1, 1)

            m4_rmm = c.mark("t")

            @blk.scalar
            def _(s):
                c.wait(s, "s", m4_rmm)
                i1 = s.copy(lgT[:, :], pup[0:8, 0:1024])
                c.inc("s", i1, 1)

            m4_lge = c.mark("s")

            @blk.tensor
            def _(t):
                c.wait(t, "t", m4_lge)
                for sub in range(TSUB):
                    i1 = t.transpose(pt4[:, sub * 8:(sub + 1) * 8],
                                     lgT[:, sub * 128:(sub + 1) * 128],
                                     ident[0:8, 0:8])
                c.inc("t", i1, 1)

            m4_ltp = c.mark("t")

            @blk.vector
            def _(v):
                c.wait(v, "v", m4_ltp)
                r3 = lambda tns: tns[:, :].rearrange("p (s e) -> p s e", e=8)

                def bc(tns):
                    return tns[:, :].unsqueeze(2).broadcast_to([128, TSUB, 8])

                def sw(i1):
                    mk = c.inc("v", i1, 1)
                    c.wait(v, "v", mk)

                sw(v.tensor_copy(lgt[:, :], pt4[:, 0:TSUB * 8]))
                sw(v.tensor_reduce(rts[:, :], r3(lgt), axis=AX.X, op=ALU.max))
                sw(v.tensor_tensor(r3(rt0), r3(lgt), bc(rts), op=ALU.subtract))

            @blk.scalar
            def _(s):
                c.wait(s, "s", c.mark("v"))
                i1 = s.activation(rt1[:, :], rt0[:, :], AF.Exp)
                c.inc("s", i1, 1)

            m4_exp = c.mark("s")

            @blk.vector
            def _(v):
                c.wait(v, "v", m4_exp)
                r3 = lambda tns: tns[:, :].rearrange("p (s e) -> p s e", e=8)

                def bc(tns):
                    return tns[:, :].unsqueeze(2).broadcast_to([128, TSUB, 8])

                def sw(i1):
                    mk = c.inc("v", i1, 1)
                    c.wait(v, "v", mk)

                sw(v.tensor_reduce(rts[:, :], r3(rt1), axis=AX.X, op=ALU.add))
                sw(v.reciprocal(rts[:, :], rts[:, :]))
                sw(v.tensor_tensor(r3(rt1), r3(rt1), bc(rts), op=ALU.mult))
                sw(v.tensor_reduce(rts[:, :], r3(rt1), axis=AX.X, op=ALU.max))
                sw(v.tensor_tensor(r3(rt2), r3(rt1), bc(rts), op=ALU.subtract))
                sw(v.tensor_scalar(rt2[:, :], rt2[:, :], 0.0, None, op0=ALU.is_ge))
                sw(v.tensor_mul(rt0[:, :], rt1[:, :], rt2[:, :]))
                sw(v.tensor_sub(rt0[:, :], rt1[:, :], rt0[:, :]))
                sw(v.tensor_reduce(rts[:, :], r3(rt0), axis=AX.X, op=ALU.max))
                sw(v.tensor_tensor(r3(rt0), r3(rt0), bc(rts), op=ALU.subtract))
                sw(v.tensor_scalar(rt0[:, :], rt0[:, :], 0.0, None, op0=ALU.is_ge))
                sw(v.tensor_add(rt2[:, :], rt2[:, :], rt0[:, :]))
                sw(v.tensor_mul(rt2[:, :], rt1[:, :], rt2[:, :]))
                sw(v.tensor_reduce(rts[:, :], r3(rt2), axis=AX.X, op=ALU.add))
                sw(v.tensor_scalar(rts[:, :], rts[:, :], 1e-20, None, op0=ALU.add))
                sw(v.reciprocal(rts[:, :], rts[:, :]))
                i1 = v.tensor_tensor(r3(comb), r3(rt2), bc(rts), op=ALU.mult)
                c.inc("v", i1, 1)

            m4_comb = c.mark("v")
            pscope.close()
            pdn = t4.enter_context(nc.psum_tensor("pdn_%d" % tt, [128, D], F32))

            m4_wup, m4_wdn, m4_y, m4_z, m4_dn, m4_ae = {}, {}, {}, {}, {}, {}
            up_items = [(e10, ic, mat) for e10 in range(NE10)
                        for ic in range(ICN) for mat in range(2)]
            up_ptr = [0]

            def emit_up_loads(upto_item, m4_wup=m4_wup, m4_y=m4_y,
                              up_items=up_items, up_ptr=up_ptr):
                while up_ptr[0] <= upto_item and up_ptr[0] < len(up_items):
                    e10, ic, mat = up_items[up_ptr[0]]
                    kidx = up_ptr[0]

                    @blk.gpsimd
                    def _(g, e10=e10, ic=ic, mat=mat, kidx=kidx):
                        if kidx >= 2:
                            pe, pc, pm = up_items[kidx - 2]
                            c.wait(g, "gp", m4_y.get((pe, pc, pm), 0))
                        i1 = g.dma_start(out=_b3(wup[mat][:, :], 128), in_=up_src(e10, mat, ic))
                        m4_wup[(e10, ic, mat)] = c.inc("upA" if mat == 0 else "upB", i1, 16)
                    up_ptr[0] += 1

            for e10 in range(NE10):
                @blk.gpsimd
                def _(g, e10=e10):
                    c.wait(g, "gp", m4_dn.get((e10 - 1, 2, TSUB - 1), 0))
                    i1 = g.dma_start(out=_b3(wdn[:, 0:DSW[0] * D], D),
                                     in_=dn_src(e10, 0))
                    m4_wdn[(e10, 0)] = c.inc("dnw", i1, 16)

                for ic in range(ICN):
                    emit_up_loads(e10 * ICN * 2 + ic * 2 + 1)
                    for mat in range(2):
                        @blk.tensor
                        def _(t, e10=e10, ic=ic, mat=mat):
                            c.wait(t, "t", m4_wup[(e10, ic, mat)])
                            c.wait(t, "t", m4_lge)
                            guard = m4_z.get((e10, ic - 1),
                                             m4_z.get((e10 - 1, ICN - 1), 0))
                            c.wait(t, "t", guard)
                            c.wait(t, "t", m4_ae.get((e10 - 1, 2, TSUB - 1), 0))
                            for dk in range(DK):
                                for th in range(2):
                                    i1 = t.matmul(
                                        pup[:, mat * 1024 + th * 512:
                                            mat * 1024 + (th + 1) * 512],
                                        wup[mat][:, dk * 128:(dk + 1) * 128],
                                        h2T[:, dk * TILE + th * 512:
                                            dk * TILE + (th + 1) * 512],
                                        start=(dk == 0), stop=(dk == DK - 1))
                            m4_y[(e10, ic, mat)] = c.inc("t", i1, 1)

                    @blk.scalar
                    def _(s, e10=e10, ic=ic):
                        c.wait(s, "s", m4_y[(e10, ic, 0)])
                        c.wait(s, "s", m4_z.get((e10, ic - 1),
                                                     m4_z.get((e10 - 1, ICN - 1), 0)))
                        i1 = s.activation(silt[:, :], pup[:, 0:1024], AF.Silu)
                        m4_z[(e10, ic, "s")] = c.inc("s", i1, 1)

                    @blk.vector
                    def _(v, e10=e10, ic=ic):
                        c.wait(v, "v", m4_z[(e10, ic, "s")])
                        c.wait(v, "v", m4_y[(e10, ic, 1)])
                        c.wait(v, "v", m4_dn.get((e10 - 1, 2, TSUB - 1), 0))
                        i1 = v.tensor_mul(zT[:, ic * TILE:(ic + 1) * TILE],
                                          silt[:, :], pup[:, 1024:2048])
                        m4_z[(e10, ic)] = c.inc("v", i1, 1)

                for sw_i in range(3):
                    nic = DSW[sw_i]
                    ic0_base = sum(DSW[:sw_i])
                    if sw_i >= 1:
                        @blk.gpsimd
                        def _(g, e10=e10, sw_i=sw_i, nic=nic):
                            c.wait(g, "gp", m4_dn[(e10, sw_i - 1, TSUB - 1)])
                            i1 = g.dma_start(out=_b3(wdn[:, 0:nic * D], D),
                                             in_=dn_src(e10, sw_i))
                            m4_wdn[(e10, sw_i)] = c.inc("dnw", i1, 16)

                    for sub in range(TSUB):
                        pbank = pdn if sub % 2 == 0 else pup

                        @blk.tensor
                        def _(t, e10=e10, sw_i=sw_i, sub=sub, nic=nic,
                              ic0_base=ic0_base, pbank=pbank):
                            c.wait(t, "t", m4_wdn[(e10, sw_i)])
                            for ic0 in range(nic):
                                c.wait(t, "t", m4_z[(e10, ic0_base + ic0)])
                            prev = m4_ae.get((e10, sw_i, sub - 2),
                                             m4_ae.get((e10, sw_i - 1, TSUB - 2 + sub),
                                                       m4_ae.get((e10 - 1, 2, TSUB - 2 + sub), 0)))
                            c.wait(t, "t", prev)
                            for ic0 in range(nic):
                                ic = ic0_base + ic0
                                for eb in range(4):
                                    i1 = t.matmul(
                                        pbank[:, eb * 512:(eb + 1) * 512],
                                        zT[:, ic * TILE + sub * 128:
                                           ic * TILE + (sub + 1) * 128],
                                        wdn[:, ic0 * D + eb * 512:ic0 * D + (eb + 1) * 512],
                                        start=(ic0 == 0), stop=(ic0 == nic - 1))
                            m4_dn[(e10, sw_i, sub)] = c.inc("t", i1, 1)

                        @blk.vector
                        def _(v, e10=e10, sw_i=sw_i, sub=sub, pbank=pbank):
                            c.wait(v, "v", m4_dn[(e10, sw_i, sub)])
                            c.wait(v, "v", m4_comb)
                            asl = acc[:, sub * D:(sub + 1) * D]
                            if e10 == 0 and sw_i == 0:
                                i1 = v.tensor_scalar(
                                    asl, pbank[:, :],
                                    comb[:, sub * 8 + e10:sub * 8 + e10 + 1],
                                    None, op0=ALU.mult)
                            else:
                                sc = (comb[:, sub * 8 + e10:sub * 8 + e10 + 1]
                                      if e10 < 8 else ones_f[:, 0:1])
                                i1 = v.scalar_tensor_tensor(asl, pbank[:, :], sc, asl,
                                                            op0=ALU.mult, op1=ALU.add)
                            m4_ae[(e10, sw_i, sub)] = c.inc("v", i1, 1)

            @blk.sync
            def _(sy, tt=tt):
                for sub in range(TSUB):
                    c.wait(sy, "sy", m4_ae[(NE10 - 1, 2, sub)])
                i1 = sy.dma_start(out=modb[1][:, :], in_=sc_mod[5 * CH:6 * CH, :])
                c.inc("ld1", i1, 16)

            m4_gb = c.mark("ld1")
            m4_ep, m4_yst = {}, {}

            for cc in range(TSUB):
                ich = tt * TSUB + cc

                @blk.sync
                def _(sy, ich=ich, cc=cc):
                    c.wait(sy, "sy", m4_ep.get(cc - 1, 0))
                    c.wait(sy, "sy", m4_yst.get(cc - 1, 0))
                    i1 = sy.dma_start(out=xm4[:, :],
                                      in_=sc_xmid[ich * CH:(ich + 1) * CH, :])
                    m4_xm[(cc, "ep")] = c.inc("xA", i1, 16)

                @blk.vector
                def _(v, ich=ich, cc=cc):
                    c.wait(v, "v", m4_xm[(cc, "ep")])
                    c.wait(v, "v", m4_gb)
                    for hf in range(2):
                        mk = c.inc("v", v.tensor_mul(
                            silt[:, :],
                            acc[:, cc * D + hf * TILE:cc * D + (hf + 1) * TILE],
                            modb[1][:, hf * TILE:(hf + 1) * TILE]), 1)
                        c.wait(v, "v", mk)
                        mk = c.inc("v", v.tensor_add(
                            xm4[:, hf * TILE:(hf + 1) * TILE], silt[:, :],
                            xm4[:, hf * TILE:(hf + 1) * TILE]), 1)
                        c.wait(v, "v", mk)
                    m4_ep[cc] = mk

                @blk.sync
                def _(sy, ich=ich, cc=cc):
                    c.wait(sy, "sy", m4_ep[cc])
                    i1 = sy.dma_start(out=y_out[ich * CH:(ich + 1) * CH, :],
                                      in_=xm4[:, :])
                    m4_yst[cc] = c.inc("st", i1, 16)

            t4.close()

        @blk.gpsimd
        def _(g):
            for sn in ("st", "v", "s", "t", "ld1"):
                c.wait(g, "gp", c.mark(sn))

        p4.close()

    return nc


def _prep_inputs(inputs):
    a = {k: np.asarray(v) for k, v in inputs.items()}
    w = {}
    w["x"] = np.ascontiguousarray(a["L0_attn_x"], dtype=np.float32)
    w["c"] = np.ascontiguousarray(a["L0_c"], dtype=np.float32)
    w["ld"] = np.ascontiguousarray(a["L0_attn_log_decay"], dtype=np.float32)
    w["wmod"] = np.ascontiguousarray(a["L0_wmod"], dtype=np.float32)
    w["wq"] = np.ascontiguousarray(a["L0_attn_Wq"], dtype=np.float32)
    w["wk"] = np.ascontiguousarray(a["L0_attn_Wk"], dtype=np.float32)
    w["wv"] = np.ascontiguousarray(a["L0_attn_Wv"], dtype=np.float32)
    w["wo"] = np.ascontiguousarray(a["L0_attn_Wo"], dtype=np.float32)
    w["wg1"] = np.ascontiguousarray(a["L0_attn_Wg1"], dtype=np.float32)
    w["wg2"] = np.ascontiguousarray(a["L0_attn_Wg2"], dtype=np.float32)
    w["normw"] = np.ascontiguousarray(a["L0_attn_norm_w"], dtype=np.float32)
    w["gwT"] = np.ascontiguousarray(a["L0_ffn_gate_w"].T, dtype=np.float32)
    w["sgT"] = np.ascontiguousarray(a["L0_ffn_shared_gate"].T, dtype=np.float32)
    w["suT"] = np.ascontiguousarray(a["L0_ffn_shared_up"].T, dtype=np.float32)
    w["sdT"] = np.ascontiguousarray(a["L0_ffn_shared_down"].T, dtype=np.float32)
    w["egT"] = np.ascontiguousarray(a["L0_ffn_expert_gate"].transpose(0, 2, 1),
                                    dtype=np.float32)
    w["euT"] = np.ascontiguousarray(a["L0_ffn_expert_up"].transpose(0, 2, 1),
                                    dtype=np.float32)
    w["edT"] = np.ascontiguousarray(a["L0_ffn_expert_down"].transpose(0, 2, 1),
                                    dtype=np.float32)
    return w


def _core_in_map(w, core, upto=4):
    b, j = core // 2, core % 2
    im = {
        "x_scan": w["x"][b],
        "c_row": w["c"][b],
        "ld": w["ld"],
        "wmod": w["wmod"],
        "wk": w["wk"],
        "wv": w["wv"],
        "own0": np.array([[OWN * j]], dtype=np.int32),
    }
    if upto >= 2:
        im["x_own"] = np.ascontiguousarray(w["x"][b, j * 2048:(j + 1) * 2048])
        im["wq"] = w["wq"]
        im["wg1"] = w["wg1"]
        im["wg2"] = w["wg2"]
        im["normw"] = w["normw"]
    if upto >= 3:
        im["wo"] = w["wo"]
    if upto >= 4:
        im["gwT"] = w["gwT"]
        im["sgT"] = w["sgT"]
        im["suT"] = w["suT"]
        im["sdT"] = w["sdT"]
        im["egT"] = w["egT"]
        im["euT"] = w["euT"]
        im["edT"] = w["edT"]
    return im


def kernel(**inputs):
    w = _prep_inputs(inputs)
    nc = build_program(upto=4)
    in_maps = [_core_in_map(w, core) for core in range(8)]
    res = run_bass_kernel_spmd(nc, in_maps, core_ids=list(range(8)))
    out = np.empty((4, 4096, 2048), dtype=np.float32)
    for core in range(8):
        b, j = core // 2, core % 2
        out[b, j * 2048:(j + 1) * 2048, :] = res.results[core]["y"]
    return out


# version 05
# speedup vs baseline: 1.1973x; 1.0061x over previous
import numpy as np
import concourse.bass as bass
import concourse.mybir as mybir
from concourse.bass_utils import run_bass_kernel_spmd
from contextlib import ExitStack

F32 = mybir.dt.float32
BF16 = mybir.dt.bfloat16
I32 = mybir.dt.int32
AF = mybir.ActivationFunctionType
ALU = mybir.AluOpType
AX = mybir.AxisListType

D = 2048
DK = 16
H = 16
HD = 128
CH = 128
NCH = 32
OWN = 16
EPS = 1e-5
ISH = 1408
ICN = 11
NE10 = 10
TSUB = 8
TILE = 1024
NTILE = 2
CHUNK_ELEMS = CH * D
DSW = (4, 4, 3)

ENGS = ("sy", "gp", "t", "v", "s",
        "xA", "xB", "st", "ld1", "wA", "wB", "wbig",
        "kvst", "Sst", "kvA", "kvB", "oTA", "oTB", "upA", "upB", "dnw")


def _w3(dram2d, p=128):
    return dram2d.rearrange("(k p) e -> p k e", p=p)


def _b3(buf2d, e):
    return buf2d.rearrange("p (k e) -> p k e", e=e)


class Ctx:
    def __init__(self, nc, stack):
        self.nc = nc
        self.h = {}
        self.cnt = {}
        self.lastw = {}
        for n in ENGS:
            self.h[n] = stack.enter_context(nc.semaphore("sem_" + n))
            self.cnt[n] = 0

    def inc(self, name, inst, k=1):
        inst.then_inc(self.h[name], k)
        self.cnt[name] += k
        return (name, self.cnt[name])

    def mark(self, name):
        return (name, self.cnt[name])

    def wait(self, eng, engname, tok):
        if not tok:
            return
        semname, val = tok
        if val <= 0:
            return
        key = (engname, semname)
        if self.lastw.get(key, -1) >= val:
            return
        eng.wait_ge(self.h[semname], val)
        self.lastw[key] = val

    def barrier_marks(self):
        return dict(self.cnt)


def _emit_barrier(blk, c, marks):
    for en, sect in (("sy", blk.sync), ("gp", blk.gpsimd), ("t", blk.tensor),
                     ("v", blk.vector), ("s", blk.scalar)):
        def f(eng, en=en):
            for sn, v in marks.items():
                if sn != en:
                    c.wait(eng, en, (sn, v))
        sect(f)


def build_program(upto=4, dbg=False, nch=NCH, nmb=48):
    nc = bass.Bass("TRN2", target_bir_lowering=False)

    x_scan = nc.dram_tensor("x_scan", [4096, D], F32, kind="ExternalInput").ap()
    c_row = nc.dram_tensor("c_row", [D], F32, kind="ExternalInput").ap()
    ld_in = nc.dram_tensor("ld", [H], F32, kind="ExternalInput").ap()
    wmod = nc.dram_tensor("wmod", [D, 6 * D], F32, kind="ExternalInput").ap()
    wk = nc.dram_tensor("wk", [D, D], F32, kind="ExternalInput").ap()
    wv = nc.dram_tensor("wv", [D, D], F32, kind="ExternalInput").ap()
    own0 = nc.dram_tensor("own0", [1, 1], I32, kind="ExternalInput").ap()
    if upto >= 2:
        x_own = nc.dram_tensor("x_own", [OWN * CH, D], F32, kind="ExternalInput").ap()
        wq = nc.dram_tensor("wq", [D, D], F32, kind="ExternalInput").ap()
        wg1 = nc.dram_tensor("wg1", [D, HD], F32, kind="ExternalInput").ap()
        wg2 = nc.dram_tensor("wg2", [HD, D], F32, kind="ExternalInput").ap()
        normw = nc.dram_tensor("normw", [D], F32, kind="ExternalInput").ap()
    if upto >= 3:
        wo = nc.dram_tensor("wo", [D, D], F32, kind="ExternalInput").ap()
    if upto >= 4:
        gwT = nc.dram_tensor("gwT", [D, 8], F32, kind="ExternalInput").ap()
        sgT = nc.dram_tensor("sgT", [D, 2 * ISH], F32, kind="ExternalInput").ap()
        suT = nc.dram_tensor("suT", [D, 2 * ISH], F32, kind="ExternalInput").ap()
        sdT = nc.dram_tensor("sdT", [2 * ISH, D], F32, kind="ExternalInput").ap()
        egT = nc.dram_tensor("egT", [8, D, ISH], F32, kind="ExternalInput").ap()
        euT = nc.dram_tensor("euT", [8, D, ISH], F32, kind="ExternalInput").ap()
        edT = nc.dram_tensor("edT", [8, ISH, D], F32, kind="ExternalInput").ap()
        y_out = nc.dram_tensor("y", [OWN * CH, D], F32, kind="ExternalOutput").ap()

    def scratch(name, shape, dt, out_at):
        kind = "ExternalOutput" if (dbg and upto == out_at) else "Internal"
        return nc.dram_tensor(name, shape, dt, kind=kind).ap()

    sc_kvs = scratch("sc_kvs", [NCH * 3 * CH, D], BF16, 1)
    sc_mod = scratch("sc_mod", [6 * CH, D], F32, 1)
    dbg_h = (nc.dram_tensor("dbg_h", [CH, D], BF16, kind="ExternalOutput").ap()
             if (dbg and upto == 1) else None)
    dbg_sm = (nc.dram_tensor("dbg_sm", [CH, 8], F32, kind="ExternalOutput").ap()
              if (dbg and upto == 1) else None)
    dbg_x = (nc.dram_tensor("dbg_x", [CH, D], F32, kind="ExternalOutput").ap()
             if (dbg and upto == 1) else None)
    dbg_ss = (nc.dram_tensor("dbg_ss", [CH, 1], F32, kind="ExternalOutput").ap()
              if (dbg and upto == 1) else None)
    sc_oT = scratch("sc_oT", [OWN * CH, D], BF16, 2) if upto >= 2 else None
    sc_xmid = scratch("sc_xmid", [OWN * CH, D], F32, 3) if upto >= 3 else None
    dbg_oTb = (nc.dram_tensor("dbg_oTb", [CH, D], BF16, kind="ExternalOutput").ap()
               if (dbg and upto == 3) else None)
    dbg_att = (nc.dram_tensor("dbg_att", [CH, D], F32, kind="ExternalOutput").ap()
               if (dbg and upto == 3) else None)

    stack = ExitStack()
    with stack:
        c = Ctx(nc, stack)

        ident = stack.enter_context(nc.sbuf_tensor("ident", [128, 128], BF16))
        ones_f = stack.enter_context(nc.sbuf_tensor("ones_f", [128, 128], F32))
        ones_1p = stack.enter_context(nc.sbuf_tensor("ones_1p", [1, 128], F32))
        ldb = stack.enter_context(nc.sbuf_tensor("ldb", [128, H], F32))
        qdec16 = stack.enter_context(nc.sbuf_tensor("qdec16", [128, H], F32))
        kdec16 = stack.enter_context(nc.sbuf_tensor("kdec16", [128, H], F32))
        cdec16 = stack.enter_context(nc.sbuf_tensor("cdec16", [128, H], F32))
        normw_fc = (stack.enter_context(nc.sbuf_tensor("normw_fc", [128, DK], F32))
                    if upto >= 2 else None)
        it_p1 = stack.enter_context(nc.sbuf_tensor("it_p1", [128, 1], F32))
        it_km = stack.enter_context(nc.sbuf_tensor("it_km", [128, 1], F32))
        c128 = stack.enter_context(nc.sbuf_tensor("c128", [128, 1], F32))
        eps_sb = stack.enter_context(nc.sbuf_tensor("eps_sb", [128, 1], F32))
        ld_sb = stack.enter_context(nc.sbuf_tensor("ld_sb", [1, H], F32))
        own0_sb = stack.enter_context(nc.sbuf_tensor("own0_sb", [1, 1], I32))
        smalls = stack.enter_context(nc.sbuf_tensor("smalls", [128, 8], F32))
        ssumsq = stack.enter_context(nc.sbuf_tensor("ssumsq", [128, 1], F32))
        junk = stack.enter_context(nc.sbuf_tensor("junk", [128, D], BF16))
        sa_rep = stack.enter_context(nc.sbuf_tensor("sa_rep", [128, D], F32))
        ca1_rep = stack.enter_context(nc.sbuf_tensor("ca1_rep", [128, D], F32))

        blk = stack.enter_context(nc.Block())

        p1 = ExitStack()
        wk_sb = p1.enter_context(nc.sbuf_tensor("wk_sb", [128, DK * D], BF16))
        wv_sb = p1.enter_context(nc.sbuf_tensor("wv_sb", [128, DK * D], BF16))

        p0 = ExitStack()
        crep = p0.enter_context(nc.sbuf_tensor("crep", [128, D], BF16))
        c_fc = p0.enter_context(nc.sbuf_tensor("c_fc", [128, DK], F32))
        csil = p0.enter_context(nc.sbuf_tensor("csil", [128, DK], F32))
        wmbuf = [p0.enter_context(nc.sbuf_tensor(f"wmbuf{i}", [128, DK * 256], BF16))
                 for i in range(2)]
        modv = p0.enter_context(nc.sbuf_tensor("modv", [128, D], F32))
        modt = [p0.enter_context(nc.sbuf_tensor(f"modt{i}", [128, D], F32))
                for i in range(2)]
        ps_m = p0.enter_context(nc.psum_tensor("ps_m", [128, 1024], F32))
        ps_sm = p0.enter_context(nc.psum_tensor("ps_sm", [128, 16], F32))

        @blk.sync
        def _(sy):
            i1 = sy.dma_start(out=ld_sb[:, :], in_=ld_in[None, :])
            c.inc("ld1", i1, 16)
            i1 = sy.dma_start(out=own0_sb[:, :], in_=own0)
            c.inc("ld1", i1, 16)
            i1 = sy.dma_start(out=c_fc[:, :], in_=c_row.rearrange("(k p) -> p k", p=128),
                              allow_slow_non_contiguous=True)
            c.inc("ld1", i1, 16)
            if upto >= 2:
                i1 = sy.dma_start(out=normw_fc[:, :],
                                  in_=normw.rearrange("(k p) -> p k", p=128),
                                  allow_slow_non_contiguous=True)
                c.inc("ld1", i1, 16)

        m_sy0 = c.mark("ld1")

        @blk.gpsimd
        def _(g):
            mk = c.inc("gp", g.memset(ident[:, :], 0.0), 1)
            g.memset(ones_f[:, :], 1.0)
            c.wait(g, "gp", mk)
            g.affine_select(out=ident[:, :], in_=ident[:, :],
                            compare_op=ALU.not_equal, fill=1.0,
                            base=0, pattern=[[-1, 128]], channel_multiplier=1)
            g.memset(ones_1p[:, :], 1.0)
            g.memset(c128[:, :], 128.0)
            g.memset(eps_sb[:, :], EPS)
            g.iota(it_p1[:, :], pattern=[[0, 1]], base=1, channel_multiplier=1,
                   allow_small_or_imprecise_dtypes=True)
            i1 = g.iota(it_km[:, :], pattern=[[0, 1]], base=127, channel_multiplier=-1,
                        allow_small_or_imprecise_dtypes=True)
            c.inc("gp", i1, 1)

        m_gp_const = c.mark("gp")

        @blk.tensor
        def _(t):
            c.wait(t, "t", m_sy0)
            c.wait(t, "t", m_gp_const)
            i1 = t.matmul(ps_sm[:, :], ones_1p[:, :], ld_sb[:, :], start=True, stop=True)
            c.inc("t", i1, 1)

        m_t_ldb = c.mark("t")

        @blk.vector
        def _(v):
            c.wait(v, "v", m_t_ldb)
            i1 = v.tensor_copy(ldb[:, :], ps_sm[:, :])
            c.inc("v", i1, 1)

        m_v_ldb = c.mark("v")

        @blk.scalar
        def _(s):
            c.wait(s, "s", m_v_ldb)
            for h in range(H):
                s.activation(qdec16[:, h:h + 1], it_p1[:, :], AF.Exp,
                             scale=ldb[:, h:h + 1])
                s.activation(kdec16[:, h:h + 1], it_km[:, :], AF.Exp,
                             scale=ldb[:, h:h + 1])
                s.activation(cdec16[:, h:h + 1], c128[:, :], AF.Exp,
                             scale=ldb[:, h:h + 1])
            i1 = s.activation(csil[:, :], c_fc[:, :], AF.Silu)
            c.inc("s", i1, 1)

        m_s_dec = c.mark("s")

        @blk.vector
        def _(v):
            c.wait(v, "v", m_s_dec)
            for ch in range(DK):
                i1 = v.tensor_scalar(crep[:, ch * 128:(ch + 1) * 128], ones_f[:, :],
                                     csil[:, ch:ch + 1], None, op0=ALU.mult)
            c.inc("v", i1, 1)

        m_v_crep = c.mark("v")

        NMB = nmb
        m_wm, m_mm_mod, m_modev, m_modst = {}, {}, {}, {}

        @blk.gpsimd
        def _(g):
            i1 = g.dma_start(out=_b3(wk_sb[:, :], D), in_=_w3(wk))
            c.inc("wbig", i1, 16)
            i1 = g.dma_start(out=_b3(wv_sb[:, :], D), in_=_w3(wv))
            c.inc("wbig", i1, 16)

        m_gp_wkv = c.mark("wbig")


        for nb in range(NMB):
            vec, q = nb // 8, nb % 8

            @blk.gpsimd
            def _(g, nb=nb):
                if nb >= 2:
                    c.wait(g, "gp", m_mm_mod[nb - 2])
                i1 = g.dma_start(
                    out=_b3(wmbuf[nb % 2][:, :], 256),
                    in_=_w3(wmod[:, nb * 256:(nb + 1) * 256]))
                m_wm[nb] = c.inc("wA" if nb % 2 == 0 else "wB", i1, 16)

            @blk.tensor
            def _(t, nb=nb):
                c.wait(t, "t", m_v_crep)
                c.wait(t, "t", m_wm[nb])
                if nb >= 2:
                    c.wait(t, "t", m_modev[nb - 2])
                for dk in range(DK):
                    i1 = t.matmul(ps_m[:, (nb % 2) * 512:(nb % 2) * 512 + 256],
                                  crep[:, dk * 128:(dk + 1) * 128],
                                  wmbuf[nb % 2][:, dk * 256:(dk + 1) * 256],
                                  start=(dk == 0), stop=(dk == DK - 1))
                m_mm_mod[nb] = c.inc("t", i1, 1)

            @blk.vector
            def _(v, nb=nb, vec=vec, q=q):
                c.wait(v, "v", m_mm_mod[nb])
                if q == 0 and nb >= 8:
                    c.wait(v, "v", m_modst[nb - 1])
                src = ps_m[:, (nb % 2) * 512:(nb % 2) * 512 + 256]
                dst = modv[:, q * 256:(q + 1) * 256]
                if vec in (1, 4):
                    i1 = v.tensor_scalar(dst, src, 1.0, None, op0=ALU.add)
                else:
                    i1 = v.tensor_copy(dst, src)
                m_modev[nb] = c.inc("v", i1, 1)
                if q == 7 and vec in (2, 5):
                    c.wait(v, "v", m_modev[nb])
                    v.tensor_scalar(modt[0][:, :], modv[:, :], 0.25, None, op0=ALU.max)
                    mk = c.inc("v", v.tensor_scalar(modt[1][:, :], modv[:, :], -0.25,
                                                    None, op0=ALU.min), 1)
                    c.wait(v, "v", mk)
                    v.tensor_scalar(modv[:, :], modv[:, :], 0.0, None, op0=ALU.is_ge)
                    mk = c.inc("v", v.tensor_sub(modt[0][:, :], modt[0][:, :],
                                                 modt[1][:, :]), 1)
                    c.wait(v, "v", mk)
                    mk = c.inc("v", v.tensor_mul(modv[:, :], modv[:, :],
                                                 modt[0][:, :]), 1)
                    c.wait(v, "v", mk)
                    i1 = v.tensor_add(modv[:, :], modv[:, :], modt[1][:, :])
                    m_modev[nb] = c.inc("v", i1, 1)

            if q == 7:
                @blk.sync
                def _(sy, nb=nb, vec=vec):
                    c.wait(sy, "sy", m_modev[nb])
                    i1 = sy.dma_start(out=sc_mod[vec * CH:(vec + 1) * CH, :],
                                      in_=modv[:, :])
                    m_modst[nb] = c.inc("st", i1, 16)

        @blk.sync
        def _(sy):
            c.wait(sy, "sy", c.mark("st"))
            i1 = sy.dma_start(out=sa_rep[:, :], in_=sc_mod[0:CH, :])
            c.inc("ld1", i1, 16)
            i1 = sy.dma_start(out=ca1_rep[:, :], in_=sc_mod[CH:2 * CH, :])
            c.inc("ld1", i1, 16)

        p0.close()

        bar0 = c.barrier_marks()
        _emit_barrier(blk, c, bar0)

        xb = [p1.enter_context(nc.sbuf_tensor(f"xb{i}", [128, D], F32)) for i in range(2)]
        h_bf = p1.enter_context(nc.sbuf_tensor("h_bf", [128, D], BF16))
        hT = p1.enter_context(nc.sbuf_tensor("hT", [128, D], BF16))
        k_tok = p1.enter_context(nc.sbuf_tensor("k_tok", [128, D], BF16))
        v_tok = p1.enter_context(nc.sbuf_tensor("v_tok", [128, D], BF16))
        kT_sb = p1.enter_context(nc.sbuf_tensor("kT_sb", [128, D], BF16))
        S_sb = p1.enter_context(nc.sbuf_tensor("S_sb", [128, D], F32))
        stmp = p1.enter_context(nc.sbuf_tensor("stmp", [128, 128], F32))
        pt1 = p1.enter_context(nc.psum_tensor("pt1", [128, D], BF16))
        pkv = p1.enter_context(nc.psum_tensor("pkv", [128, D], F32))
        pS = p1.enter_context(nc.psum_tensor("pS", [128, 1024], F32))

        m_x, m_sq, m_var, m_rstds, m_hbf = {}, {}, {}, {}, {}
        m_tph, m_hTe, m_kv, m_kve, m_kdec = {}, {}, {}, {}, {}
        m_tpk, m_kTe, m_kTst, m_vst = {}, {}, {}, {}
        m_sst, m_dS, m_supd = {}, {}, {}

        @blk.vector
        def _(v):
            i1 = v.memset(S_sb[:, :], 0.0)
            c.inc("v", i1, 1)

        m_v_sinit = c.mark("v")

        def p1_store(g_ch):
            @blk.gpsimd
            def _(g, g_ch=g_ch):
                if g_ch >= 1:
                    c.wait(g, "gp", m_supd[(g_ch - 1, 3)])
                else:
                    c.wait(g, "gp", m_v_sinit)
                i1 = g.dma_start(out=sc_kvs[(g_ch * 3 + 2) * CH:(g_ch * 3 + 3) * CH, :],
                                 in_=S_sb[:, :])
                m_sst[g_ch] = c.inc("Sst", i1, 16)

        def p1_xload(g_ch):
            par = g_ch % 2

            @blk.sync
            def _(sy, g_ch=g_ch, par=par):
                if g_ch >= 2:
                    c.wait(sy, "sy", m_hbf[g_ch - 2])
                    c.wait(sy, "sy", m_sq[g_ch - 2])
                i1 = sy.dma_start(out=xb[par][:, :],
                                  in_=x_scan[g_ch * CH:(g_ch + 1) * CH, :])
                m_x[g_ch] = c.inc("xA" if par == 0 else "xB", i1, 16)

        def p1_ln(g_ch):
            par = g_ch % 2

            @blk.scalar
            def _(s, g_ch=g_ch, par=par):
                c.wait(s, "s", m_x[g_ch])
                i1 = s.activation(junk[:, :], xb[par][:, :], AF.Square,
                                  accum_out=ssumsq[:, :])
                m_sq[g_ch] = c.inc("s", i1, 1)

            @blk.vector
            def _(v, g_ch=g_ch, par=par):
                c.wait(v, "v", m_x[g_ch])
                mk = c.inc("v", v.reduce_sum(smalls[:, 0:1], xb[par][:, :], axis=AX.X,
                                             negate=True), 1)
                c.wait(v, "v", mk)
                mk = c.inc("v", v.tensor_scalar(smalls[:, 1:2], smalls[:, 0:1],
                                                1.0 / D, None, op0=ALU.mult), 1)
                c.wait(v, "v", mk)
                c.wait(v, "v", m_sq[g_ch])
                mk = c.inc("v", v.tensor_scalar(smalls[:, 2:3], ssumsq[:, :],
                                                1.0 / D, None, op0=ALU.mult), 1)
                c.wait(v, "v", mk)
                mk = c.inc("v", v.tensor_mul(smalls[:, 3:4], smalls[:, 1:2],
                                             smalls[:, 1:2]), 1)
                c.wait(v, "v", mk)
                i1 = v.tensor_sub(smalls[:, 4:5], smalls[:, 2:3], smalls[:, 3:4])
                m_var[g_ch] = c.inc("v", i1, 1)

            @blk.scalar
            def _(s, g_ch=g_ch):
                c.wait(s, "s", m_var[g_ch])
                i1 = s.activation(smalls[:, 5:6], smalls[:, 4:5], AF.Sqrt,
                                  bias=eps_sb[:, 0:1])
                m_rstds[g_ch] = c.inc("s", i1, 1)

            @blk.vector
            def _(v, g_ch=g_ch, par=par):
                c.wait(v, "v", m_rstds[g_ch])
                v.tensor_scalar(xb[par][:, :], xb[par][:, :], smalls[:, 1:2], None,
                                op0=ALU.add)
                i1 = v.reciprocal(smalls[:, 6:7], smalls[:, 5:6])
                mr = c.inc("v", i1, 1)
                c.wait(v, "v", mr)
                mk = c.inc("v", v.scalar_tensor_tensor(xb[par][:, :], ca1_rep[:, :],
                                                       smalls[:, 6:7], xb[par][:, :],
                                                       op0=ALU.mult, op1=ALU.mult), 1)
                c.wait(v, "v", mk)
                if g_ch >= 1:
                    c.wait(v, "v", m_tph[g_ch - 1])
                i1 = v.tensor_add(h_bf[:, :], xb[par][:, :], sa_rep[:, :])
                m_hbf[g_ch] = c.inc("v", i1, 1)

        def p1_rest(g_ch):
            par = g_ch % 2

            @blk.tensor
            def _(t, g_ch=g_ch):
                c.wait(t, "t", m_hbf[g_ch])
                if g_ch >= 1:
                    c.wait(t, "t", m_kTe[g_ch - 1])
                for dk in range(DK):
                    i1 = t.transpose(pt1[:, dk * 128:(dk + 1) * 128],
                                     h_bf[:, dk * 128:(dk + 1) * 128], ident[:, :])
                m_tph[g_ch] = c.inc("t", i1, 1)

            @blk.vector
            def _(v, g_ch=g_ch):
                c.wait(v, "v", m_tph[g_ch])
                if g_ch >= 1:
                    c.wait(v, "v", m_kv[(g_ch - 1, 3)])
                for eb in range(4):
                    i1 = v.tensor_copy(hT[:, eb * 512:(eb + 1) * 512],
                                       pt1[:, eb * 512:(eb + 1) * 512])
                m_hTe[g_ch] = c.inc("v", i1, 1)

            if g_ch + 1 < nch:
                p1_xload(g_ch + 1)
                p1_ln(g_ch + 1)

            for eb in range(4):
                pp = eb % 2

                @blk.tensor
                def _(t, g_ch=g_ch, eb=eb, pp=pp):
                    c.wait(t, "t", m_hTe[g_ch])
                    c.wait(t, "t", m_gp_wkv)
                    guard = m_kve.get((g_ch, eb - 2), m_kve.get((g_ch - 1, eb + 2), 0))
                    c.wait(t, "t", guard)
                    for dk in range(DK):
                        t.matmul(pkv[:, pp * 1024:pp * 1024 + 512],
                                 hT[:, dk * 128:(dk + 1) * 128],
                                 wk_sb[:, dk * D + eb * 512:dk * D + (eb + 1) * 512],
                                 start=(dk == 0), stop=(dk == DK - 1))
                        i1 = t.matmul(pkv[:, pp * 1024 + 512:pp * 1024 + 1024],
                                      hT[:, dk * 128:(dk + 1) * 128],
                                      wv_sb[:, dk * D + eb * 512:dk * D + (eb + 1) * 512],
                                      start=(dk == 0), stop=(dk == DK - 1))
                    m_kv[(g_ch, eb)] = c.inc("t", i1, 1)

                @blk.scalar
                def _(s, g_ch=g_ch, eb=eb, pp=pp):
                    c.wait(s, "s", m_kv[(g_ch, eb)])
                    if g_ch >= 1 and eb == 0:
                        c.wait(s, "s", m_dS[(g_ch - 1, 3)])
                        c.wait(s, "s", m_vst[g_ch - 1])
                        c.wait(s, "s", m_tpk[g_ch - 1])
                        c.wait(s, "s", m_kdec[g_ch - 1])
                    s.activation(k_tok[:, eb * 512:(eb + 1) * 512],
                                 pkv[:, pp * 1024:pp * 1024 + 512], AF.Silu)
                    i1 = s.activation(v_tok[:, eb * 512:(eb + 1) * 512],
                                      pkv[:, pp * 1024 + 512:pp * 1024 + 1024], AF.Silu)
                    m_kve[(g_ch, eb)] = c.inc("s", i1, 1)

            @blk.tensor
            def _(t, g_ch=g_ch):
                c.wait(t, "t", m_kve[(g_ch, 3)])
                c.wait(t, "t", m_hTe[g_ch])
                for h in range(H):
                    i1 = t.transpose(pt1[:, h * 128:(h + 1) * 128],
                                     k_tok[:, h * 128:(h + 1) * 128], ident[:, :])
                m_tpk[g_ch] = c.inc("t", i1, 1)

            @blk.vector
            def _(v, g_ch=g_ch):
                c.wait(v, "v", m_tpk[g_ch])
                if g_ch >= 1:
                    c.wait(v, "v", m_vst[g_ch - 1])
                for eb in range(4):
                    i1 = v.tensor_copy(kT_sb[:, eb * 512:(eb + 1) * 512],
                                       pt1[:, eb * 512:(eb + 1) * 512])
                m_kTe[g_ch] = c.inc("v", i1, 1)

            @blk.vector
            def _(v, g_ch=g_ch):
                c.wait(v, "v", m_tpk[g_ch])
                if g_ch >= 1:
                    c.wait(v, "v", m_dS[(g_ch - 1, 3)])
                for h in range(H):
                    i1 = v.tensor_scalar(k_tok[:, h * 128:(h + 1) * 128],
                                         k_tok[:, h * 128:(h + 1) * 128],
                                         kdec16[:, h:h + 1], None, op0=ALU.mult)
                m_kdec[g_ch] = c.inc("v", i1, 1)

            @blk.gpsimd
            def _(g, g_ch=g_ch):
                c.wait(g, "gp", m_kTe[g_ch])
                i1 = g.dma_start(out=sc_kvs[(g_ch * 3 + 0) * CH:(g_ch * 3 + 1) * CH, :],
                                 in_=kT_sb[:, :])
                m_kTst[g_ch] = c.inc("kvst", i1, 16)
                c.wait(g, "gp", m_kve[(g_ch, 3)])
                i1 = g.dma_start(out=sc_kvs[(g_ch * 3 + 1) * CH:(g_ch * 3 + 2) * CH, :],
                                 in_=v_tok[:, :])
                m_vst[g_ch] = c.inc("kvst", i1, 16)

            for hg in range(4):
                @blk.tensor
                def _(t, g_ch=g_ch, hg=hg):
                    c.wait(t, "t", m_kdec[g_ch])
                    guard = m_supd.get((g_ch, hg - 2), m_supd.get((g_ch - 1, hg + 2), 0))
                    c.wait(t, "t", guard)
                    for h in range(hg * 4, hg * 4 + 4):
                        i1 = t.matmul(pS[:, (hg % 2) * 512 + (h % 4) * 128:
                                         (hg % 2) * 512 + (h % 4 + 1) * 128],
                                      k_tok[:, h * 128:(h + 1) * 128],
                                      v_tok[:, h * 128:(h + 1) * 128],
                                      start=True, stop=True)
                    m_dS[(g_ch, hg)] = c.inc("t", i1, 1)

                @blk.vector
                def _(v, g_ch=g_ch, hg=hg):
                    c.wait(v, "v", m_dS[(g_ch, hg)])
                    c.wait(v, "v", m_sst[g_ch])
                    for h in range(hg * 4, hg * 4 + 4):
                        i1 = v.scalar_tensor_tensor(
                            S_sb[:, h * 128:(h + 1) * 128],
                            S_sb[:, h * 128:(h + 1) * 128],
                            cdec16[:, h:h + 1],
                            pS[:, (hg % 2) * 512 + (h % 4) * 128:
                               (hg % 2) * 512 + (h % 4 + 1) * 128],
                            op0=ALU.mult, op1=ALU.add)
                    m_supd[(g_ch, hg)] = c.inc("v", i1, 1)

        p1_xload(0)
        p1_ln(0)
        for g_ch in range(nch):
            p1_store(g_ch)
            p1_rest(g_ch)

        if upto == 1:
            @blk.gpsimd
            def _(g):
                for sn in ("st", "Sst", "kvst", "v", "s", "t"):
                    c.wait(g, "gp", c.mark(sn))
            p1.close()
            return nc
        p1.close()

        bar1 = c.barrier_marks()
        _emit_barrier(blk, c, bar1)

        p2 = ExitStack()
        wq_sb = p2.enter_context(nc.sbuf_tensor("wq_sb", [128, DK * D], BF16))
        wg1_sb = p2.enter_context(nc.sbuf_tensor("wg1_sb", [128, DK * 128], BF16))
        wg2_sb = p2.enter_context(nc.sbuf_tensor("wg2_sb", [128, D], BF16))
        Dt = p2.enter_context(nc.sbuf_tensor("Dt", [128, H * 128], F32))
        difT = p2.enter_context(nc.sbuf_tensor("difT", [128, 128], F32))
        xb2 = [p2.enter_context(nc.sbuf_tensor(f"x2_{i}", [128, D], F32)) for i in range(2)]
        kvs = [p2.enter_context(nc.sbuf_tensor(f"kvs{i}", [128, 3 * D], BF16))
               for i in range(2)]
        kTb = [kvs[i][:, 0:D] for i in range(2)]
        vbb = [kvs[i][:, D:2 * D] for i in range(2)]
        Sbb = [kvs[i][:, 2 * D:3 * D] for i in range(2)]
        h_bf2 = p2.enter_context(nc.sbuf_tensor("h_bf2", [128, D], BF16))
        hT2 = p2.enter_context(nc.sbuf_tensor("hT2", [128, D], BF16))
        q_tok = p2.enter_context(nc.sbuf_tensor("q_tok", [128, D], BF16))
        q_dec = p2.enter_context(nc.sbuf_tensor("q_dec", [128, D], BF16))
        qT = p2.enter_context(nc.sbuf_tensor("qT", [128, D], BF16))
        qTd = p2.enter_context(nc.sbuf_tensor("qTd", [128, D], BF16))
        g1T = p2.enter_context(nc.sbuf_tensor("g1T", [128, 128], BF16))
        sig = p2.enter_context(nc.sbuf_tensor("sig", [128, D], F32))
        scm = p2.enter_context(nc.sbuf_tensor("scm", [128, D], BF16))
        o_sb = p2.enter_context(nc.sbuf_tensor("o_sb", [128, D], F32))
        tmpf = p2.enter_context(nc.sbuf_tensor("tmpf", [128, D], F32))
        orms = p2.enter_context(nc.sbuf_tensor("orms", [128, D], BF16))
        oT_sb = p2.enter_context(nc.sbuf_tensor("oT_sb", [128, D], BF16))
        ms16 = p2.enter_context(nc.sbuf_tensor("ms16", [128, H], F32))
        rmss = p2.enter_context(nc.sbuf_tensor("rmss", [128, H], F32))
        rrr = p2.enter_context(nc.sbuf_tensor("rrr", [128, H], F32))
        pt2 = p2.enter_context(nc.psum_tensor("pt2", [128, D], BF16))
        pq = p2.enter_context(nc.psum_tensor("pq", [128, D], F32))
        psc = p2.enter_context(nc.psum_tensor("psc", [128, 1024], F32))

        @blk.gpsimd
        def _(g):
            i1 = g.dma_start(out=_b3(wq_sb[:, :], D), in_=_w3(wq))
            c.inc("wbig", i1, 16)
            i1 = g.dma_start(out=_b3(wg1_sb[:, :], 128), in_=_w3(wg1))
            c.inc("wbig", i1, 16)
            i1 = g.dma_start(out=wg2_sb[:, :], in_=wg2)
            c.inc("wbig", i1, 16)
            mk = c.inc("gp", g.iota(difT[:, :], pattern=[[1, 128]], base=0,
                                    channel_multiplier=-1,
                                    allow_small_or_imprecise_dtypes=True), 1)
            c.wait(g, "gp", mk)
            i1 = g.affine_select(out=difT[:, :], in_=difT[:, :],
                                 compare_op=ALU.is_ge, fill=0.0,
                                 base=0, pattern=[[1, 128]], channel_multiplier=-1)
            c.inc("gp", i1, 1)

        m_gp_p2w = c.mark("wbig")
        m_gp_p2i = c.mark("gp")

        @blk.scalar
        def _(s):
            c.wait(s, "s", m_gp_p2w)
            c.wait(s, "s", m_gp_p2i)
            for h in range(H):
                i1 = s.activation(Dt[:, h * 128:(h + 1) * 128], difT[:, :], AF.Exp,
                                  scale=ldb[:, h:h + 1])
            c.inc("s", i1, 1)

        m_s_dtexp = c.mark("s")

        @blk.gpsimd
        def _(g):
            c.wait(g, "gp", m_s_dtexp)
            for h in range(H):
                i1 = g.affine_select(out=Dt[:, h * 128:(h + 1) * 128],
                                     in_=Dt[:, h * 128:(h + 1) * 128],
                                     compare_op=ALU.is_ge, fill=0.0,
                                     base=0, pattern=[[1, 128]], channel_multiplier=-1)
            c.inc("gp", i1, 1)

        m_gp_dt = c.mark("gp")

        (m2_x, m2_ld, m2_sq, m2_var, m2_rstds, m2_hbf, m2_tph, m2_hTe,
         m2_q, m2_qe, m2_qdec, m2_tpq, m2_qTe, m2_tpqd, m2_qTde, m2_g1, m2_g1e,
         m2_gate, m2_sig, m2_sc, m2_scm, m2_o, m2_oe, m2_ms, m2_rmss, m2_orms,
         m2_tpo, m2_oTe, m2_oTst) = ({} for _ in range(29))

        rholder = {}

        def p2_ln_all(i2):
            i = i2
            par = i % 2

            @blk.sync
            def _(sy, i=i, par=par):
                if i >= 2:
                    c.wait(sy, "sy", m2_hbf[i - 2])
                    c.wait(sy, "sy", m2_sq[i - 2])
                i1 = sy.dma_start(out=xb2[par][:, :], in_=x_own[i * CH:(i + 1) * CH, :])
                m2_x[i] = c.inc("xA" if par == 0 else "xB", i1, 16)

            @blk.scalar
            def _(s, i=i, par=par):
                c.wait(s, "s", m2_x[i])
                i1 = s.activation(junk[:, :], xb2[par][:, :], AF.Square,
                                  accum_out=ssumsq[:, :])
                m2_sq[i] = c.inc("s", i1, 1)

            @blk.vector
            def _(v, i=i, par=par):
                c.wait(v, "v", m2_x[i])
                mk = c.inc("v", v.reduce_sum(smalls[:, 0:1], xb2[par][:, :], axis=AX.X,
                                             negate=True), 1)
                c.wait(v, "v", mk)
                mk = c.inc("v", v.tensor_scalar(smalls[:, 1:2], smalls[:, 0:1],
                                                1.0 / D, None, op0=ALU.mult), 1)
                c.wait(v, "v", mk)
                c.wait(v, "v", m2_sq[i])
                mk = c.inc("v", v.tensor_scalar(smalls[:, 2:3], ssumsq[:, :],
                                                1.0 / D, None, op0=ALU.mult), 1)
                c.wait(v, "v", mk)
                mk = c.inc("v", v.tensor_mul(smalls[:, 3:4], smalls[:, 1:2],
                                             smalls[:, 1:2]), 1)
                c.wait(v, "v", mk)
                i1 = v.tensor_sub(smalls[:, 4:5], smalls[:, 2:3], smalls[:, 3:4])
                m2_var[i] = c.inc("v", i1, 1)

            @blk.scalar
            def _(s, i=i):
                c.wait(s, "s", m2_var[i])
                i1 = s.activation(smalls[:, 5:6], smalls[:, 4:5], AF.Sqrt, bias=eps_sb[:, 0:1])
                m2_rstds[i] = c.inc("s", i1, 1)

            @blk.vector
            def _(v, i=i, par=par):
                c.wait(v, "v", m2_rstds[i])
                v.tensor_scalar(xb2[par][:, :], xb2[par][:, :], smalls[:, 1:2], None,
                                op0=ALU.add)
                i1 = v.reciprocal(smalls[:, 6:7], smalls[:, 5:6])
                mr = c.inc("v", i1, 1)
                c.wait(v, "v", mr)
                mk = c.inc("v", v.scalar_tensor_tensor(xb2[par][:, :], ca1_rep[:, :],
                                                       smalls[:, 6:7], xb2[par][:, :],
                                                       op0=ALU.mult, op1=ALU.mult), 1)
                c.wait(v, "v", mk)
                if i >= 1:
                    c.wait(v, "v", m2_tph[i - 1])
                i1 = v.tensor_add(h_bf2[:, :], xb2[par][:, :], sa_rep[:, :])
                m2_hbf[i] = c.inc("v", i1, 1)



        p2_ln_all(0)

        for i in range(OWN):
            par = i % 2

            @blk.gpsimd
            def _(g, i=i, par=par):
                if i == 0:
                    c.wait(g, "gp", m_sy0)
                    rholder["r"] = g.register("r_off").__enter__()
                    g.reg_load(rholder["r"], own0_sb[0:1, 0:1])
                    g.reg_alu(rholder["r"], rholder["r"], 3 * CHUNK_ELEMS, ALU.mult)
                if i >= 2:
                    c.wait(g, "gp", m2_o[(i - 2, 1)])
                r = rholder["r"]
                i1 = g.dma_start(
                    out=kvs[par][:, :].rearrange("p (j e) -> p j e", e=D),
                    in_=bass.AP(sc_kvs.tensor, r, [[D, 128], [CH * D, 3], [1, D]]))
                m2_ld[i] = c.inc("kvA" if par == 0 else "kvB", i1, 16)
                g.reg_alu(r, r, 3 * CHUNK_ELEMS, ALU.add)

            @blk.tensor
            def _(t, i=i):
                c.wait(t, "t", m2_hbf[i])
                if i >= 1:
                    c.wait(t, "t", m2_oTe[i - 1])
                for dk in range(DK):
                    i1 = t.transpose(pt2[:, dk * 128:(dk + 1) * 128],
                                     h_bf2[:, dk * 128:(dk + 1) * 128], ident[:, :])
                m2_tph[i] = c.inc("t", i1, 1)

            @blk.vector
            def _(v, i=i):
                c.wait(v, "v", m2_tph[i])
                if i >= 1:
                    c.wait(v, "v", m2_g1[i - 1])
                    c.wait(v, "v", m2_q[i - 1])
                for eb in range(4):
                    i1 = v.tensor_copy(hT2[:, eb * 512:(eb + 1) * 512],
                                       pt2[:, eb * 512:(eb + 1) * 512])
                m2_hTe[i] = c.inc("v", i1, 1)

            if i + 1 < OWN:
                p2_ln_all(i + 1)

            @blk.tensor
            def _(t, i=i):
                c.wait(t, "t", m2_hTe[i])
                c.wait(t, "t", m_gp_p2w)
                if i >= 1:
                    c.wait(t, "t", m2_oe[(i - 1, 1)])
                for dk in range(DK):
                    for eb in range(4):
                        i1 = t.matmul(pq[:, eb * 512:(eb + 1) * 512],
                                      hT2[:, dk * 128:(dk + 1) * 128],
                                      wq_sb[:, dk * D + eb * 512:dk * D + (eb + 1) * 512],
                                      start=(dk == 0), stop=(dk == DK - 1))
                m2_q[i] = c.inc("t", i1, 1)

            @blk.scalar
            def _(s, i=i):
                c.wait(s, "s", m2_q[i])
                if i >= 1:
                    c.wait(s, "s", m2_tpq[i - 1])
                    c.wait(s, "s", m2_qdec[i - 1])
                for eb in range(4):
                    i1 = s.activation(q_tok[:, eb * 512:(eb + 1) * 512],
                                      pq[:, eb * 512:(eb + 1) * 512], AF.Silu)
                m2_qe[i] = c.inc("s", i1, 1)

            @blk.vector
            def _(v, i=i):
                c.wait(v, "v", m2_qe[i])
                if i >= 1:
                    c.wait(v, "v", m2_tpqd[i - 1])
                for h in range(H):
                    i1 = v.tensor_scalar(q_dec[:, h * 128:(h + 1) * 128],
                                         q_tok[:, h * 128:(h + 1) * 128],
                                         qdec16[:, h:h + 1], None, op0=ALU.mult)
                m2_qdec[i] = c.inc("v", i1, 1)

            @blk.tensor
            def _(t, i=i):
                c.wait(t, "t", m2_qe[i])
                c.wait(t, "t", m2_hTe[i])
                for h in range(H):
                    i1 = t.transpose(pt2[:, h * 128:(h + 1) * 128],
                                     q_tok[:, h * 128:(h + 1) * 128], ident[:, :])
                m2_tpq[i] = c.inc("t", i1, 1)

            @blk.vector
            def _(v, i=i):
                c.wait(v, "v", m2_tpq[i])
                if i >= 1:
                    c.wait(v, "v", m2_sc[(i - 1, 1)])
                for eb in range(4):
                    i1 = v.tensor_copy(qT[:, eb * 512:(eb + 1) * 512],
                                       pt2[:, eb * 512:(eb + 1) * 512])
                m2_qTe[i] = c.inc("v", i1, 1)

            @blk.tensor
            def _(t, i=i):
                c.wait(t, "t", m2_qdec[i])
                c.wait(t, "t", m2_qTe[i])
                for h in range(H):
                    i1 = t.transpose(pt2[:, h * 128:(h + 1) * 128],
                                     q_dec[:, h * 128:(h + 1) * 128], ident[:, :])
                m2_tpqd[i] = c.inc("t", i1, 1)

            @blk.vector
            def _(v, i=i):
                c.wait(v, "v", m2_tpqd[i])
                if i >= 1:
                    c.wait(v, "v", m2_o[(i - 1, 1)])
                for eb in range(4):
                    i1 = v.tensor_copy(qTd[:, eb * 512:(eb + 1) * 512],
                                       pt2[:, eb * 512:(eb + 1) * 512])
                m2_qTde[i] = c.inc("v", i1, 1)

            @blk.tensor
            def _(t, i=i):
                c.wait(t, "t", m2_hTe[i])
                guard = m2_scm.get((i - 1, 1), 0)
                c.wait(t, "t", guard)
                for dk in range(DK):
                    i1 = t.matmul(psc[:, 0:128],
                                  wg1_sb[:, dk * 128:(dk + 1) * 128],
                                  hT2[:, dk * 128:(dk + 1) * 128],
                                  start=(dk == 0), stop=(dk == DK - 1))
                m2_g1[i] = c.inc("t", i1, 1)

            @blk.vector
            def _(v, i=i):
                c.wait(v, "v", m2_g1[i])
                if i >= 1:
                    c.wait(v, "v", m2_gate[i - 1])
                i1 = v.tensor_copy(g1T[:, :], psc[:, 0:128])
                m2_g1e[i] = c.inc("v", i1, 1)

            @blk.tensor
            def _(t, i=i):
                c.wait(t, "t", m2_g1e[i])
                c.wait(t, "t", m2_qe[i])
                for eb in range(4):
                    i1 = t.matmul(pq[:, eb * 512:(eb + 1) * 512], g1T[:, :],
                                  wg2_sb[:, eb * 512:(eb + 1) * 512],
                                  start=True, stop=True)
                m2_gate[i] = c.inc("t", i1, 1)

            @blk.scalar
            def _(s, i=i):
                c.wait(s, "s", m2_gate[i])
                if i >= 1:
                    c.wait(s, "s", m2_ms[i - 1])
                for eb in range(4):
                    i1 = s.activation(sig[:, eb * 512:(eb + 1) * 512],
                                      pq[:, eb * 512:(eb + 1) * 512], AF.Sigmoid)
                m2_sig[i] = c.inc("s", i1, 1)

            for hg in range(2):
                @blk.tensor
                def _(t, i=i, hg=hg, par=par):
                    c.wait(t, "t", m2_qTe[i])
                    c.wait(t, "t", m2_ld[i])
                    guard = m2_scm.get((i, hg - 1), m2_scm.get((i - 1, 1), 0))
                    c.wait(t, "t", guard)
                    if hg == 0:
                        c.wait(t, "t", m2_g1e[i])
                    for h in range(hg * 8, hg * 8 + 8):
                        i1 = t.matmul(psc[:, (h % 8) * 128:(h % 8 + 1) * 128],
                                      kTb[par][:, h * 128:(h + 1) * 128],
                                      qT[:, h * 128:(h + 1) * 128],
                                      start=True, stop=True)
                    m2_sc[(i, hg)] = c.inc("t", i1, 1)

                @blk.vector
                def _(v, i=i, hg=hg):
                    c.wait(v, "v", m2_sc[(i, hg)])
                    c.wait(v, "v", m_gp_dt)
                    if i >= 1:
                        c.wait(v, "v", m2_o[(i - 1, hg)])
                    i1 = v.tensor_mul(scm[:, hg * 1024:(hg + 1) * 1024],
                                      psc[:, :],
                                      Dt[:, hg * 1024:(hg + 1) * 1024])
                    m2_scm[(i, hg)] = c.inc("v", i1, 1)

                @blk.tensor
                def _(t, i=i, hg=hg, par=par):
                    c.wait(t, "t", m2_scm[(i, hg)])
                    c.wait(t, "t", m2_qTde[i])
                    c.wait(t, "t", m2_sig[i])
                    guard = m2_oe.get((i - 1, hg), 0)
                    c.wait(t, "t", guard)
                    for h in range(hg * 8, hg * 8 + 8):
                        sl = slice((hg % 2) * 1024 + (h % 8) * 128,
                                   (hg % 2) * 1024 + (h % 8 + 1) * 128)
                        t.matmul(pq[:, sl], scm[:, h * 128:(h + 1) * 128],
                                 vbb[par][:, h * 128:(h + 1) * 128],
                                 start=True, stop=False)
                        i1 = t.matmul(pq[:, sl], qTd[:, h * 128:(h + 1) * 128],
                                      Sbb[par][:, h * 128:(h + 1) * 128],
                                      start=False, stop=True)
                    m2_o[(i, hg)] = c.inc("t", i1, 1)

                @blk.vector
                def _(v, i=i, hg=hg):
                    c.wait(v, "v", m2_o[(i, hg)])
                    i1 = v.tensor_copy(o_sb[:, hg * 1024:(hg + 1) * 1024],
                                       pq[:, (hg % 2) * 1024:(hg % 2 + 1) * 1024])
                    m2_oe[(i, hg)] = c.inc("v", i1, 1)

            @blk.vector
            def _(v, i=i):
                c.wait(v, "v", m2_sig[i])
                c.wait(v, "v", m2_oe[(i, 1)])
                mk = c.inc("v", v.tensor_mul(o_sb[:, :], o_sb[:, :], sig[:, :]), 1)
                c.wait(v, "v", mk)
                mk = c.inc("v", v.tensor_mul(tmpf[:, :], o_sb[:, :], o_sb[:, :]), 1)
                c.wait(v, "v", mk)
                i1 = v.tensor_reduce(ms16[:, :],
                                     tmpf[:, :].rearrange("p (h q) -> p h q", q=128),
                                     axis=AX.X, op=ALU.add)
                m2_ms[i] = c.inc("v", i1, 1)

            @blk.scalar
            def _(s, i=i):
                c.wait(s, "s", m2_ms[i])
                i1 = s.activation(rmss[:, :], ms16[:, :], AF.Sqrt,
                                  scale=1.0 / HD, bias=eps_sb[:, 0:1])
                m2_rmss[i] = c.inc("s", i1, 1)

            @blk.vector
            def _(v, i=i):
                c.wait(v, "v", m2_rmss[i])
                i1 = v.reciprocal(rrr[:, :], rmss[:, :])
                mr = c.inc("v", i1, 1)
                c.wait(v, "v", mr)
                if i >= 1:
                    c.wait(v, "v", m2_tpo[i - 1])
                for h in range(H):
                    i1 = v.tensor_scalar(orms[:, h * 128:(h + 1) * 128],
                                         o_sb[:, h * 128:(h + 1) * 128],
                                         rrr[:, h:h + 1], None, op0=ALU.mult)
                m2_orms[i] = c.inc("v", i1, 1)

            @blk.tensor
            def _(t, i=i):
                c.wait(t, "t", m2_orms[i])
                c.wait(t, "t", m2_qTde[i])
                for dk in range(DK):
                    i1 = t.transpose(pt2[:, dk * 128:(dk + 1) * 128],
                                     orms[:, dk * 128:(dk + 1) * 128], ident[:, :])
                m2_tpo[i] = c.inc("t", i1, 1)

            @blk.scalar
            def _(s, i=i):
                c.wait(s, "s", m2_tpo[i])
                if i >= 1:
                    c.wait(s, "s", m2_oTst[i - 1])
                for dk in range(DK):
                    i1 = s.activation(oT_sb[:, dk * 128:(dk + 1) * 128],
                                      pt2[:, dk * 128:(dk + 1) * 128], AF.Copy,
                                      scale=normw_fc[:, dk:dk + 1])
                m2_oTe[i] = c.inc("s", i1, 1)

            @blk.sync
            def _(sy, i=i):
                c.wait(sy, "sy", m2_oTe[i])
                i1 = sy.dma_start(out=sc_oT[i * CH:(i + 1) * CH, :], in_=oT_sb[:, :])
                m2_oTst[i] = c.inc("st", i1, 16)

        if upto == 2:
            @blk.gpsimd
            def _(g):
                for sn in ("st", "v", "s", "t"):
                    c.wait(g, "gp", c.mark(sn))
            p2.close()
            return nc
        p2.close()

        bar2 = c.barrier_marks()
        _emit_barrier(blk, c, bar2)

        p3 = ExitStack()
        wo_sb = p3.enter_context(nc.sbuf_tensor("wo_sb", [128, DK * D], BF16))
        ga_rep = p3.enter_context(nc.sbuf_tensor("ga_rep", [128, D], F32))
        oTb = [p3.enter_context(nc.sbuf_tensor(f"oTb{i}", [128, D], BF16)) for i in range(2)]
        xb3 = [p3.enter_context(nc.sbuf_tensor(f"x3_{i}", [128, D], F32)) for i in range(2)]
        xm_sb = p3.enter_context(nc.sbuf_tensor("xm_sb", [128, D], F32))
        tmp3 = p3.enter_context(nc.sbuf_tensor("tmp3", [128, D], F32))
        po2 = [p3.enter_context(nc.psum_tensor("po_%d" % _i, [128, D], F32))
               for _i in range(2)]

        @blk.gpsimd
        def _(g):
            i1 = g.dma_start(out=_b3(wo_sb[:, :], D), in_=_w3(wo))
            c.inc("wbig", i1, 16)

        m3_wo = c.mark("wbig")

        @blk.sync
        def _(sy):
            i1 = sy.dma_start(out=ga_rep[:, :], in_=sc_mod[2 * CH:3 * CH, :])
            c.inc("ld1", i1, 16)

        m3_ga = c.mark("ld1")
        m3_oT, m3_x, m3_mm, m3_xm, m3_st = {}, {}, {}, {}, {}

        for i in range(OWN):
            par = i % 2

            @blk.gpsimd
            def _(g, i=i, par=par):
                if i >= 2:
                    c.wait(g, "gp", m3_mm[i - 2])
                i1 = g.dma_start(out=oTb[par][:, :], in_=sc_oT[i * CH:(i + 1) * CH, :])
                m3_oT[i] = c.inc("oTA" if par == 0 else "oTB", i1, 16)

            @blk.sync
            def _(sy, i=i, par=par):
                if i >= 2:
                    c.wait(sy, "sy", m3_xm[i - 2])
                i1 = sy.dma_start(out=xb3[par][:, :], in_=x_own[i * CH:(i + 1) * CH, :])
                m3_x[i] = c.inc("xA" if par == 0 else "xB", i1, 16)

            @blk.tensor
            def _(t, i=i, par=par):
                c.wait(t, "t", m3_oT[i])
                c.wait(t, "t", m3_wo)
                if i >= 2:
                    c.wait(t, "t", m3_xm[i - 2])
                po = po2[i % 2]
                for dk in range(DK):
                    for eb in range(4):
                        i1 = t.matmul(po[:, eb * 512:(eb + 1) * 512],
                                      oTb[par][:, dk * 128:(dk + 1) * 128],
                                      wo_sb[:, dk * D + eb * 512:dk * D + (eb + 1) * 512],
                                      start=(dk == 0), stop=(dk == DK - 1))
                m3_mm[i] = c.inc("t", i1, 1)

            if dbg and upto == 3 and i == 0:
                @blk.vector
                def _(v):
                    c.wait(v, "v", m3_mm[0])
                    i1 = v.tensor_copy(tmp3[:, :], po[:, :])
                    c.inc("v", i1, 1)

                @blk.sync
                def _(sy):
                    c.wait(sy, "sy", c.mark("v"))
                    i1 = sy.dma_start(out=dbg_att, in_=tmp3[:, :])
                    c.inc("st", i1, 16)
                    i1 = sy.dma_start(out=dbg_oTb, in_=oTb[0][:, :])
                    c.inc("st", i1, 16)

            @blk.vector
            def _(v, i=i, par=par):
                c.wait(v, "v", m3_mm[i])
                if dbg and upto == 3 and i == 0:
                    c.wait(v, "v", c.mark("st"))
                c.wait(v, "v", m3_x[i])
                c.wait(v, "v", m3_ga)
                if i >= 1:
                    c.wait(v, "v", m3_st[i - 1])
                po = po2[i % 2]
                mk = c.inc("v", v.tensor_mul(tmp3[:, :], po[:, :], ga_rep[:, :]), 1)
                c.wait(v, "v", mk)
                i1 = v.tensor_add(xm_sb[:, :], tmp3[:, :], xb3[par][:, :])
                m3_xm[i] = c.inc("v", i1, 1)

            @blk.sync
            def _(sy, i=i):
                c.wait(sy, "sy", m3_xm[i])
                i1 = sy.dma_start(out=sc_xmid[i * CH:(i + 1) * CH, :], in_=xm_sb[:, :])
                m3_st[i] = c.inc("st", i1, 16)

        if upto == 3:
            @blk.gpsimd
            def _(g):
                for sn in ("st", "v", "s", "t"):
                    c.wait(g, "gp", c.mark(sn))
            p3.close()
            return nc
        p3.close()

        bar3 = c.barrier_marks()
        _emit_barrier(blk, c, bar3)

        p4 = ExitStack()
        h2T = p4.enter_context(nc.sbuf_tensor("h2T", [128, DK * TILE], BF16))
        zT = p4.enter_context(nc.sbuf_tensor("zT", [128, ICN * TILE], BF16))
        acc = p4.enter_context(nc.sbuf_tensor("acc", [128, TSUB * D], F32))
        wup = [p4.enter_context(nc.sbuf_tensor(f"wup{i}", [128, DK * 128], BF16))
               for i in range(2)]
        wdn = p4.enter_context(nc.sbuf_tensor("wdn", [128, 4 * D], BF16))
        modb = [p4.enter_context(nc.sbuf_tensor(f"modb{i}", [128, D], F32))
                for i in range(2)]
        gwT_sb = p4.enter_context(nc.sbuf_tensor("gwT_sb", [128, DK * 8], BF16))
        xm4b = [p4.enter_context(nc.sbuf_tensor(f"xm4_{i}", [128, D], F32))
                for i in range(2)]
        xm4 = xm4b[0]
        h2bf = p4.enter_context(nc.sbuf_tensor("h2bf", [128, D], BF16))
        silt = p4.enter_context(nc.sbuf_tensor("silt", [128, TILE], F32))
        lgT = p4.enter_context(nc.sbuf_tensor("lgT", [8, TILE], BF16))
        lgt = p4.enter_context(nc.sbuf_tensor("lgt", [128, TSUB * 8], F32))
        rt0 = p4.enter_context(nc.sbuf_tensor("rt0", [128, TSUB * 8], F32))
        rt1 = p4.enter_context(nc.sbuf_tensor("rt1", [128, TSUB * 8], F32))
        rt2 = p4.enter_context(nc.sbuf_tensor("rt2", [128, TSUB * 8], F32))
        rts = p4.enter_context(nc.sbuf_tensor("rts", [128, TSUB], F32))
        comb = p4.enter_context(nc.sbuf_tensor("comb", [128, TSUB * 8], F32))

        @blk.gpsimd
        def _(g):
            i1 = g.dma_start(out=_b3(gwT_sb[:, :], 8), in_=_w3(gwT))
            c.inc("wbig", i1, 16)

        m4_gw = c.mark("wbig")

        def up_src(e10, mat, ic):
            if e10 < 8:
                base = (egT if mat == 0 else euT)[e10]
            else:
                base = (sgT if mat == 0 else suT)[:, (e10 - 8) * ISH:(e10 - 7) * ISH]
            return _w3(base[:, ic * 128:(ic + 1) * 128])

        def dn_src(e10, sw_i):
            r0 = sum(DSW[:sw_i]) * 128
            r1 = r0 + DSW[sw_i] * 128
            if e10 < 8:
                base = edT[e10]
            else:
                base = sdT[(e10 - 8) * ISH:(e10 - 7) * ISH, :]
            return _w3(base[r0:r1, :])

        for tt in range(NTILE):
            bar_t = c.barrier_marks()
            _emit_barrier(blk, c, bar_t)

            t4 = ExitStack()
            pscope = ExitStack()
            pup = t4.enter_context(nc.psum_tensor("pup_%d" % tt, [128, D], F32))
            pt4 = pscope.enter_context(nc.psum_tensor("pt4_%d" % tt, [128, D], BF16))

            m4_xm, m4_sq, m4_var, m4_rstds, m4_hbf = {}, {}, {}, {}, {}
            m4_tph, m4_he = {}, {}

            @blk.sync
            def _(sy, tt=tt):
                i1 = sy.dma_start(out=modb[0][:, :], in_=sc_mod[4 * CH:5 * CH, :])
                c.inc("ld1", i1, 16)
                i1 = sy.dma_start(out=modb[1][:, :], in_=sc_mod[3 * CH:4 * CH, :])
                c.inc("ld1", i1, 16)

            m4_mods = c.mark("ld1")

            for cc in range(TSUB):
                ich = tt * TSUB + cc

                xm4 = xm4b[cc % 2]

                @blk.sync
                def _(sy, ich=ich, cc=cc, xm4=xm4):
                    c.wait(sy, "sy", m4_hbf.get(cc - 2, 0))
                    c.wait(sy, "sy", m4_sq.get(cc - 2, 0))
                    i1 = sy.dma_start(out=xm4[:, :],
                                      in_=sc_xmid[ich * CH:(ich + 1) * CH, :])
                    m4_xm[cc] = c.inc("xA" if cc % 2 == 0 else "xB", i1, 16)

                @blk.scalar
                def _(s, cc=cc, xm4=xm4):
                    c.wait(s, "s", m4_xm[cc])
                    i1 = s.activation(junk[:, :], xm4[:, :], AF.Square,
                                      accum_out=ssumsq[:, :])
                    m4_sq[cc] = c.inc("s", i1, 1)

                @blk.vector
                def _(v, cc=cc, xm4=xm4):
                    c.wait(v, "v", m4_xm[cc])
                    mk = c.inc("v", v.reduce_sum(smalls[:, 0:1], xm4[:, :], axis=AX.X,
                                                 negate=True), 1)
                    c.wait(v, "v", mk)
                    mk = c.inc("v", v.tensor_scalar(smalls[:, 1:2], smalls[:, 0:1],
                                                    1.0 / D, None, op0=ALU.mult), 1)
                    c.wait(v, "v", mk)
                    c.wait(v, "v", m4_sq[cc])
                    mk = c.inc("v", v.tensor_scalar(smalls[:, 2:3], ssumsq[:, :],
                                                    1.0 / D, None, op0=ALU.mult), 1)
                    c.wait(v, "v", mk)
                    mk = c.inc("v", v.tensor_mul(smalls[:, 3:4], smalls[:, 1:2],
                                                 smalls[:, 1:2]), 1)
                    c.wait(v, "v", mk)
                    i1 = v.tensor_sub(smalls[:, 4:5], smalls[:, 2:3], smalls[:, 3:4])
                    m4_var[cc] = c.inc("v", i1, 1)

                @blk.scalar
                def _(s, cc=cc):
                    c.wait(s, "s", m4_var[cc])
                    i1 = s.activation(smalls[:, 5:6], smalls[:, 4:5], AF.Sqrt, bias=eps_sb[:, 0:1])
                    m4_rstds[cc] = c.inc("s", i1, 1)

                @blk.vector
                def _(v, cc=cc, xm4=xm4):
                    c.wait(v, "v", m4_rstds[cc])
                    v.tensor_scalar(xm4[:, :], xm4[:, :], smalls[:, 1:2], None,
                                    op0=ALU.add)
                    i1 = v.reciprocal(smalls[:, 6:7], smalls[:, 5:6])
                    mr = c.inc("v", i1, 1)
                    c.wait(v, "v", mr)
                    c.wait(v, "v", m4_mods)
                    mk = c.inc("v", v.scalar_tensor_tensor(xm4[:, :], modb[0][:, :],
                                                           smalls[:, 6:7], xm4[:, :],
                                                           op0=ALU.mult, op1=ALU.mult), 1)
                    c.wait(v, "v", mk)
                    c.wait(v, "v", m4_tph.get(cc - 1, 0))
                    i1 = v.tensor_add(h2bf[:, :], xm4[:, :], modb[1][:, :])
                    m4_hbf[cc] = c.inc("v", i1, 1)

                @blk.tensor
                def _(t, cc=cc):
                    c.wait(t, "t", m4_hbf[cc])
                    c.wait(t, "t", m4_he.get(cc - 1, 0))
                    for dk in range(DK):
                        i1 = t.transpose(pt4[:, dk * 128:(dk + 1) * 128],
                                         h2bf[:, dk * 128:(dk + 1) * 128], ident[:, :])
                    m4_tph[cc] = c.inc("t", i1, 1)

                @blk.scalar
                def _(s, cc=cc):
                    c.wait(s, "s", m4_tph[cc])
                    for dk in range(DK):
                        i1 = s.copy(h2T[:, dk * TILE + cc * 128:dk * TILE + (cc + 1) * 128],
                                    pt4[:, dk * 128:(dk + 1) * 128])
                    m4_he[cc] = c.inc("s", i1, 1)

            m4_hall = c.mark("s")

            @blk.tensor
            def _(t):
                c.wait(t, "t", m4_hall)
                c.wait(t, "t", m4_gw)
                for th in range(2):
                    for dk in range(DK):
                        i1 = t.matmul(pup[0:8, th * 512:(th + 1) * 512],
                                      gwT_sb[:, dk * 8:(dk + 1) * 8],
                                      h2T[:, dk * TILE + th * 512:dk * TILE + (th + 1) * 512],
                                      start=(dk == 0), stop=(dk == DK - 1))
                c.inc("t", i1, 1)

            m4_rmm = c.mark("t")

            @blk.scalar
            def _(s):
                c.wait(s, "s", m4_rmm)
                i1 = s.copy(lgT[:, :], pup[0:8, 0:1024])
                c.inc("s", i1, 1)

            m4_lge = c.mark("s")

            @blk.tensor
            def _(t):
                c.wait(t, "t", m4_lge)
                for sub in range(TSUB):
                    i1 = t.transpose(pt4[:, sub * 8:(sub + 1) * 8],
                                     lgT[:, sub * 128:(sub + 1) * 128],
                                     ident[0:8, 0:8])
                c.inc("t", i1, 1)

            m4_ltp = c.mark("t")

            @blk.vector
            def _(v):
                c.wait(v, "v", m4_ltp)
                r3 = lambda tns: tns[:, :].rearrange("p (s e) -> p s e", e=8)

                def bc(tns):
                    return tns[:, :].unsqueeze(2).broadcast_to([128, TSUB, 8])

                def sw(i1):
                    mk = c.inc("v", i1, 1)
                    c.wait(v, "v", mk)

                sw(v.tensor_copy(lgt[:, :], pt4[:, 0:TSUB * 8]))
                sw(v.tensor_reduce(rts[:, :], r3(lgt), axis=AX.X, op=ALU.max))
                sw(v.tensor_tensor(r3(rt0), r3(lgt), bc(rts), op=ALU.subtract))

            @blk.scalar
            def _(s):
                c.wait(s, "s", c.mark("v"))
                i1 = s.activation(rt1[:, :], rt0[:, :], AF.Exp)
                c.inc("s", i1, 1)

            m4_exp = c.mark("s")

            @blk.vector
            def _(v):
                c.wait(v, "v", m4_exp)
                r3 = lambda tns: tns[:, :].rearrange("p (s e) -> p s e", e=8)

                def bc(tns):
                    return tns[:, :].unsqueeze(2).broadcast_to([128, TSUB, 8])

                def sw(i1):
                    mk = c.inc("v", i1, 1)
                    c.wait(v, "v", mk)

                sw(v.tensor_reduce(rts[:, :], r3(rt1), axis=AX.X, op=ALU.add))
                sw(v.reciprocal(rts[:, :], rts[:, :]))
                sw(v.tensor_tensor(r3(rt1), r3(rt1), bc(rts), op=ALU.mult))
                sw(v.tensor_reduce(rts[:, :], r3(rt1), axis=AX.X, op=ALU.max))
                sw(v.tensor_tensor(r3(rt2), r3(rt1), bc(rts), op=ALU.subtract))
                sw(v.tensor_scalar(rt2[:, :], rt2[:, :], 0.0, None, op0=ALU.is_ge))
                sw(v.tensor_mul(rt0[:, :], rt1[:, :], rt2[:, :]))
                sw(v.tensor_sub(rt0[:, :], rt1[:, :], rt0[:, :]))
                sw(v.tensor_reduce(rts[:, :], r3(rt0), axis=AX.X, op=ALU.max))
                sw(v.tensor_tensor(r3(rt0), r3(rt0), bc(rts), op=ALU.subtract))
                sw(v.tensor_scalar(rt0[:, :], rt0[:, :], 0.0, None, op0=ALU.is_ge))
                sw(v.tensor_add(rt2[:, :], rt2[:, :], rt0[:, :]))
                sw(v.tensor_mul(rt2[:, :], rt1[:, :], rt2[:, :]))
                sw(v.tensor_reduce(rts[:, :], r3(rt2), axis=AX.X, op=ALU.add))
                sw(v.tensor_scalar(rts[:, :], rts[:, :], 1e-20, None, op0=ALU.add))
                sw(v.reciprocal(rts[:, :], rts[:, :]))
                i1 = v.tensor_tensor(r3(comb), r3(rt2), bc(rts), op=ALU.mult)
                c.inc("v", i1, 1)

            m4_comb = c.mark("v")
            pscope.close()
            pdn = t4.enter_context(nc.psum_tensor("pdn_%d" % tt, [128, D], F32))

            m4_wup, m4_wdn, m4_y, m4_z, m4_dn, m4_ae = {}, {}, {}, {}, {}, {}
            up_items = [(e10, ic, mat) for e10 in range(NE10)
                        for ic in range(ICN) for mat in range(2)]
            up_ptr = [0]

            def emit_up_loads(upto_item, m4_wup=m4_wup, m4_y=m4_y,
                              up_items=up_items, up_ptr=up_ptr):
                while up_ptr[0] <= upto_item and up_ptr[0] < len(up_items):
                    e10, ic, mat = up_items[up_ptr[0]]
                    kidx = up_ptr[0]

                    @blk.gpsimd
                    def _(g, e10=e10, ic=ic, mat=mat, kidx=kidx):
                        if kidx >= 2:
                            pe, pc, pm = up_items[kidx - 2]
                            c.wait(g, "gp", m4_y.get((pe, pc, pm), 0))
                        i1 = g.dma_start(out=_b3(wup[mat][:, :], 128), in_=up_src(e10, mat, ic))
                        m4_wup[(e10, ic, mat)] = c.inc("upA" if mat == 0 else "upB", i1, 16)
                    up_ptr[0] += 1

            for e10 in range(NE10):
                @blk.gpsimd
                def _(g, e10=e10):
                    c.wait(g, "gp", m4_dn.get((e10 - 1, 2, TSUB - 1), 0))
                    i1 = g.dma_start(out=_b3(wdn[:, 0:DSW[0] * D], D),
                                     in_=dn_src(e10, 0))
                    m4_wdn[(e10, 0)] = c.inc("dnw", i1, 16)

                for ic in range(ICN):
                    emit_up_loads(e10 * ICN * 2 + ic * 2 + 1)
                    for mat in range(2):
                        @blk.tensor
                        def _(t, e10=e10, ic=ic, mat=mat):
                            c.wait(t, "t", m4_wup[(e10, ic, mat)])
                            c.wait(t, "t", m4_lge)
                            guard = m4_z.get((e10, ic - 1),
                                             m4_z.get((e10 - 1, ICN - 1), 0))
                            c.wait(t, "t", guard)
                            c.wait(t, "t", m4_ae.get((e10 - 1, 2, TSUB - 1), 0))
                            for dk in range(DK):
                                for th in range(2):
                                    i1 = t.matmul(
                                        pup[:, mat * 1024 + th * 512:
                                            mat * 1024 + (th + 1) * 512],
                                        wup[mat][:, dk * 128:(dk + 1) * 128],
                                        h2T[:, dk * TILE + th * 512:
                                            dk * TILE + (th + 1) * 512],
                                        start=(dk == 0), stop=(dk == DK - 1))
                            m4_y[(e10, ic, mat)] = c.inc("t", i1, 1)

                    @blk.scalar
                    def _(s, e10=e10, ic=ic):
                        c.wait(s, "s", m4_y[(e10, ic, 0)])
                        c.wait(s, "s", m4_z.get((e10, ic - 1),
                                                     m4_z.get((e10 - 1, ICN - 1), 0)))
                        i1 = s.activation(silt[:, :], pup[:, 0:1024], AF.Silu)
                        m4_z[(e10, ic, "s")] = c.inc("s", i1, 1)

                    @blk.vector
                    def _(v, e10=e10, ic=ic):
                        c.wait(v, "v", m4_z[(e10, ic, "s")])
                        c.wait(v, "v", m4_y[(e10, ic, 1)])
                        c.wait(v, "v", m4_dn.get((e10 - 1, 2, TSUB - 1), 0))
                        i1 = v.tensor_mul(zT[:, ic * TILE:(ic + 1) * TILE],
                                          silt[:, :], pup[:, 1024:2048])
                        m4_z[(e10, ic)] = c.inc("v", i1, 1)

                for sw_i in range(3):
                    nic = DSW[sw_i]
                    ic0_base = sum(DSW[:sw_i])
                    if sw_i >= 1:
                        @blk.gpsimd
                        def _(g, e10=e10, sw_i=sw_i, nic=nic):
                            c.wait(g, "gp", m4_dn[(e10, sw_i - 1, TSUB - 1)])
                            i1 = g.dma_start(out=_b3(wdn[:, 0:nic * D], D),
                                             in_=dn_src(e10, sw_i))
                            m4_wdn[(e10, sw_i)] = c.inc("dnw", i1, 16)

                    for sub in range(TSUB):
                        pbank = pdn if sub % 2 == 0 else pup

                        @blk.tensor
                        def _(t, e10=e10, sw_i=sw_i, sub=sub, nic=nic,
                              ic0_base=ic0_base, pbank=pbank):
                            c.wait(t, "t", m4_wdn[(e10, sw_i)])
                            for ic0 in range(nic):
                                c.wait(t, "t", m4_z[(e10, ic0_base + ic0)])
                            prev = m4_ae.get((e10, sw_i, sub - 2),
                                             m4_ae.get((e10, sw_i - 1, TSUB - 2 + sub),
                                                       m4_ae.get((e10 - 1, 2, TSUB - 2 + sub), 0)))
                            c.wait(t, "t", prev)
                            for ic0 in range(nic):
                                ic = ic0_base + ic0
                                for eb in range(4):
                                    i1 = t.matmul(
                                        pbank[:, eb * 512:(eb + 1) * 512],
                                        zT[:, ic * TILE + sub * 128:
                                           ic * TILE + (sub + 1) * 128],
                                        wdn[:, ic0 * D + eb * 512:ic0 * D + (eb + 1) * 512],
                                        start=(ic0 == 0), stop=(ic0 == nic - 1))
                            m4_dn[(e10, sw_i, sub)] = c.inc("t", i1, 1)

                        @blk.vector
                        def _(v, e10=e10, sw_i=sw_i, sub=sub, pbank=pbank):
                            c.wait(v, "v", m4_dn[(e10, sw_i, sub)])
                            c.wait(v, "v", m4_comb)
                            asl = acc[:, sub * D:(sub + 1) * D]
                            if e10 == 0 and sw_i == 0:
                                i1 = v.tensor_scalar(
                                    asl, pbank[:, :],
                                    comb[:, sub * 8 + e10:sub * 8 + e10 + 1],
                                    None, op0=ALU.mult)
                            else:
                                sc = (comb[:, sub * 8 + e10:sub * 8 + e10 + 1]
                                      if e10 < 8 else ones_f[:, 0:1])
                                i1 = v.scalar_tensor_tensor(asl, pbank[:, :], sc, asl,
                                                            op0=ALU.mult, op1=ALU.add)
                            m4_ae[(e10, sw_i, sub)] = c.inc("v", i1, 1)

            @blk.sync
            def _(sy, tt=tt):
                for sub in range(TSUB):
                    c.wait(sy, "sy", m4_ae[(NE10 - 1, 2, sub)])
                i1 = sy.dma_start(out=modb[1][:, :], in_=sc_mod[5 * CH:6 * CH, :])
                c.inc("ld1", i1, 16)

            m4_gb = c.mark("ld1")
            m4_ep, m4_yst = {}, {}

            for cc in range(TSUB):
                ich = tt * TSUB + cc

                xm4 = xm4b[cc % 2]

                @blk.sync
                def _(sy, ich=ich, cc=cc, xm4=xm4):
                    c.wait(sy, "sy", m4_ep.get(cc - 2, 0))
                    c.wait(sy, "sy", m4_yst.get(cc - 2, 0))
                    i1 = sy.dma_start(out=xm4[:, :],
                                      in_=sc_xmid[ich * CH:(ich + 1) * CH, :])
                    m4_xm[(cc, "ep")] = c.inc("xA" if cc % 2 == 0 else "xB", i1, 16)

                @blk.vector
                def _(v, ich=ich, cc=cc, xm4=xm4):
                    c.wait(v, "v", m4_xm[(cc, "ep")])
                    c.wait(v, "v", m4_gb)
                    for hf in range(2):
                        mk = c.inc("v", v.tensor_mul(
                            silt[:, :],
                            acc[:, cc * D + hf * TILE:cc * D + (hf + 1) * TILE],
                            modb[1][:, hf * TILE:(hf + 1) * TILE]), 1)
                        c.wait(v, "v", mk)
                        mk = c.inc("v", v.tensor_add(
                            xm4[:, hf * TILE:(hf + 1) * TILE], silt[:, :],
                            xm4[:, hf * TILE:(hf + 1) * TILE]), 1)
                        c.wait(v, "v", mk)
                    m4_ep[cc] = mk

                @blk.sync
                def _(sy, ich=ich, cc=cc, xm4=xm4):
                    c.wait(sy, "sy", m4_ep[cc])
                    i1 = sy.dma_start(out=y_out[ich * CH:(ich + 1) * CH, :],
                                      in_=xm4[:, :])
                    m4_yst[cc] = c.inc("st", i1, 16)

            t4.close()

        @blk.gpsimd
        def _(g):
            for sn in ("st", "v", "s", "t", "ld1"):
                c.wait(g, "gp", c.mark(sn))

        p4.close()

    return nc


def _prep_inputs(inputs):
    a = {k: np.asarray(v) for k, v in inputs.items()}
    w = {}
    w["x"] = np.ascontiguousarray(a["L0_attn_x"], dtype=np.float32)
    w["c"] = np.ascontiguousarray(a["L0_c"], dtype=np.float32)
    w["ld"] = np.ascontiguousarray(a["L0_attn_log_decay"], dtype=np.float32)
    w["wmod"] = np.ascontiguousarray(a["L0_wmod"], dtype=np.float32)
    w["wq"] = np.ascontiguousarray(a["L0_attn_Wq"], dtype=np.float32)
    w["wk"] = np.ascontiguousarray(a["L0_attn_Wk"], dtype=np.float32)
    w["wv"] = np.ascontiguousarray(a["L0_attn_Wv"], dtype=np.float32)
    w["wo"] = np.ascontiguousarray(a["L0_attn_Wo"], dtype=np.float32)
    w["wg1"] = np.ascontiguousarray(a["L0_attn_Wg1"], dtype=np.float32)
    w["wg2"] = np.ascontiguousarray(a["L0_attn_Wg2"], dtype=np.float32)
    w["normw"] = np.ascontiguousarray(a["L0_attn_norm_w"], dtype=np.float32)
    w["gwT"] = np.ascontiguousarray(a["L0_ffn_gate_w"].T, dtype=np.float32)
    w["sgT"] = np.ascontiguousarray(a["L0_ffn_shared_gate"].T, dtype=np.float32)
    w["suT"] = np.ascontiguousarray(a["L0_ffn_shared_up"].T, dtype=np.float32)
    w["sdT"] = np.ascontiguousarray(a["L0_ffn_shared_down"].T, dtype=np.float32)
    w["egT"] = np.ascontiguousarray(a["L0_ffn_expert_gate"].transpose(0, 2, 1),
                                    dtype=np.float32)
    w["euT"] = np.ascontiguousarray(a["L0_ffn_expert_up"].transpose(0, 2, 1),
                                    dtype=np.float32)
    w["edT"] = np.ascontiguousarray(a["L0_ffn_expert_down"].transpose(0, 2, 1),
                                    dtype=np.float32)
    return w


def _core_in_map(w, core, upto=4):
    b, j = core // 2, core % 2
    im = {
        "x_scan": w["x"][b],
        "c_row": w["c"][b],
        "ld": w["ld"],
        "wmod": w["wmod"],
        "wk": w["wk"],
        "wv": w["wv"],
        "own0": np.array([[OWN * j]], dtype=np.int32),
    }
    if upto >= 2:
        im["x_own"] = np.ascontiguousarray(w["x"][b, j * 2048:(j + 1) * 2048])
        im["wq"] = w["wq"]
        im["wg1"] = w["wg1"]
        im["wg2"] = w["wg2"]
        im["normw"] = w["normw"]
    if upto >= 3:
        im["wo"] = w["wo"]
    if upto >= 4:
        im["gwT"] = w["gwT"]
        im["sgT"] = w["sgT"]
        im["suT"] = w["suT"]
        im["sdT"] = w["sdT"]
        im["egT"] = w["egT"]
        im["euT"] = w["euT"]
        im["edT"] = w["edT"]
    return im


def kernel(**inputs):
    w = _prep_inputs(inputs)
    nc = build_program(upto=4)
    in_maps = [_core_in_map(w, core) for core in range(8)]
    res = run_bass_kernel_spmd(nc, in_maps, core_ids=list(range(8)))
    out = np.empty((4, 4096, 2048), dtype=np.float32)
    for core in range(8):
        b, j = core // 2, core % 2
        out[b, j * 2048:(j + 1) * 2048, :] = res.results[core]["y"]
    return out


# version 06 (final)
# speedup vs baseline: 1.2362x; 1.0325x over previous
import numpy as np
import concourse.bass as bass
import concourse.mybir as mybir
from concourse.bass_utils import run_bass_kernel_spmd
from contextlib import ExitStack

F32 = mybir.dt.float32
BF16 = mybir.dt.bfloat16
I32 = mybir.dt.int32
AF = mybir.ActivationFunctionType
ALU = mybir.AluOpType
AX = mybir.AxisListType

D = 2048
DK = 16
H = 16
HD = 128
CH = 128
NCH = 32
OWN = 16
EPS = 1e-5
ISH = 1408
ICN = 11
NE10 = 10
TSUB = 8
TILE = 1024
NTILE = 2
CHUNK_ELEMS = CH * D
DSW = (4, 4, 3)

ENGS = ("sy", "gp", "t", "v", "s",
        "xA", "xB", "st", "ld1", "wA", "wB", "wbig",
        "kvst", "Sst", "kvA", "kvB", "oTA", "oTB", "upA", "upB", "dnw")


def _w3(dram2d, p=128):
    return dram2d.rearrange("(k p) e -> p k e", p=p)


def _b3(buf2d, e):
    return buf2d.rearrange("p (k e) -> p k e", e=e)


class Ctx:
    def __init__(self, nc, stack):
        self.nc = nc
        self.h = {}
        self.cnt = {}
        self.lastw = {}
        for n in ENGS:
            self.h[n] = stack.enter_context(nc.semaphore("sem_" + n))
            self.cnt[n] = 0

    def inc(self, name, inst, k=1):
        inst.then_inc(self.h[name], k)
        self.cnt[name] += k
        return (name, self.cnt[name])

    def mark(self, name):
        return (name, self.cnt[name])

    def wait(self, eng, engname, tok):
        if not tok:
            return
        semname, val = tok
        if val <= 0:
            return
        key = (engname, semname)
        if self.lastw.get(key, -1) >= val:
            return
        eng.wait_ge(self.h[semname], val)
        self.lastw[key] = val

    def barrier_marks(self):
        return dict(self.cnt)


def _emit_barrier(blk, c, marks):
    for en, sect in (("sy", blk.sync), ("gp", blk.gpsimd), ("t", blk.tensor),
                     ("v", blk.vector), ("s", blk.scalar)):
        def f(eng, en=en):
            for sn, v in marks.items():
                if sn != en:
                    c.wait(eng, en, (sn, v))
        sect(f)


def build_program(upto=4, dbg=False, nch=NCH, nmb=48):
    nc = bass.Bass("TRN2", target_bir_lowering=False)

    x_scan = nc.dram_tensor("x_scan", [4096, D], F32, kind="ExternalInput").ap()
    c_row = nc.dram_tensor("c_row", [D], F32, kind="ExternalInput").ap()
    ld_in = nc.dram_tensor("ld", [H], F32, kind="ExternalInput").ap()
    wmod = nc.dram_tensor("wmod", [D, 6 * D], F32, kind="ExternalInput").ap()
    wk = nc.dram_tensor("wk", [D, D], F32, kind="ExternalInput").ap()
    wv = nc.dram_tensor("wv", [D, D], F32, kind="ExternalInput").ap()
    own0 = nc.dram_tensor("own0", [1, 1], I32, kind="ExternalInput").ap()
    if upto >= 2:
        x_own = nc.dram_tensor("x_own", [OWN * CH, D], F32, kind="ExternalInput").ap()
        wq = nc.dram_tensor("wq", [D, D], F32, kind="ExternalInput").ap()
        wg1 = nc.dram_tensor("wg1", [D, HD], F32, kind="ExternalInput").ap()
        wg2 = nc.dram_tensor("wg2", [HD, D], F32, kind="ExternalInput").ap()
        normw = nc.dram_tensor("normw", [D], F32, kind="ExternalInput").ap()
    if upto >= 3:
        wo = nc.dram_tensor("wo", [D, D], F32, kind="ExternalInput").ap()
    if upto >= 4:
        gwT = nc.dram_tensor("gwT", [D, 8], F32, kind="ExternalInput").ap()
        sgT = nc.dram_tensor("sgT", [D, 2 * ISH], F32, kind="ExternalInput").ap()
        suT = nc.dram_tensor("suT", [D, 2 * ISH], F32, kind="ExternalInput").ap()
        sdT = nc.dram_tensor("sdT", [2 * ISH, D], F32, kind="ExternalInput").ap()
        egT = nc.dram_tensor("egT", [8, D, ISH], F32, kind="ExternalInput").ap()
        euT = nc.dram_tensor("euT", [8, D, ISH], F32, kind="ExternalInput").ap()
        edT = nc.dram_tensor("edT", [8, ISH, D], F32, kind="ExternalInput").ap()
        y_out = nc.dram_tensor("y", [OWN * CH, D], F32, kind="ExternalOutput").ap()

    def scratch(name, shape, dt, out_at):
        kind = "ExternalOutput" if (dbg and upto == out_at) else "Internal"
        return nc.dram_tensor(name, shape, dt, kind=kind).ap()

    sc_kvs = scratch("sc_kvs", [NCH * 3 * CH, D], BF16, 1)
    sc_mod = scratch("sc_mod", [6 * CH, D], F32, 1)
    dbg_h = (nc.dram_tensor("dbg_h", [CH, D], BF16, kind="ExternalOutput").ap()
             if (dbg and upto == 1) else None)
    dbg_sm = (nc.dram_tensor("dbg_sm", [CH, 8], F32, kind="ExternalOutput").ap()
              if (dbg and upto == 1) else None)
    dbg_x = (nc.dram_tensor("dbg_x", [CH, D], F32, kind="ExternalOutput").ap()
             if (dbg and upto == 1) else None)
    dbg_ss = (nc.dram_tensor("dbg_ss", [CH, 1], F32, kind="ExternalOutput").ap()
              if (dbg and upto == 1) else None)
    sc_oT = scratch("sc_oT", [OWN * CH, D], BF16, 2) if upto >= 2 else None
    sc_xmid = scratch("sc_xmid", [OWN * CH, D], F32, 3) if upto >= 3 else None
    dbg_oTb = (nc.dram_tensor("dbg_oTb", [CH, D], BF16, kind="ExternalOutput").ap()
               if (dbg and upto == 3) else None)
    dbg_att = (nc.dram_tensor("dbg_att", [CH, D], F32, kind="ExternalOutput").ap()
               if (dbg and upto == 3) else None)

    stack = ExitStack()
    with stack:
        c = Ctx(nc, stack)

        ident = stack.enter_context(nc.sbuf_tensor("ident", [128, 128], BF16))
        ones_f = stack.enter_context(nc.sbuf_tensor("ones_f", [128, 128], F32))
        ones_1p = stack.enter_context(nc.sbuf_tensor("ones_1p", [1, 128], F32))
        ldb = stack.enter_context(nc.sbuf_tensor("ldb", [128, H], F32))
        qdec16 = stack.enter_context(nc.sbuf_tensor("qdec16", [128, H], F32))
        kdec16 = stack.enter_context(nc.sbuf_tensor("kdec16", [128, H], F32))
        cdec16 = stack.enter_context(nc.sbuf_tensor("cdec16", [128, H], F32))
        normw_fc = (stack.enter_context(nc.sbuf_tensor("normw_fc", [128, DK], F32))
                    if upto >= 2 else None)
        it_p1 = stack.enter_context(nc.sbuf_tensor("it_p1", [128, 1], F32))
        it_km = stack.enter_context(nc.sbuf_tensor("it_km", [128, 1], F32))
        c128 = stack.enter_context(nc.sbuf_tensor("c128", [128, 1], F32))
        eps_sb = stack.enter_context(nc.sbuf_tensor("eps_sb", [128, 1], F32))
        ld_sb = stack.enter_context(nc.sbuf_tensor("ld_sb", [1, H], F32))
        own0_sb = stack.enter_context(nc.sbuf_tensor("own0_sb", [1, 1], I32))
        smalls = stack.enter_context(nc.sbuf_tensor("smalls", [128, 8], F32))
        ssumsq = stack.enter_context(nc.sbuf_tensor("ssumsq", [128, 1], F32))
        junk = stack.enter_context(nc.sbuf_tensor("junk", [128, D], BF16))
        sa_rep = stack.enter_context(nc.sbuf_tensor("sa_rep", [128, D], F32))
        ca1_rep = stack.enter_context(nc.sbuf_tensor("ca1_rep", [128, D], F32))

        blk = stack.enter_context(nc.Block())

        p1 = ExitStack()
        wk_sb = p1.enter_context(nc.sbuf_tensor("wk_sb", [128, DK * D], BF16))
        wv_sb = p1.enter_context(nc.sbuf_tensor("wv_sb", [128, DK * D], BF16))

        p0 = ExitStack()
        crep = p0.enter_context(nc.sbuf_tensor("crep", [128, D], BF16))
        c_fc = p0.enter_context(nc.sbuf_tensor("c_fc", [128, DK], F32))
        csil = p0.enter_context(nc.sbuf_tensor("csil", [128, DK], F32))
        wmbuf = [p0.enter_context(nc.sbuf_tensor(f"wmbuf{i}", [128, DK * 256], BF16))
                 for i in range(2)]
        modv = p0.enter_context(nc.sbuf_tensor("modv", [128, D], F32))
        modt = [p0.enter_context(nc.sbuf_tensor(f"modt{i}", [128, D], F32))
                for i in range(2)]
        ps_m = p0.enter_context(nc.psum_tensor("ps_m", [128, 1024], F32))
        ps_sm = p0.enter_context(nc.psum_tensor("ps_sm", [128, 16], F32))

        @blk.sync
        def _(sy):
            i1 = sy.dma_start(out=ld_sb[:, :], in_=ld_in[None, :])
            c.inc("ld1", i1, 16)
            i1 = sy.dma_start(out=own0_sb[:, :], in_=own0)
            c.inc("ld1", i1, 16)
            i1 = sy.dma_start(out=c_fc[:, :], in_=c_row.rearrange("(k p) -> p k", p=128),
                              allow_slow_non_contiguous=True)
            c.inc("ld1", i1, 16)
            if upto >= 2:
                i1 = sy.dma_start(out=normw_fc[:, :],
                                  in_=normw.rearrange("(k p) -> p k", p=128),
                                  allow_slow_non_contiguous=True)
                c.inc("ld1", i1, 16)

        m_sy0 = c.mark("ld1")

        @blk.gpsimd
        def _(g):
            mk = c.inc("gp", g.memset(ident[:, :], 0.0), 1)
            g.memset(ones_f[:, :], 1.0)
            c.wait(g, "gp", mk)
            g.affine_select(out=ident[:, :], in_=ident[:, :],
                            compare_op=ALU.not_equal, fill=1.0,
                            base=0, pattern=[[-1, 128]], channel_multiplier=1)
            g.memset(ones_1p[:, :], 1.0)
            g.memset(c128[:, :], 128.0)
            g.memset(eps_sb[:, :], EPS)
            g.iota(it_p1[:, :], pattern=[[0, 1]], base=1, channel_multiplier=1,
                   allow_small_or_imprecise_dtypes=True)
            i1 = g.iota(it_km[:, :], pattern=[[0, 1]], base=127, channel_multiplier=-1,
                        allow_small_or_imprecise_dtypes=True)
            c.inc("gp", i1, 1)

        m_gp_const = c.mark("gp")

        @blk.tensor
        def _(t):
            c.wait(t, "t", m_sy0)
            c.wait(t, "t", m_gp_const)
            i1 = t.matmul(ps_sm[:, :], ones_1p[:, :], ld_sb[:, :], start=True, stop=True)
            c.inc("t", i1, 1)

        m_t_ldb = c.mark("t")

        @blk.vector
        def _(v):
            c.wait(v, "v", m_t_ldb)
            i1 = v.tensor_copy(ldb[:, :], ps_sm[:, :])
            c.inc("v", i1, 1)

        m_v_ldb = c.mark("v")

        @blk.scalar
        def _(s):
            c.wait(s, "s", m_v_ldb)
            for h in range(H):
                s.activation(qdec16[:, h:h + 1], it_p1[:, :], AF.Exp,
                             scale=ldb[:, h:h + 1])
                s.activation(kdec16[:, h:h + 1], it_km[:, :], AF.Exp,
                             scale=ldb[:, h:h + 1])
                s.activation(cdec16[:, h:h + 1], c128[:, :], AF.Exp,
                             scale=ldb[:, h:h + 1])
            i1 = s.activation(csil[:, :], c_fc[:, :], AF.Silu)
            c.inc("s", i1, 1)

        m_s_dec = c.mark("s")

        @blk.vector
        def _(v):
            c.wait(v, "v", m_s_dec)
            for ch in range(DK):
                i1 = v.tensor_scalar(crep[:, ch * 128:(ch + 1) * 128], ones_f[:, :],
                                     csil[:, ch:ch + 1], None, op0=ALU.mult)
            c.inc("v", i1, 1)

        m_v_crep = c.mark("v")

        NMB = nmb
        m_wm, m_mm_mod, m_modev, m_modst = {}, {}, {}, {}

        for nb in range(NMB):
            vec, q = nb // 8, nb % 8

            @blk.gpsimd
            def _(g, nb=nb):
                if nb >= 2:
                    c.wait(g, "gp", m_mm_mod[nb - 2])
                i1 = g.dma_start(
                    out=_b3(wmbuf[nb % 2][:, :], 256),
                    in_=_w3(wmod[:, nb * 256:(nb + 1) * 256]))
                m_wm[nb] = c.inc("wA" if nb % 2 == 0 else "wB", i1, 16)

            @blk.tensor
            def _(t, nb=nb):
                c.wait(t, "t", m_v_crep)
                c.wait(t, "t", m_wm[nb])
                if nb >= 2:
                    c.wait(t, "t", m_modev[nb - 2])
                for dk in range(DK):
                    i1 = t.matmul(ps_m[:, (nb % 2) * 512:(nb % 2) * 512 + 256],
                                  crep[:, dk * 128:(dk + 1) * 128],
                                  wmbuf[nb % 2][:, dk * 256:(dk + 1) * 256],
                                  start=(dk == 0), stop=(dk == DK - 1))
                m_mm_mod[nb] = c.inc("t", i1, 1)

            @blk.vector
            def _(v, nb=nb, vec=vec, q=q):
                c.wait(v, "v", m_mm_mod[nb])
                if q == 0 and nb >= 8:
                    c.wait(v, "v", m_modst[nb - 1])
                src = ps_m[:, (nb % 2) * 512:(nb % 2) * 512 + 256]
                dst = modv[:, q * 256:(q + 1) * 256]
                if vec in (1, 4):
                    i1 = v.tensor_scalar(dst, src, 1.0, None, op0=ALU.add)
                else:
                    i1 = v.tensor_copy(dst, src)
                m_modev[nb] = c.inc("v", i1, 1)
                if q == 7 and vec in (2, 5):
                    c.wait(v, "v", m_modev[nb])
                    v.tensor_scalar(modt[0][:, :], modv[:, :], 0.25, None, op0=ALU.max)
                    mk = c.inc("v", v.tensor_scalar(modt[1][:, :], modv[:, :], -0.25,
                                                    None, op0=ALU.min), 1)
                    c.wait(v, "v", mk)
                    v.tensor_scalar(modv[:, :], modv[:, :], 0.0, None, op0=ALU.is_ge)
                    mk = c.inc("v", v.tensor_sub(modt[0][:, :], modt[0][:, :],
                                                 modt[1][:, :]), 1)
                    c.wait(v, "v", mk)
                    mk = c.inc("v", v.tensor_mul(modv[:, :], modv[:, :],
                                                 modt[0][:, :]), 1)
                    c.wait(v, "v", mk)
                    i1 = v.tensor_add(modv[:, :], modv[:, :], modt[1][:, :])
                    m_modev[nb] = c.inc("v", i1, 1)

            if q == 7:
                @blk.sync
                def _(sy, nb=nb, vec=vec):
                    c.wait(sy, "sy", m_modev[nb])
                    i1 = sy.dma_start(out=sc_mod[vec * CH:(vec + 1) * CH, :],
                                      in_=modv[:, :])
                    m_modst[nb] = c.inc("st", i1, 16)

        @blk.gpsimd
        def _(g):
            i1 = g.dma_start(out=_b3(wk_sb[:, :], D), in_=_w3(wk))
            c.inc("wbig", i1, 16)
            i1 = g.dma_start(out=_b3(wv_sb[:, :], D), in_=_w3(wv))
            c.inc("wbig", i1, 16)

        m_gp_wkv = c.mark("wbig")

        @blk.sync
        def _(sy):
            c.wait(sy, "sy", c.mark("st"))
            i1 = sy.dma_start(out=sa_rep[:, :], in_=sc_mod[0:CH, :])
            c.inc("ld1", i1, 16)
            i1 = sy.dma_start(out=ca1_rep[:, :], in_=sc_mod[CH:2 * CH, :])
            c.inc("ld1", i1, 16)

        p0.close()

        bar0 = c.barrier_marks()
        _emit_barrier(blk, c, bar0)

        xb = [p1.enter_context(nc.sbuf_tensor(f"xb{i}", [128, D], F32)) for i in range(2)]
        h_bf = p1.enter_context(nc.sbuf_tensor("h_bf", [128, D], BF16))
        hT = p1.enter_context(nc.sbuf_tensor("hT", [128, D], BF16))
        k_tok = p1.enter_context(nc.sbuf_tensor("k_tok", [128, D], BF16))
        v_tok = p1.enter_context(nc.sbuf_tensor("v_tok", [128, D], BF16))
        kT_sb = p1.enter_context(nc.sbuf_tensor("kT_sb", [128, D], BF16))
        S_sb = p1.enter_context(nc.sbuf_tensor("S_sb", [128, D], F32))
        stmp = p1.enter_context(nc.sbuf_tensor("stmp", [128, 128], F32))
        pt1 = p1.enter_context(nc.psum_tensor("pt1", [128, D], BF16))
        pkv = p1.enter_context(nc.psum_tensor("pkv", [128, D], F32))
        pS = p1.enter_context(nc.psum_tensor("pS", [128, 1024], F32))

        m_x, m_sq, m_var, m_rstds, m_hbf = {}, {}, {}, {}, {}
        m_tph, m_hTe, m_kv, m_kve, m_kdec = {}, {}, {}, {}, {}
        m_tpk, m_kTe, m_kTst, m_vst = {}, {}, {}, {}
        m_sst, m_dS, m_supd = {}, {}, {}

        @blk.vector
        def _(v):
            i1 = v.memset(S_sb[:, :], 0.0)
            c.inc("v", i1, 1)

        m_v_sinit = c.mark("v")

        def p1_store(g_ch):
            @blk.gpsimd
            def _(g, g_ch=g_ch):
                if g_ch >= 1:
                    c.wait(g, "gp", m_supd[(g_ch - 1, 3)])
                else:
                    c.wait(g, "gp", m_v_sinit)
                i1 = g.dma_start(out=sc_kvs[(g_ch * 3 + 2) * CH:(g_ch * 3 + 3) * CH, :],
                                 in_=S_sb[:, :])
                m_sst[g_ch] = c.inc("Sst", i1, 16)

        def p1_xload(g_ch):
            par = g_ch % 2

            @blk.sync
            def _(sy, g_ch=g_ch, par=par):
                if g_ch >= 2:
                    c.wait(sy, "sy", m_hbf[g_ch - 2])
                    c.wait(sy, "sy", m_sq[g_ch - 2])
                i1 = sy.dma_start(out=xb[par][:, :],
                                  in_=x_scan[g_ch * CH:(g_ch + 1) * CH, :])
                m_x[g_ch] = c.inc("xA" if par == 0 else "xB", i1, 16)

        def p1_ln(g_ch):
            par = g_ch % 2

            @blk.scalar
            def _(s, g_ch=g_ch, par=par):
                c.wait(s, "s", m_x[g_ch])
                i1 = s.activation(junk[:, :], xb[par][:, :], AF.Square,
                                  accum_out=ssumsq[:, :])
                m_sq[g_ch] = c.inc("s", i1, 1)

            @blk.vector
            def _(v, g_ch=g_ch, par=par):
                c.wait(v, "v", m_x[g_ch])
                mk = c.inc("v", v.reduce_sum(smalls[:, 0:1], xb[par][:, :], axis=AX.X,
                                             negate=True), 1)
                c.wait(v, "v", mk)
                mk = c.inc("v", v.tensor_scalar(smalls[:, 1:2], smalls[:, 0:1],
                                                1.0 / D, None, op0=ALU.mult), 1)
                c.wait(v, "v", mk)
                c.wait(v, "v", m_sq[g_ch])
                mk = c.inc("v", v.tensor_scalar(smalls[:, 2:3], ssumsq[:, :],
                                                1.0 / D, None, op0=ALU.mult), 1)
                c.wait(v, "v", mk)
                mk = c.inc("v", v.tensor_mul(smalls[:, 3:4], smalls[:, 1:2],
                                             smalls[:, 1:2]), 1)
                c.wait(v, "v", mk)
                i1 = v.tensor_sub(smalls[:, 4:5], smalls[:, 2:3], smalls[:, 3:4])
                m_var[g_ch] = c.inc("v", i1, 1)

            @blk.scalar
            def _(s, g_ch=g_ch):
                c.wait(s, "s", m_var[g_ch])
                i1 = s.activation(smalls[:, 5:6], smalls[:, 4:5], AF.Sqrt,
                                  bias=eps_sb[:, 0:1])
                m_rstds[g_ch] = c.inc("s", i1, 1)

            @blk.vector
            def _(v, g_ch=g_ch, par=par):
                c.wait(v, "v", m_rstds[g_ch])
                v.tensor_scalar(xb[par][:, :], xb[par][:, :], smalls[:, 1:2], None,
                                op0=ALU.add)
                i1 = v.reciprocal(smalls[:, 6:7], smalls[:, 5:6])
                mr = c.inc("v", i1, 1)
                c.wait(v, "v", mr)
                mk = c.inc("v", v.scalar_tensor_tensor(xb[par][:, :], ca1_rep[:, :],
                                                       smalls[:, 6:7], xb[par][:, :],
                                                       op0=ALU.mult, op1=ALU.mult), 1)
                c.wait(v, "v", mk)
                if g_ch >= 1:
                    c.wait(v, "v", m_tph[g_ch - 1])
                i1 = v.tensor_add(h_bf[:, :], xb[par][:, :], sa_rep[:, :])
                m_hbf[g_ch] = c.inc("v", i1, 1)

        def p1_rest(g_ch):
            par = g_ch % 2

            @blk.tensor
            def _(t, g_ch=g_ch):
                c.wait(t, "t", m_hbf[g_ch])
                if g_ch >= 1:
                    c.wait(t, "t", m_kTe[g_ch - 1])
                for dk in range(DK):
                    i1 = t.transpose(pt1[:, dk * 128:(dk + 1) * 128],
                                     h_bf[:, dk * 128:(dk + 1) * 128], ident[:, :])
                m_tph[g_ch] = c.inc("t", i1, 1)

            @blk.vector
            def _(v, g_ch=g_ch):
                c.wait(v, "v", m_tph[g_ch])
                if g_ch >= 1:
                    c.wait(v, "v", m_kv[(g_ch - 1, 3)])
                for eb in range(4):
                    i1 = v.tensor_copy(hT[:, eb * 512:(eb + 1) * 512],
                                       pt1[:, eb * 512:(eb + 1) * 512])
                m_hTe[g_ch] = c.inc("v", i1, 1)

            if g_ch + 1 < nch:
                p1_xload(g_ch + 1)
                p1_ln(g_ch + 1)

            for eb in range(4):
                pp = eb % 2

                @blk.tensor
                def _(t, g_ch=g_ch, eb=eb, pp=pp):
                    c.wait(t, "t", m_hTe[g_ch])
                    c.wait(t, "t", m_gp_wkv)
                    guard = m_kve.get((g_ch, eb - 2), m_kve.get((g_ch - 1, eb + 2), 0))
                    c.wait(t, "t", guard)
                    for dk in range(DK):
                        t.matmul(pkv[:, pp * 1024:pp * 1024 + 512],
                                 hT[:, dk * 128:(dk + 1) * 128],
                                 wk_sb[:, dk * D + eb * 512:dk * D + (eb + 1) * 512],
                                 start=(dk == 0), stop=(dk == DK - 1))
                        i1 = t.matmul(pkv[:, pp * 1024 + 512:pp * 1024 + 1024],
                                      hT[:, dk * 128:(dk + 1) * 128],
                                      wv_sb[:, dk * D + eb * 512:dk * D + (eb + 1) * 512],
                                      start=(dk == 0), stop=(dk == DK - 1))
                    m_kv[(g_ch, eb)] = c.inc("t", i1, 1)

                @blk.scalar
                def _(s, g_ch=g_ch, eb=eb, pp=pp):
                    c.wait(s, "s", m_kv[(g_ch, eb)])
                    if g_ch >= 1 and eb == 0:
                        c.wait(s, "s", m_dS[(g_ch - 1, 3)])
                        c.wait(s, "s", m_vst[g_ch - 1])
                        c.wait(s, "s", m_tpk[g_ch - 1])
                        c.wait(s, "s", m_kdec[g_ch - 1])
                    s.activation(k_tok[:, eb * 512:(eb + 1) * 512],
                                 pkv[:, pp * 1024:pp * 1024 + 512], AF.Silu)
                    i1 = s.activation(v_tok[:, eb * 512:(eb + 1) * 512],
                                      pkv[:, pp * 1024 + 512:pp * 1024 + 1024], AF.Silu)
                    m_kve[(g_ch, eb)] = c.inc("s", i1, 1)

            @blk.tensor
            def _(t, g_ch=g_ch):
                c.wait(t, "t", m_kve[(g_ch, 3)])
                c.wait(t, "t", m_hTe[g_ch])
                for h in range(H):
                    i1 = t.transpose(pt1[:, h * 128:(h + 1) * 128],
                                     k_tok[:, h * 128:(h + 1) * 128], ident[:, :])
                m_tpk[g_ch] = c.inc("t", i1, 1)

            @blk.vector
            def _(v, g_ch=g_ch):
                c.wait(v, "v", m_tpk[g_ch])
                if g_ch >= 1:
                    c.wait(v, "v", m_vst[g_ch - 1])
                for eb in range(4):
                    i1 = v.tensor_copy(kT_sb[:, eb * 512:(eb + 1) * 512],
                                       pt1[:, eb * 512:(eb + 1) * 512])
                m_kTe[g_ch] = c.inc("v", i1, 1)

            @blk.vector
            def _(v, g_ch=g_ch):
                c.wait(v, "v", m_tpk[g_ch])
                if g_ch >= 1:
                    c.wait(v, "v", m_dS[(g_ch - 1, 3)])
                for h in range(H):
                    i1 = v.tensor_scalar(k_tok[:, h * 128:(h + 1) * 128],
                                         k_tok[:, h * 128:(h + 1) * 128],
                                         kdec16[:, h:h + 1], None, op0=ALU.mult)
                m_kdec[g_ch] = c.inc("v", i1, 1)

            @blk.gpsimd
            def _(g, g_ch=g_ch):
                c.wait(g, "gp", m_kTe[g_ch])
                i1 = g.dma_start(out=sc_kvs[(g_ch * 3 + 0) * CH:(g_ch * 3 + 1) * CH, :],
                                 in_=kT_sb[:, :])
                m_kTst[g_ch] = c.inc("kvst", i1, 16)
                c.wait(g, "gp", m_kve[(g_ch, 3)])
                i1 = g.dma_start(out=sc_kvs[(g_ch * 3 + 1) * CH:(g_ch * 3 + 2) * CH, :],
                                 in_=v_tok[:, :])
                m_vst[g_ch] = c.inc("kvst", i1, 16)

            for hg in range(4):
                @blk.tensor
                def _(t, g_ch=g_ch, hg=hg):
                    c.wait(t, "t", m_kdec[g_ch])
                    guard = m_supd.get((g_ch, hg - 2), m_supd.get((g_ch - 1, hg + 2), 0))
                    c.wait(t, "t", guard)
                    for h in range(hg * 4, hg * 4 + 4):
                        i1 = t.matmul(pS[:, (hg % 2) * 512 + (h % 4) * 128:
                                         (hg % 2) * 512 + (h % 4 + 1) * 128],
                                      k_tok[:, h * 128:(h + 1) * 128],
                                      v_tok[:, h * 128:(h + 1) * 128],
                                      start=True, stop=True)
                    m_dS[(g_ch, hg)] = c.inc("t", i1, 1)

                @blk.vector
                def _(v, g_ch=g_ch, hg=hg):
                    c.wait(v, "v", m_dS[(g_ch, hg)])
                    c.wait(v, "v", m_sst[g_ch])
                    for h in range(hg * 4, hg * 4 + 4):
                        i1 = v.scalar_tensor_tensor(
                            S_sb[:, h * 128:(h + 1) * 128],
                            S_sb[:, h * 128:(h + 1) * 128],
                            cdec16[:, h:h + 1],
                            pS[:, (hg % 2) * 512 + (h % 4) * 128:
                               (hg % 2) * 512 + (h % 4 + 1) * 128],
                            op0=ALU.mult, op1=ALU.add)
                    m_supd[(g_ch, hg)] = c.inc("v", i1, 1)

        p1_xload(0)
        p1_ln(0)
        for g_ch in range(nch):
            p1_store(g_ch)
            p1_rest(g_ch)

        if upto == 1:
            @blk.gpsimd
            def _(g):
                for sn in ("st", "Sst", "kvst", "v", "s", "t"):
                    c.wait(g, "gp", c.mark(sn))
            p1.close()
            return nc
        p1.close()

        bar1 = c.barrier_marks()
        _emit_barrier(blk, c, bar1)

        p2 = ExitStack()
        wq_sb = p2.enter_context(nc.sbuf_tensor("wq_sb", [128, DK * D], BF16))
        wg1_sb = p2.enter_context(nc.sbuf_tensor("wg1_sb", [128, DK * 128], BF16))
        wg2_sb = p2.enter_context(nc.sbuf_tensor("wg2_sb", [128, D], BF16))
        Dt = p2.enter_context(nc.sbuf_tensor("Dt", [128, H * 128], F32))
        difT = p2.enter_context(nc.sbuf_tensor("difT", [128, 128], F32))
        xb2 = [p2.enter_context(nc.sbuf_tensor(f"x2_{i}", [128, D], F32)) for i in range(2)]
        kvs = [p2.enter_context(nc.sbuf_tensor(f"kvs{i}", [128, 3 * D], BF16))
               for i in range(2)]
        kTb = [kvs[i][:, 0:D] for i in range(2)]
        vbb = [kvs[i][:, D:2 * D] for i in range(2)]
        Sbb = [kvs[i][:, 2 * D:3 * D] for i in range(2)]
        h_bf2 = p2.enter_context(nc.sbuf_tensor("h_bf2", [128, D], BF16))
        hT2 = p2.enter_context(nc.sbuf_tensor("hT2", [128, D], BF16))
        q_tok = p2.enter_context(nc.sbuf_tensor("q_tok", [128, D], BF16))
        q_dec = p2.enter_context(nc.sbuf_tensor("q_dec", [128, D], BF16))
        qT = p2.enter_context(nc.sbuf_tensor("qT", [128, D], BF16))
        qTd = p2.enter_context(nc.sbuf_tensor("qTd", [128, D], BF16))
        g1T = p2.enter_context(nc.sbuf_tensor("g1T", [128, 128], BF16))
        sig = p2.enter_context(nc.sbuf_tensor("sig", [128, D], F32))
        scm = p2.enter_context(nc.sbuf_tensor("scm", [128, D], BF16))
        o_sb = p2.enter_context(nc.sbuf_tensor("o_sb", [128, D], F32))
        tmpf = p2.enter_context(nc.sbuf_tensor("tmpf", [128, D], F32))
        orms = p2.enter_context(nc.sbuf_tensor("orms", [128, D], BF16))
        oT_sb = p2.enter_context(nc.sbuf_tensor("oT_sb", [128, D], BF16))
        ms16 = p2.enter_context(nc.sbuf_tensor("ms16", [128, H], F32))
        rmss = p2.enter_context(nc.sbuf_tensor("rmss", [128, H], F32))
        rrr = p2.enter_context(nc.sbuf_tensor("rrr", [128, H], F32))
        pt2 = p2.enter_context(nc.psum_tensor("pt2", [128, D], BF16))
        pq = p2.enter_context(nc.psum_tensor("pq", [128, D], F32))
        psc = p2.enter_context(nc.psum_tensor("psc", [128, 1024], F32))

        @blk.gpsimd
        def _(g):
            i1 = g.dma_start(out=_b3(wq_sb[:, :], D), in_=_w3(wq))
            c.inc("wbig", i1, 16)
            i1 = g.dma_start(out=_b3(wg1_sb[:, :], 128), in_=_w3(wg1))
            c.inc("wbig", i1, 16)
            i1 = g.dma_start(out=wg2_sb[:, :], in_=wg2)
            c.inc("wbig", i1, 16)
            mk = c.inc("gp", g.iota(difT[:, :], pattern=[[1, 128]], base=0,
                                    channel_multiplier=-1,
                                    allow_small_or_imprecise_dtypes=True), 1)
            c.wait(g, "gp", mk)
            i1 = g.affine_select(out=difT[:, :], in_=difT[:, :],
                                 compare_op=ALU.is_ge, fill=0.0,
                                 base=0, pattern=[[1, 128]], channel_multiplier=-1)
            c.inc("gp", i1, 1)

        m_gp_p2w = c.mark("wbig")
        m_gp_p2i = c.mark("gp")

        @blk.scalar
        def _(s):
            c.wait(s, "s", m_gp_p2w)
            c.wait(s, "s", m_gp_p2i)
            for h in range(H):
                i1 = s.activation(Dt[:, h * 128:(h + 1) * 128], difT[:, :], AF.Exp,
                                  scale=ldb[:, h:h + 1])
            c.inc("s", i1, 1)

        m_s_dtexp = c.mark("s")

        @blk.gpsimd
        def _(g):
            c.wait(g, "gp", m_s_dtexp)
            for h in range(H):
                i1 = g.affine_select(out=Dt[:, h * 128:(h + 1) * 128],
                                     in_=Dt[:, h * 128:(h + 1) * 128],
                                     compare_op=ALU.is_ge, fill=0.0,
                                     base=0, pattern=[[1, 128]], channel_multiplier=-1)
            c.inc("gp", i1, 1)

        m_gp_dt = c.mark("gp")

        (m2_x, m2_ld, m2_sq, m2_var, m2_rstds, m2_hbf, m2_tph, m2_hTe,
         m2_q, m2_qe, m2_qdec, m2_tpq, m2_qTe, m2_tpqd, m2_qTde, m2_g1, m2_g1e,
         m2_gate, m2_sig, m2_sc, m2_scm, m2_o, m2_oe, m2_ms, m2_rmss, m2_orms,
         m2_tpo, m2_oTe, m2_oTst) = ({} for _ in range(29))

        rholder = {}

        def p2_ln_all(i2):
            i = i2
            par = i % 2

            @blk.sync
            def _(sy, i=i, par=par):
                if i >= 2:
                    c.wait(sy, "sy", m2_hbf[i - 2])
                    c.wait(sy, "sy", m2_sq[i - 2])
                i1 = sy.dma_start(out=xb2[par][:, :], in_=x_own[i * CH:(i + 1) * CH, :])
                m2_x[i] = c.inc("xA" if par == 0 else "xB", i1, 16)

            @blk.scalar
            def _(s, i=i, par=par):
                c.wait(s, "s", m2_x[i])
                i1 = s.activation(junk[:, :], xb2[par][:, :], AF.Square,
                                  accum_out=ssumsq[:, :])
                m2_sq[i] = c.inc("s", i1, 1)

            @blk.vector
            def _(v, i=i, par=par):
                c.wait(v, "v", m2_x[i])
                mk = c.inc("v", v.reduce_sum(smalls[:, 0:1], xb2[par][:, :], axis=AX.X,
                                             negate=True), 1)
                c.wait(v, "v", mk)
                mk = c.inc("v", v.tensor_scalar(smalls[:, 1:2], smalls[:, 0:1],
                                                1.0 / D, None, op0=ALU.mult), 1)
                c.wait(v, "v", mk)
                c.wait(v, "v", m2_sq[i])
                mk = c.inc("v", v.tensor_scalar(smalls[:, 2:3], ssumsq[:, :],
                                                1.0 / D, None, op0=ALU.mult), 1)
                c.wait(v, "v", mk)
                mk = c.inc("v", v.tensor_mul(smalls[:, 3:4], smalls[:, 1:2],
                                             smalls[:, 1:2]), 1)
                c.wait(v, "v", mk)
                i1 = v.tensor_sub(smalls[:, 4:5], smalls[:, 2:3], smalls[:, 3:4])
                m2_var[i] = c.inc("v", i1, 1)

            @blk.scalar
            def _(s, i=i):
                c.wait(s, "s", m2_var[i])
                i1 = s.activation(smalls[:, 5:6], smalls[:, 4:5], AF.Sqrt, bias=eps_sb[:, 0:1])
                m2_rstds[i] = c.inc("s", i1, 1)

            @blk.vector
            def _(v, i=i, par=par):
                c.wait(v, "v", m2_rstds[i])
                v.tensor_scalar(xb2[par][:, :], xb2[par][:, :], smalls[:, 1:2], None,
                                op0=ALU.add)
                i1 = v.reciprocal(smalls[:, 6:7], smalls[:, 5:6])
                mr = c.inc("v", i1, 1)
                c.wait(v, "v", mr)
                mk = c.inc("v", v.scalar_tensor_tensor(xb2[par][:, :], ca1_rep[:, :],
                                                       smalls[:, 6:7], xb2[par][:, :],
                                                       op0=ALU.mult, op1=ALU.mult), 1)
                c.wait(v, "v", mk)
                if i >= 1:
                    c.wait(v, "v", m2_tph[i - 1])
                i1 = v.tensor_add(h_bf2[:, :], xb2[par][:, :], sa_rep[:, :])
                m2_hbf[i] = c.inc("v", i1, 1)



        p2_ln_all(0)

        for i in range(OWN):
            par = i % 2

            @blk.gpsimd
            def _(g, i=i, par=par):
                if i == 0:
                    c.wait(g, "gp", m_sy0)
                    rholder["r"] = g.register("r_off").__enter__()
                    g.reg_load(rholder["r"], own0_sb[0:1, 0:1])
                    g.reg_alu(rholder["r"], rholder["r"], 3 * CHUNK_ELEMS, ALU.mult)
                if i >= 2:
                    c.wait(g, "gp", m2_o[(i - 2, 1)])
                r = rholder["r"]
                i1 = g.dma_start(
                    out=kvs[par][:, :].rearrange("p (j e) -> p j e", e=D),
                    in_=bass.AP(sc_kvs.tensor, r, [[D, 128], [CH * D, 3], [1, D]]))
                m2_ld[i] = c.inc("kvA" if par == 0 else "kvB", i1, 16)
                g.reg_alu(r, r, 3 * CHUNK_ELEMS, ALU.add)

            @blk.tensor
            def _(t, i=i):
                c.wait(t, "t", m2_hbf[i])
                if i >= 1:
                    c.wait(t, "t", m2_oTe[i - 1])
                for dk in range(DK):
                    i1 = t.transpose(pt2[:, dk * 128:(dk + 1) * 128],
                                     h_bf2[:, dk * 128:(dk + 1) * 128], ident[:, :])
                m2_tph[i] = c.inc("t", i1, 1)

            @blk.vector
            def _(v, i=i):
                c.wait(v, "v", m2_tph[i])
                if i >= 1:
                    c.wait(v, "v", m2_g1[i - 1])
                    c.wait(v, "v", m2_q[i - 1])
                for eb in range(4):
                    i1 = v.tensor_copy(hT2[:, eb * 512:(eb + 1) * 512],
                                       pt2[:, eb * 512:(eb + 1) * 512])
                m2_hTe[i] = c.inc("v", i1, 1)

            if i + 1 < OWN:
                p2_ln_all(i + 1)

            @blk.tensor
            def _(t, i=i):
                c.wait(t, "t", m2_hTe[i])
                c.wait(t, "t", m_gp_p2w)
                if i >= 1:
                    c.wait(t, "t", m2_oe[(i - 1, 1)])
                for dk in range(DK):
                    for eb in range(4):
                        i1 = t.matmul(pq[:, eb * 512:(eb + 1) * 512],
                                      hT2[:, dk * 128:(dk + 1) * 128],
                                      wq_sb[:, dk * D + eb * 512:dk * D + (eb + 1) * 512],
                                      start=(dk == 0), stop=(dk == DK - 1))
                m2_q[i] = c.inc("t", i1, 1)

            @blk.scalar
            def _(s, i=i):
                c.wait(s, "s", m2_q[i])
                if i >= 1:
                    c.wait(s, "s", m2_tpq[i - 1])
                    c.wait(s, "s", m2_qdec[i - 1])
                for eb in range(4):
                    i1 = s.activation(q_tok[:, eb * 512:(eb + 1) * 512],
                                      pq[:, eb * 512:(eb + 1) * 512], AF.Silu)
                m2_qe[i] = c.inc("s", i1, 1)

            @blk.vector
            def _(v, i=i):
                c.wait(v, "v", m2_qe[i])
                if i >= 1:
                    c.wait(v, "v", m2_tpqd[i - 1])
                for h in range(H):
                    i1 = v.tensor_scalar(q_dec[:, h * 128:(h + 1) * 128],
                                         q_tok[:, h * 128:(h + 1) * 128],
                                         qdec16[:, h:h + 1], None, op0=ALU.mult)
                m2_qdec[i] = c.inc("v", i1, 1)

            @blk.tensor
            def _(t, i=i):
                c.wait(t, "t", m2_qe[i])
                c.wait(t, "t", m2_hTe[i])
                for h in range(H):
                    i1 = t.transpose(pt2[:, h * 128:(h + 1) * 128],
                                     q_tok[:, h * 128:(h + 1) * 128], ident[:, :])
                m2_tpq[i] = c.inc("t", i1, 1)

            @blk.vector
            def _(v, i=i):
                c.wait(v, "v", m2_tpq[i])
                if i >= 1:
                    c.wait(v, "v", m2_sc[(i - 1, 1)])
                for eb in range(4):
                    i1 = v.tensor_copy(qT[:, eb * 512:(eb + 1) * 512],
                                       pt2[:, eb * 512:(eb + 1) * 512])
                m2_qTe[i] = c.inc("v", i1, 1)

            @blk.tensor
            def _(t, i=i):
                c.wait(t, "t", m2_qdec[i])
                c.wait(t, "t", m2_qTe[i])
                for h in range(H):
                    i1 = t.transpose(pt2[:, h * 128:(h + 1) * 128],
                                     q_dec[:, h * 128:(h + 1) * 128], ident[:, :])
                m2_tpqd[i] = c.inc("t", i1, 1)

            @blk.vector
            def _(v, i=i):
                c.wait(v, "v", m2_tpqd[i])
                if i >= 1:
                    c.wait(v, "v", m2_o[(i - 1, 1)])
                for eb in range(4):
                    i1 = v.tensor_copy(qTd[:, eb * 512:(eb + 1) * 512],
                                       pt2[:, eb * 512:(eb + 1) * 512])
                m2_qTde[i] = c.inc("v", i1, 1)

            @blk.tensor
            def _(t, i=i):
                c.wait(t, "t", m2_hTe[i])
                guard = m2_scm.get((i - 1, 1), 0)
                c.wait(t, "t", guard)
                for dk in range(DK):
                    i1 = t.matmul(psc[:, 0:128],
                                  wg1_sb[:, dk * 128:(dk + 1) * 128],
                                  hT2[:, dk * 128:(dk + 1) * 128],
                                  start=(dk == 0), stop=(dk == DK - 1))
                m2_g1[i] = c.inc("t", i1, 1)

            @blk.vector
            def _(v, i=i):
                c.wait(v, "v", m2_g1[i])
                if i >= 1:
                    c.wait(v, "v", m2_gate[i - 1])
                i1 = v.tensor_copy(g1T[:, :], psc[:, 0:128])
                m2_g1e[i] = c.inc("v", i1, 1)

            @blk.tensor
            def _(t, i=i):
                c.wait(t, "t", m2_g1e[i])
                c.wait(t, "t", m2_qe[i])
                for eb in range(4):
                    i1 = t.matmul(pq[:, eb * 512:(eb + 1) * 512], g1T[:, :],
                                  wg2_sb[:, eb * 512:(eb + 1) * 512],
                                  start=True, stop=True)
                m2_gate[i] = c.inc("t", i1, 1)

            @blk.scalar
            def _(s, i=i):
                c.wait(s, "s", m2_gate[i])
                if i >= 1:
                    c.wait(s, "s", m2_ms[i - 1])
                for eb in range(4):
                    i1 = s.activation(sig[:, eb * 512:(eb + 1) * 512],
                                      pq[:, eb * 512:(eb + 1) * 512], AF.Sigmoid)
                m2_sig[i] = c.inc("s", i1, 1)

            for hg in range(2):
                @blk.tensor
                def _(t, i=i, hg=hg, par=par):
                    c.wait(t, "t", m2_qTe[i])
                    c.wait(t, "t", m2_ld[i])
                    guard = m2_scm.get((i, hg - 1), m2_scm.get((i - 1, 1), 0))
                    c.wait(t, "t", guard)
                    if hg == 0:
                        c.wait(t, "t", m2_g1e[i])
                    for h in range(hg * 8, hg * 8 + 8):
                        i1 = t.matmul(psc[:, (h % 8) * 128:(h % 8 + 1) * 128],
                                      kTb[par][:, h * 128:(h + 1) * 128],
                                      qT[:, h * 128:(h + 1) * 128],
                                      start=True, stop=True)
                    m2_sc[(i, hg)] = c.inc("t", i1, 1)

                @blk.vector
                def _(v, i=i, hg=hg):
                    c.wait(v, "v", m2_sc[(i, hg)])
                    c.wait(v, "v", m_gp_dt)
                    if i >= 1:
                        c.wait(v, "v", m2_o[(i - 1, hg)])
                    i1 = v.tensor_mul(scm[:, hg * 1024:(hg + 1) * 1024],
                                      psc[:, :],
                                      Dt[:, hg * 1024:(hg + 1) * 1024])
                    m2_scm[(i, hg)] = c.inc("v", i1, 1)

                @blk.tensor
                def _(t, i=i, hg=hg, par=par):
                    c.wait(t, "t", m2_scm[(i, hg)])
                    c.wait(t, "t", m2_qTde[i])
                    c.wait(t, "t", m2_sig[i])
                    guard = m2_oe.get((i - 1, hg), 0)
                    c.wait(t, "t", guard)
                    for h in range(hg * 8, hg * 8 + 8):
                        sl = slice((hg % 2) * 1024 + (h % 8) * 128,
                                   (hg % 2) * 1024 + (h % 8 + 1) * 128)
                        t.matmul(pq[:, sl], scm[:, h * 128:(h + 1) * 128],
                                 vbb[par][:, h * 128:(h + 1) * 128],
                                 start=True, stop=False)
                        i1 = t.matmul(pq[:, sl], qTd[:, h * 128:(h + 1) * 128],
                                      Sbb[par][:, h * 128:(h + 1) * 128],
                                      start=False, stop=True)
                    m2_o[(i, hg)] = c.inc("t", i1, 1)

                @blk.vector
                def _(v, i=i, hg=hg):
                    c.wait(v, "v", m2_o[(i, hg)])
                    i1 = v.tensor_copy(o_sb[:, hg * 1024:(hg + 1) * 1024],
                                       pq[:, (hg % 2) * 1024:(hg % 2 + 1) * 1024])
                    m2_oe[(i, hg)] = c.inc("v", i1, 1)

            @blk.vector
            def _(v, i=i):
                c.wait(v, "v", m2_sig[i])
                c.wait(v, "v", m2_oe[(i, 1)])
                mk = c.inc("v", v.tensor_mul(o_sb[:, :], o_sb[:, :], sig[:, :]), 1)
                c.wait(v, "v", mk)
                mk = c.inc("v", v.tensor_mul(tmpf[:, :], o_sb[:, :], o_sb[:, :]), 1)
                c.wait(v, "v", mk)
                i1 = v.tensor_reduce(ms16[:, :],
                                     tmpf[:, :].rearrange("p (h q) -> p h q", q=128),
                                     axis=AX.X, op=ALU.add)
                m2_ms[i] = c.inc("v", i1, 1)

            @blk.scalar
            def _(s, i=i):
                c.wait(s, "s", m2_ms[i])
                i1 = s.activation(rmss[:, :], ms16[:, :], AF.Sqrt,
                                  scale=1.0 / HD, bias=eps_sb[:, 0:1])
                m2_rmss[i] = c.inc("s", i1, 1)

            @blk.vector
            def _(v, i=i):
                c.wait(v, "v", m2_rmss[i])
                i1 = v.reciprocal(rrr[:, :], rmss[:, :])
                mr = c.inc("v", i1, 1)
                c.wait(v, "v", mr)
                if i >= 1:
                    c.wait(v, "v", m2_tpo[i - 1])
                for h in range(H):
                    i1 = v.tensor_scalar(orms[:, h * 128:(h + 1) * 128],
                                         o_sb[:, h * 128:(h + 1) * 128],
                                         rrr[:, h:h + 1], None, op0=ALU.mult)
                m2_orms[i] = c.inc("v", i1, 1)

            @blk.tensor
            def _(t, i=i):
                c.wait(t, "t", m2_orms[i])
                c.wait(t, "t", m2_qTde[i])
                for dk in range(DK):
                    i1 = t.transpose(pt2[:, dk * 128:(dk + 1) * 128],
                                     orms[:, dk * 128:(dk + 1) * 128], ident[:, :])
                m2_tpo[i] = c.inc("t", i1, 1)

            @blk.scalar
            def _(s, i=i):
                c.wait(s, "s", m2_tpo[i])
                if i >= 1:
                    c.wait(s, "s", m2_oTst[i - 1])
                for dk in range(DK):
                    i1 = s.activation(oT_sb[:, dk * 128:(dk + 1) * 128],
                                      pt2[:, dk * 128:(dk + 1) * 128], AF.Copy,
                                      scale=normw_fc[:, dk:dk + 1])
                m2_oTe[i] = c.inc("s", i1, 1)

            @blk.sync
            def _(sy, i=i):
                c.wait(sy, "sy", m2_oTe[i])
                i1 = sy.dma_start(out=sc_oT[i * CH:(i + 1) * CH, :], in_=oT_sb[:, :])
                m2_oTst[i] = c.inc("st", i1, 16)

        if upto == 2:
            @blk.gpsimd
            def _(g):
                for sn in ("st", "v", "s", "t"):
                    c.wait(g, "gp", c.mark(sn))
            p2.close()
            return nc
        p2.close()

        bar2 = c.barrier_marks()
        _emit_barrier(blk, c, bar2)

        p3 = ExitStack()
        wo_sb = p3.enter_context(nc.sbuf_tensor("wo_sb", [128, DK * D], BF16))
        ga_rep = p3.enter_context(nc.sbuf_tensor("ga_rep", [128, D], F32))
        oTb = [p3.enter_context(nc.sbuf_tensor(f"oTb{i}", [128, D], BF16)) for i in range(2)]
        xb3 = [p3.enter_context(nc.sbuf_tensor(f"x3_{i}", [128, D], F32)) for i in range(2)]
        xm_sb = p3.enter_context(nc.sbuf_tensor("xm_sb", [128, D], F32))
        tmp3 = p3.enter_context(nc.sbuf_tensor("tmp3", [128, D], F32))
        po = p3.enter_context(nc.psum_tensor("po", [128, D], F32))

        @blk.gpsimd
        def _(g):
            i1 = g.dma_start(out=_b3(wo_sb[:, :], D), in_=_w3(wo))
            c.inc("wbig", i1, 16)

        m3_wo = c.mark("wbig")

        @blk.sync
        def _(sy):
            i1 = sy.dma_start(out=ga_rep[:, :], in_=sc_mod[2 * CH:3 * CH, :])
            c.inc("ld1", i1, 16)

        m3_ga = c.mark("ld1")
        m3_oT, m3_x, m3_mm, m3_xm, m3_st = {}, {}, {}, {}, {}

        for i in range(OWN):
            par = i % 2

            @blk.gpsimd
            def _(g, i=i, par=par):
                if i >= 2:
                    c.wait(g, "gp", m3_mm[i - 2])
                i1 = g.dma_start(out=oTb[par][:, :], in_=sc_oT[i * CH:(i + 1) * CH, :])
                m3_oT[i] = c.inc("oTA" if par == 0 else "oTB", i1, 16)

            @blk.sync
            def _(sy, i=i, par=par):
                if i >= 2:
                    c.wait(sy, "sy", m3_xm[i - 2])
                i1 = sy.dma_start(out=xb3[par][:, :], in_=x_own[i * CH:(i + 1) * CH, :])
                m3_x[i] = c.inc("xA" if par == 0 else "xB", i1, 16)

            @blk.tensor
            def _(t, i=i, par=par):
                c.wait(t, "t", m3_oT[i])
                c.wait(t, "t", m3_wo)
                if i >= 1:
                    c.wait(t, "t", m3_xm[i - 1])
                for dk in range(DK):
                    for eb in range(4):
                        i1 = t.matmul(po[:, eb * 512:(eb + 1) * 512],
                                      oTb[par][:, dk * 128:(dk + 1) * 128],
                                      wo_sb[:, dk * D + eb * 512:dk * D + (eb + 1) * 512],
                                      start=(dk == 0), stop=(dk == DK - 1))
                m3_mm[i] = c.inc("t", i1, 1)

            if dbg and upto == 3 and i == 0:
                @blk.vector
                def _(v):
                    c.wait(v, "v", m3_mm[0])
                    i1 = v.tensor_copy(tmp3[:, :], po[:, :])
                    c.inc("v", i1, 1)

                @blk.sync
                def _(sy):
                    c.wait(sy, "sy", c.mark("v"))
                    i1 = sy.dma_start(out=dbg_att, in_=tmp3[:, :])
                    c.inc("st", i1, 16)
                    i1 = sy.dma_start(out=dbg_oTb, in_=oTb[0][:, :])
                    c.inc("st", i1, 16)

            @blk.vector
            def _(v, i=i, par=par):
                c.wait(v, "v", m3_mm[i])
                if dbg and upto == 3 and i == 0:
                    c.wait(v, "v", c.mark("st"))
                c.wait(v, "v", m3_x[i])
                c.wait(v, "v", m3_ga)
                if i >= 1:
                    c.wait(v, "v", m3_st[i - 1])
                mk = c.inc("v", v.tensor_mul(tmp3[:, :], po[:, :], ga_rep[:, :]), 1)
                c.wait(v, "v", mk)
                i1 = v.tensor_add(xm_sb[:, :], tmp3[:, :], xb3[par][:, :])
                m3_xm[i] = c.inc("v", i1, 1)

            @blk.sync
            def _(sy, i=i):
                c.wait(sy, "sy", m3_xm[i])
                i1 = sy.dma_start(out=sc_xmid[i * CH:(i + 1) * CH, :], in_=xm_sb[:, :])
                m3_st[i] = c.inc("st", i1, 16)

        if upto == 3:
            @blk.gpsimd
            def _(g):
                for sn in ("st", "v", "s", "t"):
                    c.wait(g, "gp", c.mark(sn))
            p3.close()
            return nc
        p3.close()

        bar3 = c.barrier_marks()
        _emit_barrier(blk, c, bar3)

        p4 = ExitStack()
        h2T = p4.enter_context(nc.sbuf_tensor("h2T", [128, DK * TILE], BF16))
        zT = p4.enter_context(nc.sbuf_tensor("zT", [128, ICN * TILE], BF16))
        acc = p4.enter_context(nc.sbuf_tensor("acc", [128, TSUB * D], F32))
        wup = [p4.enter_context(nc.sbuf_tensor(f"wup{i}", [128, DK * 128], BF16))
               for i in range(2)]
        wdn = p4.enter_context(nc.sbuf_tensor("wdn", [128, 4 * D], BF16))
        modb = [p4.enter_context(nc.sbuf_tensor(f"modb{i}", [128, D], F32))
                for i in range(2)]
        gwT_sb = p4.enter_context(nc.sbuf_tensor("gwT_sb", [128, DK * 8], BF16))
        xm4 = p4.enter_context(nc.sbuf_tensor("xm4", [128, D], F32))
        h2bf = p4.enter_context(nc.sbuf_tensor("h2bf", [128, D], BF16))
        silt = p4.enter_context(nc.sbuf_tensor("silt", [128, TILE], F32))
        lgT = p4.enter_context(nc.sbuf_tensor("lgT", [8, TILE], BF16))
        lgt = p4.enter_context(nc.sbuf_tensor("lgt", [128, TSUB * 8], F32))
        rt0 = p4.enter_context(nc.sbuf_tensor("rt0", [128, TSUB * 8], F32))
        rt1 = p4.enter_context(nc.sbuf_tensor("rt1", [128, TSUB * 8], F32))
        rt2 = p4.enter_context(nc.sbuf_tensor("rt2", [128, TSUB * 8], F32))
        rts = p4.enter_context(nc.sbuf_tensor("rts", [128, TSUB], F32))
        comb = p4.enter_context(nc.sbuf_tensor("comb", [128, TSUB * 8], F32))

        @blk.gpsimd
        def _(g):
            i1 = g.dma_start(out=_b3(gwT_sb[:, :], 8), in_=_w3(gwT))
            c.inc("wbig", i1, 16)

        m4_gw = c.mark("wbig")

        def up_src(e10, mat, ic):
            if e10 < 8:
                base = (egT if mat == 0 else euT)[e10]
            else:
                base = (sgT if mat == 0 else suT)[:, (e10 - 8) * ISH:(e10 - 7) * ISH]
            return _w3(base[:, ic * 128:(ic + 1) * 128])

        def dn_src(e10, sw_i):
            r0 = sum(DSW[:sw_i]) * 128
            r1 = r0 + DSW[sw_i] * 128
            if e10 < 8:
                base = edT[e10]
            else:
                base = sdT[(e10 - 8) * ISH:(e10 - 7) * ISH, :]
            return _w3(base[r0:r1, :])

        for tt in range(NTILE):
            bar_t = c.barrier_marks()
            _emit_barrier(blk, c, bar_t)

            t4 = ExitStack()
            pscope = ExitStack()
            pup = t4.enter_context(nc.psum_tensor("pup_%d" % tt, [128, D], F32))
            pt4 = pscope.enter_context(nc.psum_tensor("pt4_%d" % tt, [128, D], BF16))

            m4_xm, m4_sq, m4_var, m4_rstds, m4_hbf = {}, {}, {}, {}, {}
            m4_tph, m4_he = {}, {}

            @blk.sync
            def _(sy, tt=tt):
                i1 = sy.dma_start(out=modb[0][:, :], in_=sc_mod[4 * CH:5 * CH, :])
                c.inc("ld1", i1, 16)
                i1 = sy.dma_start(out=modb[1][:, :], in_=sc_mod[3 * CH:4 * CH, :])
                c.inc("ld1", i1, 16)

            m4_mods = c.mark("ld1")

            for cc in range(TSUB):
                ich = tt * TSUB + cc

                @blk.sync
                def _(sy, ich=ich, cc=cc):
                    c.wait(sy, "sy", m4_hbf.get(cc - 1, 0))
                    c.wait(sy, "sy", m4_sq.get(cc - 1, 0))
                    i1 = sy.dma_start(out=xm4[:, :],
                                      in_=sc_xmid[ich * CH:(ich + 1) * CH, :])
                    m4_xm[cc] = c.inc("xA", i1, 16)

                @blk.scalar
                def _(s, cc=cc):
                    c.wait(s, "s", m4_xm[cc])
                    i1 = s.activation(junk[:, :], xm4[:, :], AF.Square,
                                      accum_out=ssumsq[:, :])
                    m4_sq[cc] = c.inc("s", i1, 1)

                @blk.vector
                def _(v, cc=cc):
                    c.wait(v, "v", m4_xm[cc])
                    mk = c.inc("v", v.reduce_sum(smalls[:, 0:1], xm4[:, :], axis=AX.X,
                                                 negate=True), 1)
                    c.wait(v, "v", mk)
                    mk = c.inc("v", v.tensor_scalar(smalls[:, 1:2], smalls[:, 0:1],
                                                    1.0 / D, None, op0=ALU.mult), 1)
                    c.wait(v, "v", mk)
                    c.wait(v, "v", m4_sq[cc])
                    mk = c.inc("v", v.tensor_scalar(smalls[:, 2:3], ssumsq[:, :],
                                                    1.0 / D, None, op0=ALU.mult), 1)
                    c.wait(v, "v", mk)
                    mk = c.inc("v", v.tensor_mul(smalls[:, 3:4], smalls[:, 1:2],
                                                 smalls[:, 1:2]), 1)
                    c.wait(v, "v", mk)
                    i1 = v.tensor_sub(smalls[:, 4:5], smalls[:, 2:3], smalls[:, 3:4])
                    m4_var[cc] = c.inc("v", i1, 1)

                @blk.scalar
                def _(s, cc=cc):
                    c.wait(s, "s", m4_var[cc])
                    i1 = s.activation(smalls[:, 5:6], smalls[:, 4:5], AF.Sqrt, bias=eps_sb[:, 0:1])
                    m4_rstds[cc] = c.inc("s", i1, 1)

                @blk.vector
                def _(v, cc=cc):
                    c.wait(v, "v", m4_rstds[cc])
                    v.tensor_scalar(xm4[:, :], xm4[:, :], smalls[:, 1:2], None,
                                    op0=ALU.add)
                    i1 = v.reciprocal(smalls[:, 6:7], smalls[:, 5:6])
                    mr = c.inc("v", i1, 1)
                    c.wait(v, "v", mr)
                    c.wait(v, "v", m4_mods)
                    mk = c.inc("v", v.scalar_tensor_tensor(xm4[:, :], modb[0][:, :],
                                                           smalls[:, 6:7], xm4[:, :],
                                                           op0=ALU.mult, op1=ALU.mult), 1)
                    c.wait(v, "v", mk)
                    c.wait(v, "v", m4_tph.get(cc - 1, 0))
                    i1 = v.tensor_add(h2bf[:, :], xm4[:, :], modb[1][:, :])
                    m4_hbf[cc] = c.inc("v", i1, 1)

                @blk.tensor
                def _(t, cc=cc):
                    c.wait(t, "t", m4_hbf[cc])
                    c.wait(t, "t", m4_he.get(cc - 1, 0))
                    for dk in range(DK):
                        i1 = t.transpose(pt4[:, dk * 128:(dk + 1) * 128],
                                         h2bf[:, dk * 128:(dk + 1) * 128], ident[:, :])
                    m4_tph[cc] = c.inc("t", i1, 1)

                @blk.scalar
                def _(s, cc=cc):
                    c.wait(s, "s", m4_tph[cc])
                    for dk in range(DK):
                        i1 = s.copy(h2T[:, dk * TILE + cc * 128:dk * TILE + (cc + 1) * 128],
                                    pt4[:, dk * 128:(dk + 1) * 128])
                    m4_he[cc] = c.inc("s", i1, 1)

            m4_hall = c.mark("s")

            @blk.tensor
            def _(t):
                c.wait(t, "t", m4_hall)
                c.wait(t, "t", m4_gw)
                for th in range(2):
                    for dk in range(DK):
                        i1 = t.matmul(pup[0:8, th * 512:(th + 1) * 512],
                                      gwT_sb[:, dk * 8:(dk + 1) * 8],
                                      h2T[:, dk * TILE + th * 512:dk * TILE + (th + 1) * 512],
                                      start=(dk == 0), stop=(dk == DK - 1))
                c.inc("t", i1, 1)

            m4_rmm = c.mark("t")

            @blk.scalar
            def _(s):
                c.wait(s, "s", m4_rmm)
                i1 = s.copy(lgT[:, :], pup[0:8, 0:1024])
                c.inc("s", i1, 1)

            m4_lge = c.mark("s")

            @blk.tensor
            def _(t):
                c.wait(t, "t", m4_lge)
                for sub in range(TSUB):
                    i1 = t.transpose(pt4[:, sub * 8:(sub + 1) * 8],
                                     lgT[:, sub * 128:(sub + 1) * 128],
                                     ident[0:8, 0:8])
                c.inc("t", i1, 1)

            m4_ltp = c.mark("t")

            @blk.vector
            def _(v):
                c.wait(v, "v", m4_ltp)
                r3 = lambda tns: tns[:, :].rearrange("p (s e) -> p s e", e=8)

                def bc(tns):
                    return tns[:, :].unsqueeze(2).broadcast_to([128, TSUB, 8])

                def sw(i1):
                    mk = c.inc("v", i1, 1)
                    c.wait(v, "v", mk)

                sw(v.tensor_copy(lgt[:, :], pt4[:, 0:TSUB * 8]))
                sw(v.tensor_reduce(rts[:, :], r3(lgt), axis=AX.X, op=ALU.max))
                sw(v.tensor_tensor(r3(rt0), r3(lgt), bc(rts), op=ALU.subtract))

            @blk.scalar
            def _(s):
                c.wait(s, "s", c.mark("v"))
                i1 = s.activation(rt1[:, :], rt0[:, :], AF.Exp)
                c.inc("s", i1, 1)

            m4_exp = c.mark("s")

            @blk.vector
            def _(v):
                c.wait(v, "v", m4_exp)
                r3 = lambda tns: tns[:, :].rearrange("p (s e) -> p s e", e=8)

                def bc(tns):
                    return tns[:, :].unsqueeze(2).broadcast_to([128, TSUB, 8])

                def sw(i1):
                    mk = c.inc("v", i1, 1)
                    c.wait(v, "v", mk)

                sw(v.tensor_reduce(rts[:, :], r3(rt1), axis=AX.X, op=ALU.add))
                sw(v.reciprocal(rts[:, :], rts[:, :]))
                sw(v.tensor_tensor(r3(rt1), r3(rt1), bc(rts), op=ALU.mult))
                sw(v.tensor_reduce(rts[:, :], r3(rt1), axis=AX.X, op=ALU.max))
                sw(v.tensor_tensor(r3(rt2), r3(rt1), bc(rts), op=ALU.subtract))
                sw(v.tensor_scalar(rt2[:, :], rt2[:, :], 0.0, None, op0=ALU.is_ge))
                sw(v.tensor_mul(rt0[:, :], rt1[:, :], rt2[:, :]))
                sw(v.tensor_sub(rt0[:, :], rt1[:, :], rt0[:, :]))
                sw(v.tensor_reduce(rts[:, :], r3(rt0), axis=AX.X, op=ALU.max))
                sw(v.tensor_tensor(r3(rt0), r3(rt0), bc(rts), op=ALU.subtract))
                sw(v.tensor_scalar(rt0[:, :], rt0[:, :], 0.0, None, op0=ALU.is_ge))
                sw(v.tensor_add(rt2[:, :], rt2[:, :], rt0[:, :]))
                sw(v.tensor_mul(rt2[:, :], rt1[:, :], rt2[:, :]))
                sw(v.tensor_reduce(rts[:, :], r3(rt2), axis=AX.X, op=ALU.add))
                sw(v.tensor_scalar(rts[:, :], rts[:, :], 1e-20, None, op0=ALU.add))
                sw(v.reciprocal(rts[:, :], rts[:, :]))
                i1 = v.tensor_tensor(r3(comb), r3(rt2), bc(rts), op=ALU.mult)
                c.inc("v", i1, 1)

            m4_comb = c.mark("v")
            pscope.close()
            pdn = t4.enter_context(nc.psum_tensor("pdn_%d" % tt, [128, D], F32))

            m4_wup, m4_wdn, m4_y, m4_z, m4_dn, m4_ae = {}, {}, {}, {}, {}, {}
            up_items = [(e10, ic, mat) for e10 in range(NE10)
                        for ic in range(ICN) for mat in range(2)]
            up_ptr = [0]

            def emit_up_loads(upto_item, m4_wup=m4_wup, m4_y=m4_y,
                              up_items=up_items, up_ptr=up_ptr):
                while up_ptr[0] <= upto_item and up_ptr[0] < len(up_items):
                    e10, ic, mat = up_items[up_ptr[0]]
                    kidx = up_ptr[0]

                    @blk.gpsimd
                    def _(g, e10=e10, ic=ic, mat=mat, kidx=kidx):
                        if kidx >= 2:
                            pe, pc, pm = up_items[kidx - 2]
                            c.wait(g, "gp", m4_y.get((pe, pc, pm), 0))
                        i1 = g.dma_start(out=_b3(wup[mat][:, :], 128), in_=up_src(e10, mat, ic))
                        m4_wup[(e10, ic, mat)] = c.inc("upA" if mat == 0 else "upB", i1, 16)
                    up_ptr[0] += 1

            for e10 in range(NE10):
                @blk.gpsimd
                def _(g, e10=e10):
                    c.wait(g, "gp", m4_dn.get((e10 - 1, 2, TSUB - 1), 0))
                    i1 = g.dma_start(out=_b3(wdn[:, 0:DSW[0] * D], D),
                                     in_=dn_src(e10, 0))
                    m4_wdn[(e10, 0)] = c.inc("dnw", i1, 16)

                for ic in range(ICN):
                    emit_up_loads(e10 * ICN * 2 + ic * 2 + 1)
                    for mat in range(2):
                        @blk.tensor
                        def _(t, e10=e10, ic=ic, mat=mat):
                            c.wait(t, "t", m4_wup[(e10, ic, mat)])
                            c.wait(t, "t", m4_lge)
                            guard = m4_z.get((e10, ic - 1),
                                             m4_z.get((e10 - 1, ICN - 1), 0))
                            c.wait(t, "t", guard)
                            c.wait(t, "t", m4_ae.get((e10 - 1, 2, TSUB - 1), 0))
                            for dk in range(DK):
                                for th in range(2):
                                    i1 = t.matmul(
                                        pup[:, mat * 1024 + th * 512:
                                            mat * 1024 + (th + 1) * 512],
                                        wup[mat][:, dk * 128:(dk + 1) * 128],
                                        h2T[:, dk * TILE + th * 512:
                                            dk * TILE + (th + 1) * 512],
                                        start=(dk == 0), stop=(dk == DK - 1))
                            m4_y[(e10, ic, mat)] = c.inc("t", i1, 1)

                    @blk.scalar
                    def _(s, e10=e10, ic=ic):
                        c.wait(s, "s", m4_y[(e10, ic, 0)])
                        c.wait(s, "s", m4_z.get((e10, ic - 1),
                                                     m4_z.get((e10 - 1, ICN - 1), 0)))
                        i1 = s.activation(silt[:, :], pup[:, 0:1024], AF.Silu)
                        m4_z[(e10, ic, "s")] = c.inc("s", i1, 1)

                    @blk.vector
                    def _(v, e10=e10, ic=ic):
                        c.wait(v, "v", m4_z[(e10, ic, "s")])
                        c.wait(v, "v", m4_y[(e10, ic, 1)])
                        c.wait(v, "v", m4_dn.get((e10 - 1, 2, TSUB - 1), 0))
                        i1 = v.tensor_mul(zT[:, ic * TILE:(ic + 1) * TILE],
                                          silt[:, :], pup[:, 1024:2048])
                        m4_z[(e10, ic)] = c.inc("v", i1, 1)

                for sw_i in range(3):
                    nic = DSW[sw_i]
                    ic0_base = sum(DSW[:sw_i])
                    if sw_i >= 1:
                        @blk.gpsimd
                        def _(g, e10=e10, sw_i=sw_i, nic=nic):
                            c.wait(g, "gp", m4_dn[(e10, sw_i - 1, TSUB - 1)])
                            i1 = g.dma_start(out=_b3(wdn[:, 0:nic * D], D),
                                             in_=dn_src(e10, sw_i))
                            m4_wdn[(e10, sw_i)] = c.inc("dnw", i1, 16)

                    for sub in range(TSUB):
                        pbank = pdn if sub % 2 == 0 else pup

                        @blk.tensor
                        def _(t, e10=e10, sw_i=sw_i, sub=sub, nic=nic,
                              ic0_base=ic0_base, pbank=pbank):
                            c.wait(t, "t", m4_wdn[(e10, sw_i)])
                            for ic0 in range(nic):
                                c.wait(t, "t", m4_z[(e10, ic0_base + ic0)])
                            if sub % 2 == 1:
                                c.wait(t, "t", m4_z[(e10, ICN - 1)])
                            prev = m4_ae.get((e10, sw_i, sub - 2),
                                             m4_ae.get((e10, sw_i - 1, TSUB - 2 + sub),
                                                       m4_ae.get((e10 - 1, 2, TSUB - 2 + sub), 0)))
                            c.wait(t, "t", prev)
                            for ic0 in range(nic):
                                ic = ic0_base + ic0
                                for eb in range(4):
                                    i1 = t.matmul(
                                        pbank[:, eb * 512:(eb + 1) * 512],
                                        zT[:, ic * TILE + sub * 128:
                                           ic * TILE + (sub + 1) * 128],
                                        wdn[:, ic0 * D + eb * 512:ic0 * D + (eb + 1) * 512],
                                        start=(ic0 == 0), stop=(ic0 == nic - 1))
                            m4_dn[(e10, sw_i, sub)] = c.inc("t", i1, 1)

                        @blk.vector
                        def _(v, e10=e10, sw_i=sw_i, sub=sub, pbank=pbank):
                            c.wait(v, "v", m4_dn[(e10, sw_i, sub)])
                            c.wait(v, "v", m4_comb)
                            asl = acc[:, sub * D:(sub + 1) * D]
                            if e10 == 0 and sw_i == 0:
                                i1 = v.tensor_scalar(
                                    asl, pbank[:, :],
                                    comb[:, sub * 8 + e10:sub * 8 + e10 + 1],
                                    None, op0=ALU.mult)
                            else:
                                sc = (comb[:, sub * 8 + e10:sub * 8 + e10 + 1]
                                      if e10 < 8 else ones_f[:, 0:1])
                                i1 = v.scalar_tensor_tensor(asl, pbank[:, :], sc, asl,
                                                            op0=ALU.mult, op1=ALU.add)
                            m4_ae[(e10, sw_i, sub)] = c.inc("v", i1, 1)

            @blk.sync
            def _(sy, tt=tt):
                for sub in range(TSUB):
                    c.wait(sy, "sy", m4_ae[(NE10 - 1, 2, sub)])
                i1 = sy.dma_start(out=modb[1][:, :], in_=sc_mod[5 * CH:6 * CH, :])
                c.inc("ld1", i1, 16)

            m4_gb = c.mark("ld1")
            m4_ep, m4_yst = {}, {}

            for cc in range(TSUB):
                ich = tt * TSUB + cc

                @blk.sync
                def _(sy, ich=ich, cc=cc):
                    c.wait(sy, "sy", m4_ep.get(cc - 1, 0))
                    c.wait(sy, "sy", m4_yst.get(cc - 1, 0))
                    i1 = sy.dma_start(out=xm4[:, :],
                                      in_=sc_xmid[ich * CH:(ich + 1) * CH, :])
                    m4_xm[(cc, "ep")] = c.inc("xA", i1, 16)

                @blk.vector
                def _(v, ich=ich, cc=cc):
                    c.wait(v, "v", m4_xm[(cc, "ep")])
                    c.wait(v, "v", m4_gb)
                    for hf in range(2):
                        mk = c.inc("v", v.tensor_mul(
                            silt[:, :],
                            acc[:, cc * D + hf * TILE:cc * D + (hf + 1) * TILE],
                            modb[1][:, hf * TILE:(hf + 1) * TILE]), 1)
                        c.wait(v, "v", mk)
                        mk = c.inc("v", v.tensor_add(
                            xm4[:, hf * TILE:(hf + 1) * TILE], silt[:, :],
                            xm4[:, hf * TILE:(hf + 1) * TILE]), 1)
                        c.wait(v, "v", mk)
                    m4_ep[cc] = mk

                @blk.sync
                def _(sy, ich=ich, cc=cc):
                    c.wait(sy, "sy", m4_ep[cc])
                    i1 = sy.dma_start(out=y_out[ich * CH:(ich + 1) * CH, :],
                                      in_=xm4[:, :])
                    m4_yst[cc] = c.inc("st", i1, 16)

            t4.close()

        @blk.gpsimd
        def _(g):
            for sn in ("st", "v", "s", "t", "ld1"):
                c.wait(g, "gp", c.mark(sn))

        p4.close()

    return nc


def _prep_inputs(inputs):
    a = {k: np.asarray(v) for k, v in inputs.items()}
    w = {}
    w["x"] = np.ascontiguousarray(a["L0_attn_x"], dtype=np.float32)
    w["c"] = np.ascontiguousarray(a["L0_c"], dtype=np.float32)
    w["ld"] = np.ascontiguousarray(a["L0_attn_log_decay"], dtype=np.float32)
    w["wmod"] = np.ascontiguousarray(a["L0_wmod"], dtype=np.float32)
    w["wq"] = np.ascontiguousarray(a["L0_attn_Wq"], dtype=np.float32)
    w["wk"] = np.ascontiguousarray(a["L0_attn_Wk"], dtype=np.float32)
    w["wv"] = np.ascontiguousarray(a["L0_attn_Wv"], dtype=np.float32)
    w["wo"] = np.ascontiguousarray(a["L0_attn_Wo"], dtype=np.float32)
    w["wg1"] = np.ascontiguousarray(a["L0_attn_Wg1"], dtype=np.float32)
    w["wg2"] = np.ascontiguousarray(a["L0_attn_Wg2"], dtype=np.float32)
    w["normw"] = np.ascontiguousarray(a["L0_attn_norm_w"], dtype=np.float32)
    w["gwT"] = np.ascontiguousarray(a["L0_ffn_gate_w"].T, dtype=np.float32)
    w["sgT"] = np.ascontiguousarray(a["L0_ffn_shared_gate"].T, dtype=np.float32)
    w["suT"] = np.ascontiguousarray(a["L0_ffn_shared_up"].T, dtype=np.float32)
    w["sdT"] = np.ascontiguousarray(a["L0_ffn_shared_down"].T, dtype=np.float32)
    w["egT"] = np.ascontiguousarray(a["L0_ffn_expert_gate"].transpose(0, 2, 1),
                                    dtype=np.float32)
    w["euT"] = np.ascontiguousarray(a["L0_ffn_expert_up"].transpose(0, 2, 1),
                                    dtype=np.float32)
    w["edT"] = np.ascontiguousarray(a["L0_ffn_expert_down"].transpose(0, 2, 1),
                                    dtype=np.float32)
    return w


def _core_in_map(w, core, upto=4):
    b, j = core // 2, core % 2
    im = {
        "x_scan": w["x"][b],
        "c_row": w["c"][b],
        "ld": w["ld"],
        "wmod": w["wmod"],
        "wk": w["wk"],
        "wv": w["wv"],
        "own0": np.array([[OWN * j]], dtype=np.int32),
    }
    if upto >= 2:
        im["x_own"] = np.ascontiguousarray(w["x"][b, j * 2048:(j + 1) * 2048])
        im["wq"] = w["wq"]
        im["wg1"] = w["wg1"]
        im["wg2"] = w["wg2"]
        im["normw"] = w["normw"]
    if upto >= 3:
        im["wo"] = w["wo"]
    if upto >= 4:
        im["gwT"] = w["gwT"]
        im["sgT"] = w["sgT"]
        im["suT"] = w["suT"]
        im["sdT"] = w["sdT"]
        im["egT"] = w["egT"]
        im["euT"] = w["euT"]
        im["edT"] = w["edT"]
    return im


def kernel(**inputs):
    w = _prep_inputs(inputs)
    nc = build_program(upto=4)
    in_maps = [_core_in_map(w, core) for core in range(8)]
    res = run_bass_kernel_spmd(nc, in_maps, core_ids=list(range(8)))
    out = np.empty((4, 4096, 2048), dtype=np.float32)
    for core in range(8):
        b, j = core // 2, core % 2
        out[b, j * 2048:(j + 1) * 2048, :] = res.results[core]["y"]
    return out
